# version 05
# baseline (speedup 1.0000x reference)
import contextlib
import numpy as np
import ml_dtypes

import concourse.bass as bass
import concourse.mybir as mybir
from concourse.bass_utils import run_bass_kernel_spmd

F32 = mybir.dt.float32
BF16 = mybir.dt.bfloat16
I32 = mybir.dt.int32
AF = mybir.ActivationFunctionType
ALU = mybir.AluOpType
AX = mybir.AxisListType

D = 2048
NC = 16
TOK = 1024
NTG = 2
TG = 512
HID = 8192
T = 4096
H = 16
DH = 128
SM = float(1.0 / np.sqrt(128.0))
EPS = 1e-5
NB = 64
BLK = 64
TOPK = 8
BIGM = 1.0e30


class Sch:

    ENGS = ("pe", "dve", "act", "sp", "pool")
    SERIAL = ("dve", "act")

    def __init__(self, nc):
        self.nc = nc
        self.q = {e: [] for e in self.ENGS}
        self.cnt = {e: 0 for e in self.ENGS}
        self.ch_cnt = {}
        self._lastw = {}

    def emit(self, eng, fn, waits=(), inc=False):
        assert eng in ("pe", "dve", "act")
        waits = list(waits)
        if eng in self.SERIAL:
            waits.append((eng, self.cnt[eng]))
            inc = True
        ws = self._filter(eng, waits)
        if inc:
            self.cnt[eng] += 1
            out = (eng, self.cnt[eng])
        else:
            out = None
        self.q[eng].append((ws, fn, eng if inc else None))
        return out

    def dma(self, queue, channel, fn, waits=()):
        assert queue in ("sp", "pool")
        ws = self._filter(queue, list(waits))
        self.ch_cnt[channel] = self.ch_cnt.get(channel, 0) + 16
        self.q[queue].append((ws, fn, channel))
        return (channel, self.ch_cnt[channel])

    def _filter(self, eng, waits):
        ws = []
        for ev in waits:
            if ev is None:
                continue
            tgt, val = ev
            if tgt == eng and tgt in ("pe",):
                continue
            key = (eng, tgt)
            if self._lastw.get(key, -1) >= val:
                continue
            self._lastw[key] = val
            ws.append((tgt, val))
        return ws

    def now(self, eng):
        return (eng, self.cnt[eng])

    def ch_now(self, channel):
        return (channel, self.ch_cnt.get(channel, 0))

    def build(self, block, mksem):
        engmap = dict(pe="tensor", dve="vector", act="scalar", sp="sync", pool="gpsimd")
        sems = {}

        def sem(name):
            if name not in sems:
                sems[name] = mksem(name)
            return sems[name]

        for eng in self.ENGS:
            for ws, fn, inc_t in self.q[eng]:
                for tgt, val in ws:
                    sem(tgt)
                if inc_t is not None:
                    sem(inc_t)

        def reg(eng):
            items = self.q[eng]

            def body(e):
                for ws, fn, inc_t in items:
                    for tgt, val in ws:
                        e.wait_ge(sems[tgt], val)
                    inst = fn(e)
                    if inc_t is not None:
                        inst.then_inc(sems[inc_t], 16 if eng in ("sp", "pool") else 1)
            body.__name__ = f"body_{eng}"
            getattr(block, engmap[eng])(body)

        for eng in self.ENGS:
            if self.q[eng]:
                reg(eng)


class LaunchBuilder:
    def __init__(self, name):
        self.name = name
        self.nc = bass.Bass("TRN2", target_bir_lowering=False, debug=False)
        self.es = contextlib.ExitStack()
        self.sb_bytes = 0
        self.dram = {}

    def inp(self, name, shape, dt):
        t = self.nc.dram_tensor(name, list(shape), dt, kind="ExternalInput")
        self.dram[name] = t
        return t

    def out(self, name, shape, dt):
        t = self.nc.dram_tensor(name, list(shape), dt, kind="ExternalOutput")
        self.dram[name] = t
        return t

    def scratch(self, name, shape, dt):
        t = self.nc.dram_tensor(name, list(shape), dt)
        self.dram[name] = t
        return t

    def begin(self):
        self.block = self.es.enter_context(self.nc.Block())
        self.S = Sch(self.nc)

    def sbuf(self, name, shape, dt):
        n = 1
        for s in shape[1:]:
            n *= s
        self.sb_bytes += n * {F32: 4, BF16: 2, I32: 4}[dt] * 128
        return self.es.enter_context(self.nc.sbuf_tensor(name, list(shape), dt))

    def psum_banks(self, n=8):
        return [self.es.enter_context(self.nc.psum_tensor(f"bank{i}", [128, 512], F32))
                for i in range(n)]

    def finish(self):
        self.S.build(self.block,
                     lambda name: self.es.enter_context(self.nc.semaphore(f"s_{name}")))
        self.es.close()
        return self.nc


def alloc_common(L):
    t = {}
    t["ones_col_f"] = L.sbuf("ones_col_f", [128, 1], F32)
    t["ones_col_b"] = L.sbuf("ones_col_b", [128, 1], BF16)
    t["ones_row_f"] = L.sbuf("ones_row_f", [1, 128], F32)
    t["negk_row_f"] = L.sbuf("negk_row_f", [1, 128], F32)
    t["negk_row_b"] = L.sbuf("negk_row_b", [1, 128], BF16)
    t["corr"] = L.sbuf("corr", [1, 3 * TOK], BF16)
    t["stats"] = L.sbuf("stats", [1, 4 * TOK], F32)
    t["eps"] = L.sbuf("eps", [1, 1], F32)
    S = L.S
    S.emit("dve", lambda v: v.memset(t["ones_col_f"][:, :], 1.0))
    S.emit("dve", lambda v: v.memset(t["ones_col_b"][:, :], 1.0))
    S.emit("dve", lambda v: v.memset(t["ones_row_f"][:, :], 1.0))
    S.emit("dve", lambda v: v.memset(t["negk_row_f"][:, :], -128.0))
    S.emit("dve", lambda v: v.memset(t["negk_row_b"][:, :], -128.0))
    ev = S.emit("dve", lambda v: v.memset(t["eps"][:, :], EPS), inc=True)
    t["consts_ev"] = ev
    return t


def ln_phase(L, t, xsrc, xh_sb, banks, xstage, sqstage, pre=(), sx=False):
    S = L.S
    st = t["stats"]
    R = lambda r, tg: st[0:1, r * TOK + tg * TG: r * TOK + (tg + 1) * TG]
    xslot_free = [None, None]
    sqslot_free = [None, None]
    pe_ev = None
    for c in range(NC):
        for tg in range(NTG):
            sl = (c * NTG + tg) % 2
            xs = xstage[:, sl * TG:(sl + 1) * TG]
            qs = sqstage[:, sl * TG:(sl + 1) * TG]
            ev_ld = S.dma("sp", f"xs{sl}", lambda s, c=c, tg=tg, xs=xs: s.dma_start(
                out=xs, in_=xsrc[c * 128:(c + 1) * 128, tg * TG:(tg + 1) * TG]),
                waits=(xslot_free[sl] or []) + list(pre))
            ev_sq = S.emit("act", lambda a, xs=xs, qs=qs: a.activation(qs, xs, AF.Square),
                           waits=[ev_ld] + (sqslot_free[sl] or []), inc=True)
            ev_m1 = S.emit("pe", lambda p, c=c, tg=tg, xs=xs: p.matmul(
                banks[tg][0:1, :], t["ones_col_f"][:, :], xs,
                start=(c == 0), stop=(c == NC - 1)),
                waits=[ev_ld, t["consts_ev"]], inc=True)
            pe_ev = S.emit("pe", lambda p, c=c, tg=tg, qs=qs: p.matmul(
                banks[2 + tg][0:1, :], t["ones_col_f"][:, :], qs,
                start=(c == 0), stop=(c == NC - 1)),
                waits=[ev_sq], inc=True)
            xslot_free[sl] = [ev_m1, ev_sq]
            sqslot_free[sl] = [pe_ev]
    for tg in range(NTG):
        S.emit("dve", lambda v, tg=tg: v.tensor_copy(R(0, tg), banks[tg][0:1, :]), waits=[pe_ev])
        S.emit("dve", lambda v, tg=tg: v.tensor_copy(R(1, tg), banks[2 + tg][0:1, :]))
    S.emit("dve", lambda v: v.tensor_scalar_mul(st[0:1, 2 * TOK:3 * TOK], st[0:1, 0:TOK], 1.0 / D))
    S.emit("dve", lambda v: v.tensor_scalar_mul(st[0:1, 1 * TOK:2 * TOK], st[0:1, 1 * TOK:2 * TOK], 1.0 / D))
    S.emit("dve", lambda v: v.tensor_tensor(st[0:1, 0:TOK], st[0:1, 2 * TOK:3 * TOK],
                                            st[0:1, 2 * TOK:3 * TOK], ALU.mult))
    ev = S.emit("dve", lambda v: v.tensor_tensor(st[0:1, 1 * TOK:2 * TOK], st[0:1, 1 * TOK:2 * TOK],
                                                 st[0:1, 0:TOK], ALU.subtract), inc=True)
    ev = S.emit("act", lambda a: a.activation(st[0:1, 1 * TOK:2 * TOK], st[0:1, 1 * TOK:2 * TOK],
                                              AF.Sqrt, bias=t["eps"][:, :]), waits=[ev], inc=True)
    ev = S.emit("dve", lambda v: v.reciprocal(st[0:1, 3 * TOK:4 * TOK], st[0:1, 1 * TOK:2 * TOK]),
                waits=[ev], inc=True)
    ev_mr = S.emit("dve", lambda v: v.tensor_tensor(st[0:1, 2 * TOK:3 * TOK], st[0:1, 2 * TOK:3 * TOK],
                                                    st[0:1, 3 * TOK:4 * TOK], ALU.mult), inc=True)
    bev = None
    for tg in range(NTG):
        S.emit("pe", lambda p, tg=tg: p.matmul(banks[4 + tg][:, :], t["ones_row_f"][:, :],
                                               R(3, tg), start=True, stop=True), waits=[ev_mr])
        bev = S.emit("pe", lambda p, tg=tg: p.matmul(banks[6 + tg][:, :], t["ones_row_f"][:, :],
                                                     R(2, tg), start=True, stop=True), inc=True)
    last = None
    for c in range(NC):
        for tg in range(NTG):
            sl = (c * NTG + tg) % 2
            xs = xstage[:, sl * TG:(sl + 1) * TG]
            qs = sqstage[:, sl * TG:(sl + 1) * TG]
            ev_ld = S.dma("sp", f"xs{sl}", lambda s, c=c, tg=tg, xs=xs: s.dma_start(
                out=xs, in_=xsrc[c * 128:(c + 1) * 128, tg * TG:(tg + 1) * TG]),
                waits=(xslot_free[sl] or []))
            S.emit("dve", lambda v, tg=tg, xs=xs, qs=qs: v.tensor_tensor(
                qs, xs, banks[4 + tg][:, :], ALU.mult),
                waits=[ev_ld, bev] + (sqslot_free[sl] or []))
            last = S.emit("dve", lambda v, c=c, tg=tg, qs=qs: v.tensor_tensor(
                xh_sb[:, c * TOK + tg * TG: c * TOK + (tg + 1) * TG],
                qs, banks[6 + tg][:, :], ALU.subtract), inc=True)
            xslot_free[sl] = [last]
            sqslot_free[sl] = [last]
    if sx:
        for c in range(NC):
            for tg in range(NTG):
                pe_ev = S.emit("pe", lambda p, c=c, tg=tg: p.matmul(
                    banks[tg][0:1, :], t["ones_col_b"][:, :],
                    xh_sb[:, c * TOK + tg * TG: c * TOK + (tg + 1) * TG],
                    start=(c == 0), stop=(c == NC - 1)),
                    waits=[last], inc=(c == NC - 1))
        for tg in range(NTG):
            S.emit("dve", lambda v, tg=tg: v.tensor_copy(R(0, tg), banks[tg][0:1, :]),
                   waits=[pe_ev], inc=True)
        last = S.emit("dve", lambda v: v.tensor_copy(t["corr"][0:1, 0:TOK], st[0:1, 0:TOK]),
                      inc=True)
    return last


def ffn_phase(L, t, xh_sb, h_sb, yp_sb, wbuf, banks, rstage,
              wgT, wuT, wdT, quant, sq_sb, sd05_ap,
              xold, resid_out, xh_ready, wfree=None):
    S = L.S
    st = t["stats"]
    bank_free = {}
    wslot_free = wfree if wfree is not None else [None, None]
    h_ev = {}
    y_ev = {}
    rslot_free = [[xh_ready], [xh_ready]]
    res_done = []

    for half in range(2):
        for which in ("g", "u"):
            W = wgT if which == "g" else wuT
            for ic in range(32):
                gi = half * 32 + ic
                sl = ic % 2
                ev_w = S.dma("pool", f"wb{sl}", lambda g, gi=gi, sl=sl, W=W: g.dma_start(
                    out=wbuf[sl].ap().rearrange("p (c m) -> p c m", m=128)[:, 0:16, :],
                    in_=W[:, gi, :, :]),
                    waits=[wslot_free[sl]])
                mm_last = [None, None]
                for c in range(NC):
                    for tg in range(NTG):
                        pb = banks[sl * 2 + tg]
                        ev = S.emit("pe", lambda p, c=c, tg=tg, sl=sl, pb=pb: p.matmul(
                            pb[:, :], wbuf[sl][:, c * 128:(c + 1) * 128],
                            xh_sb[:, c * TOK + tg * TG: c * TOK + (tg + 1) * TG],
                            start=(c == 0), stop=(not quant and c == NC - 1)),
                            waits=[ev_w, xh_ready, bank_free.get(sl * 2 + tg)],
                            inc=(not quant and c == NC - 1))
                        mm_last[tg] = ev
                if quant:
                    for tg in range(NTG):
                        pb = banks[sl * 2 + tg]
                        mm_last[tg] = S.emit("pe", lambda p, tg=tg, pb=pb: p.matmul(
                            pb[:, :], t["negk_row_b"][:, :],
                            t["corr"][0:1, tg * TG:(tg + 1) * TG],
                            start=False, stop=True), inc=True)
                wslot_free[sl] = mm_last[1]
                for tg in range(NTG):
                    pb = banks[sl * 2 + tg]
                    hsl = h_sb[:, ic * TOK + tg * TG: ic * TOK + (tg + 1) * TG]
                    if which == "g":
                        if quant:
                            ev = S.emit("act", lambda a, pb=pb, hsl=hsl, gi=gi: a.activation(
                                hsl, pb[:, :], AF.Silu, scale=sq_sb[:, gi:gi + 1]),
                                waits=[mm_last[tg]], inc=True)
                        else:
                            ev = S.emit("act", lambda a, pb=pb, hsl=hsl: a.activation(
                                hsl, pb[:, :], AF.Silu), waits=[mm_last[tg]], inc=True)
                    else:
                        if quant:
                            ev = S.emit("dve", lambda v, pb=pb, hsl=hsl, gi=gi: v.scalar_tensor_tensor(
                                hsl, pb[:, :], sq_sb[:, 64 + gi:64 + gi + 1], hsl,
                                ALU.mult, ALU.mult),
                                waits=[mm_last[tg], h_ev.get((ic, tg))], inc=True)
                        else:
                            ev = S.emit("dve", lambda v, pb=pb, hsl=hsl: v.tensor_mul(
                                hsl, pb[:, :], hsl),
                                waits=[mm_last[tg], h_ev.get((ic, tg))], inc=True)
                    bank_free[sl * 2 + tg] = ev
                    h_ev[(ic, tg)] = ev
        if quant:
            pe_ev = None
            for ic in range(32):
                for tg in range(NTG):
                    pe_ev = S.emit("pe", lambda p, ic=ic, tg=tg: p.matmul(
                        banks[4 + tg][0:1, :], t["ones_col_b"][:, :],
                        h_sb[:, ic * TOK + tg * TG: ic * TOK + (tg + 1) * TG],
                        start=(ic == 0), stop=(ic == 31)),
                        waits=[h_ev[(ic, tg)], bank_free.get(4 + tg)],
                        inc=(ic == 31))
            for tg in range(NTG):
                shev = S.emit("dve", lambda v, tg=tg: v.tensor_copy(
                    st[0:1, 1 * TOK + tg * TG: 1 * TOK + (tg + 1) * TG],
                    banks[4 + tg][0:1, :]), waits=[pe_ev], inc=True)
                bank_free[4 + tg] = shev
            shev = S.emit("dve", lambda v: v.tensor_copy(
                t["corr"][0:1, TOK:2 * TOK], st[0:1, TOK:2 * TOK]), inc=True)
            shev = S.emit("dve", lambda v: v.tensor_sub(
                t["corr"][0:1, 2 * TOK:3 * TOK], st[0:1, TOK:2 * TOK],
                t["corr"][0:1, TOK:2 * TOK]), inc=True)
        for dc in range(NC):
            sl = dc % 2
            ev_w = S.dma("pool", f"wb{sl}", lambda g, dc=dc, half=half, sl=sl: g.dma_start(
                out=wbuf[sl].ap().rearrange("p (c m) -> p c m", m=128),
                in_=wdT[:, dc, half * 32:(half + 1) * 32, :]),
                waits=[wslot_free[sl]])
            mm_last = [None, None]
            for c in range(32):
                for tg in range(NTG):
                    pb = banks[sl * 2 + tg]
                    ev = S.emit("pe", lambda p, c=c, tg=tg, sl=sl, pb=pb: p.matmul(
                        pb[:, :], wbuf[sl][:, c * 128:(c + 1) * 128],
                        h_sb[:, c * TOK + tg * TG: c * TOK + (tg + 1) * TG],
                        start=(c == 0), stop=(not quant and c == 31)),
                        waits=[ev_w, h_ev[(c, tg)], bank_free.get(sl * 2 + tg)],
                        inc=(not quant and c == 31))
                    mm_last[tg] = ev
            if quant:
                for tg in range(NTG):
                    pb = banks[sl * 2 + tg]
                    S.emit("pe", lambda p, tg=tg, pb=pb: p.matmul(
                        pb[:, :], t["negk_row_b"][:, :],
                        t["corr"][0:1, TOK + tg * TG: TOK + (tg + 1) * TG],
                        start=False, stop=False), waits=[shev])
                    mm_last[tg] = S.emit("pe", lambda p, tg=tg, pb=pb: p.matmul(
                        pb[:, :], t["negk_row_b"][:, :],
                        t["corr"][0:1, 2 * TOK + tg * TG: 2 * TOK + (tg + 1) * TG],
                        start=False, stop=True), inc=True)
            wslot_free[sl] = mm_last[1]
            for tg in range(NTG):
                pb = banks[sl * 2 + tg]
                ysl = yp_sb[:, dc * TOK + tg * TG: dc * TOK + (tg + 1) * TG]
                if half == 0:
                    ev = S.emit("dve", lambda v, pb=pb, ysl=ysl: v.tensor_copy(ysl, pb[:, :]),
                                waits=[mm_last[tg]], inc=True)
                    bank_free[sl * 2 + tg] = ev
                    y_ev[(dc, tg)] = ev
                else:
                    rsl = (dc * NTG + tg) % 2
                    rs = rstage[:, rsl * TG:(rsl + 1) * TG]
                    ev_x = S.dma("sp", f"rs{rsl}", lambda s, dc=dc, tg=tg, rs=rs: s.dma_start(
                        out=rs, in_=xold[dc * 128:(dc + 1) * 128, tg * TG:(tg + 1) * TG]),
                        waits=list(rslot_free[rsl]))
                    ev = S.emit("dve", lambda v, pb=pb, ysl=ysl: v.tensor_add(ysl, ysl, pb[:, :]),
                                waits=[mm_last[tg], y_ev[(dc, tg)]], inc=True)
                    bank_free[sl * 2 + tg] = ev
                    ev = S.emit("dve", lambda v, ysl=ysl, rs=rs, dc=dc: v.scalar_tensor_tensor(
                        ysl, ysl, sd05_ap(dc), rs, ALU.mult, ALU.add),
                        waits=[ev_x], inc=True)
                    rslot_free[rsl] = [ev]
                    ev2 = S.dma("sp", "sr", lambda s, dc=dc, tg=tg, ysl=ysl: s.dma_start(
                        out=resid_out[dc * 128:(dc + 1) * 128, tg * TG:(tg + 1) * TG],
                        in_=ysl), waits=[ev])
                    res_done.append(ev2)
    return res_done[-1]


def proj_phase(L, t, xh_sb, pbuf, wbuf, banks, Wt, bias_sb, out_hbm, xh_ready,
               extra=(), nout=NC, wfree=None):
    S = L.S
    wslot_free = wfree if wfree is not None else [None, None]
    pslot_free = [None, None]
    bank_free = {}
    last_st = None
    for oc in range(nout):
        sl = oc % 2
        ev_w = S.dma("pool", f"wb{sl}", lambda g, oc=oc, sl=sl: g.dma_start(
            out=wbuf[sl].ap().rearrange("p (c m) -> p c m", m=128)[:, 0:16, :],
            in_=Wt[:, oc, :, :]),
            waits=[wslot_free[sl]])
        mm_last = [None, None]
        for c in range(NC):
            for tg in range(NTG):
                pb = banks[sl * 2 + tg]
                ev = S.emit("pe", lambda p, c=c, tg=tg, sl=sl, pb=pb: p.matmul(
                    pb[:, :], wbuf[sl][:, c * 128:(c + 1) * 128],
                    xh_sb[:, c * TOK + tg * TG: c * TOK + (tg + 1) * TG],
                    start=(c == 0), stop=(c == NC - 1)),
                    waits=[ev_w, xh_ready, bank_free.get(sl * 2 + tg)],
                    inc=(c == NC - 1))
                mm_last[tg] = ev
        wslot_free[sl] = mm_last[1]
        for tg in range(NTG):
            pb = banks[sl * 2 + tg]
            dst = pbuf[:, sl * TOK + tg * TG: sl * TOK + (tg + 1) * TG]
            if bias_sb is not None:
                ev = S.emit("dve", lambda v, pb=pb, dst=dst, oc=oc: v.tensor_scalar_add(
                    dst, pb[:, :], bias_sb[:, oc:oc + 1]),
                    waits=[mm_last[tg], pslot_free[sl]] + list(extra), inc=True)
            else:
                ev = S.emit("dve", lambda v, pb=pb, dst=dst: v.tensor_copy(dst, pb[:, :]),
                            waits=[mm_last[tg], pslot_free[sl]] + list(extra), inc=True)
            bank_free[sl * 2 + tg] = ev
            last_st = S.dma("sp", f"po{sl}", lambda s, oc=oc, tg=tg, dst=dst: s.dma_start(
                out=out_hbm[oc * 128:(oc + 1) * 128, tg * TG:(tg + 1) * TG], in_=dst),
                waits=[ev])
            pslot_free[sl] = last_st
    return last_st


def build_launch_A(debug=False):
    L = LaunchBuilder("A")
    xT = L.inp("xT", [D, TOK], F32)
    wgT = L.inp("wgT", [128, 64, 16, 128], F32)
    wuT = L.inp("wuT", [128, 64, 16, 128], F32)
    wdT = L.inp("wdT", [128, 16, 64, 128], F32)
    Wq = L.inp("Wq", [128, 16, 16, 128], F32)
    Wk = L.inp("Wk", [128, 16, 16, 128], F32)
    Wv = L.inp("Wv", [128, 16, 16, 128], F32)
    b_qkv = L.inp("b_qkv", [128, 3 * NC], F32)
    resid1 = L.out("resid1", [D, TOK], F32)
    dbg = {}
    qT = L.out("qT", [D, TOK], BF16)
    kT = L.out("kT", [D, TOK], BF16)
    vT = L.out("vT", [D, TOK], BF16)

    L.begin()
    S = L.S
    t = alloc_common(L)
    banks = L.psum_banks()
    xh = L.sbuf("xh", [128, NC * TOK], BF16)
    h = L.sbuf("h", [128, 32 * TOK], BF16)
    yp = L.sbuf("yp", [128, NC * TOK], F32)
    wbuf = [L.sbuf("wb0", [128, 4096], BF16), L.sbuf("wb1", [128, 4096], BF16)]
    xstage = L.sbuf("xstage", [128, 2 * TG], F32)
    sqstage = yp[:, 0:2 * TG]
    bqkv_sb = L.sbuf("bqkv_sb", [128, 3 * NC], F32)
    ev_b = S.dma("pool", "cb", lambda g: g.dma_start(out=bqkv_sb[:, :], in_=b_qkv[:, :]))

    if debug:
        dbg["xh1"] = L.out("dbg_xh1", [128, NC * TOK], BF16)
        dbg["h"] = L.out("dbg_h", [128, 32 * TOK], BF16)
        dbg["xh2"] = L.out("dbg_xh2", [128, NC * TOK], BF16)
    wfree = [None, None]
    ev = ln_phase(L, t, xT.ap(), xh, banks, xstage, sqstage)
    if debug:
        ev = S.dma("sp", "dbg", lambda s: s.dma_start(out=dbg["xh1"][:, :], in_=xh[:, :]),
                   waits=[ev])
    ev = ffn_phase(L, t, xh, h, yp, wbuf, banks, xstage,
                   wgT, wuT, wdT, quant=False, sq_sb=None, sd05_ap=lambda dc: 0.5,
                   xold=xT.ap(), resid_out=resid1.ap(), xh_ready=ev, wfree=wfree)
    if debug:
        ev = S.dma("sp", "dbg", lambda s: s.dma_start(out=dbg["h"][:, :], in_=h[:, :]),
                   waits=[ev])
    ev = ln_phase(L, t, resid1.ap(), xh, banks, xstage, sqstage, pre=[ev])
    if debug:
        ev = S.dma("sp", "dbg", lambda s: s.dma_start(out=dbg["xh2"][:, :], in_=xh[:, :]),
                   waits=[ev])
    pq = h[:, 0:2 * TOK]
    pk = h[:, 2 * TOK:4 * TOK]
    pv = h[:, 4 * TOK:6 * TOK]
    e1 = proj_phase(L, t, xh, pq, wbuf, banks, Wq, bqkv_sb[:, 0:NC], qT.ap(), ev, extra=[ev_b], wfree=wfree)
    e2 = proj_phase(L, t, xh, pk, wbuf, banks, Wk, bqkv_sb[:, NC:2 * NC], kT.ap(), e1, extra=[ev_b], wfree=wfree)
    e3 = proj_phase(L, t, xh, pv, wbuf, banks, Wv, bqkv_sb[:, 2 * NC:3 * NC], vT.ap(), e2, extra=[ev_b], wfree=wfree)
    S.dma("sp", "fin", lambda s: s.nop(), waits=[e3, ev_b, S.ch_now("po0"), S.ch_now("po1")])
    return L




def attn_phase(L, t, S_ps, qsrc, kh, vh, vtm, P, PT, iota_sb, thr_sb, ident_sb,
               kfull, vfull, outT, scr, pre=(), qmine_in=None, ar1_q=None,
               b01=None, extra_qk=(), dbg=None):
    S = L.S
    nm, den, outstage = scr["nm"], scr["den"], scr["outstage"]
    qm = scr.get("qm")
    kh_free = list(pre)
    vh_free = list(pre)
    qm_free = []
    exp_prev = {}
    store_prev = {}
    bank_ev = [None] * 8
    bank_bf = [S_ps[:, b * 512:(b + 1) * 512].bitcast(BF16) for b in range(8)]
    last_store = None
    out_evac_ev = None

    for h in range(16):
        ev_k = S.dma("sp", "kh", lambda s, h=h: s.dma_start(
            out=kh[:, :], in_=kfull[h * 128:(h + 1) * 128, :]),
            waits=kh_free)
        ev_v = S.dma("sp", "vh", lambda s, h=h: s.dma_start(
            out=vh[:, :], in_=vfull[h * 128:(h + 1) * 128, :]),
            waits=vh_free)
        if qmine_in is not None:
            ev_q = S.dma("sp", "qm", lambda s, h=h: s.dma_start(
                out=qm[:, :], in_=qmine_in[h, :, :]), waits=qm_free)
            qv = qm
            qoff = 0
        else:
            ev_q = None
            qv = ar1_q
            qoff = h * 1024
        tr_ev = None
        vtm_evs = []
        for grp in range(4):
            bb = 6 + (grp % 2)
            for j in range(8):
                sc = grp * 8 + j
                tr_ev = S.emit("pe", lambda p, bb=bb, j=j, sc=sc: p.transpose(
                    bank_bf[bb][:, j * 128:(j + 1) * 128], vh[:, sc * 128:(sc + 1) * 128],
                    ident_sb[:, :]),
                    waits=[ev_v, bank_ev[bb], t["ident_ev"]], inc=(j == 7))
            ev = S.emit("act", lambda a, grp=grp, bb=bb: a.activation(
                vtm[:, grp * 1024:(grp + 1) * 1024], bank_bf[bb][:, :], AF.Copy),
                waits=[tr_ev], inc=True)
            bank_ev[bb] = ev
            vtm_evs.append(ev)
        vh_free = [tr_ev]
        exp_evs = []
        norm_evs = {}
        for qtg in range(2):
            for qt4 in range(4):
                qtidx = qtg * 4 + qt4
                mm = None
                for kt in range(8):
                    mm = S.emit("pe", lambda p, kt=kt, qtidx=qtidx, qv=qv, qoff=qoff: p.matmul(
                        S_ps[:, kt * 512:(kt + 1) * 512],
                        qv[:, qoff + qtidx * 128: qoff + (qtidx + 1) * 128],
                        kh[:, kt * 512:(kt + 1) * 512], start=True, stop=True),
                        waits=[ev_k, ev_q, bank_ev[kt]] + vtm_evs + list(extra_qk),
                        inc=(kt == 7))
                ev = S.emit("dve", lambda v, qtidx=qtidx: v.tensor_reduce(
                    scr["nm8"].ap().unsqueeze(2),
                    S_ps.ap().rearrange("p (b n) -> p b n", n=512), AX.X, ALU.max),
                    waits=[mm, exp_prev.get(qtidx)], inc=False)
                ev = S.emit("dve", lambda v: v.reduce_max(
                    scr["nm1"][:, :], scr["nm8"][:, :], AX.X), inc=False)
                ev = S.emit("dve", lambda v, qtidx=qtidx: v.tensor_scalar_mul(
                    nm[:, qtidx:qtidx + 1], scr["nm1"][:, :], -SM), inc=True)
                pdst = P[:, qt4 * 4096:(qt4 + 1) * 4096]
                eexp = S.emit("act", lambda a, pdst=pdst, qtidx=qtidx: a.activation(
                    pdst, S_ps[:, :], AF.Exp, bias=nm[:, qtidx:qtidx + 1], scale=SM),
                    waits=[ev, out_evac_ev], inc=True)
                exp_evs.append(eexp)
                exp_prev[qtidx] = eexp
                for b in range(8):
                    bank_ev[b] = eexp
                if b01 is not None:
                    ev = S.dma("pool", "b01m", lambda g, pdst=pdst, qtidx=qtidx, h=h: g.tensor_tensor(
                        pdst.rearrange("p (a b) -> p a b", b=64), pdst.rearrange("p (a b) -> p a b", b=64),
                        b01[:, (qtidx * 16 + h) * 64:(qtidx * 16 + h + 1) * 64]
                        .unsqueeze(2).broadcast_to([128, 64, 64]), ALU.mult),
                        waits=[eexp])
                ev = S.emit("dve", lambda v, pdst=pdst, qtidx=qtidx: v.scalar_tensor_tensor(
                    pdst, iota_sb[:, :], thr_sb[:, qtidx:qtidx + 1], pdst,
                    ALU.is_le, ALU.mult, accum_out=den[:, qtidx:qtidx + 1]),
                    waits=[eexp, ev if b01 is not None else None,
                           t["iota_ev"], t.get("thr_ev")], inc=True)
                ev = S.emit("dve", lambda v, qtidx=qtidx: v.reciprocal(
                    den[:, qtidx:qtidx + 1], den[:, qtidx:qtidx + 1]), inc=True)
                ev = S.emit("act", lambda a, pdst=pdst, qtidx=qtidx: a.activation(
                    pdst, pdst, AF.Copy, scale=den[:, qtidx:qtidx + 1]),
                    waits=[ev], inc=True)
                norm_evs[qt4] = ev
            pt_evs = {}
            for rnd in range(4):
                for j in range(4):
                    bb = j
                    tr = None
                    for half in range(2):
                        sc = rnd * 8 + j * 2 + half
                        for qt4 in range(4):
                            tr = S.emit("pe", lambda p, bb=bb, half=half, qt4=qt4, sc=sc: p.transpose(
                                bank_bf[bb][:, half * 512 + qt4 * 128: half * 512 + (qt4 + 1) * 128],
                                P[:, qt4 * 4096 + sc * 128: qt4 * 4096 + (sc + 1) * 128],
                                ident_sb[:, :]),
                                waits=[norm_evs[qt4], bank_ev[bb]],
                                inc=(qt4 == 3 and half == 1))
                    sc0 = rnd * 8 + j * 2
                    ev = S.emit("act", lambda a, sc0=sc0, bb=bb: a.activation(
                        PT[:, sc0 * 512:(sc0 + 2) * 512], bank_bf[bb][:, :], AF.Copy),
                        waits=[tr], inc=True)
                    bank_ev[bb] = ev
                    pt_evs[sc0] = ev
                    pt_evs[sc0 + 1] = ev
            pv = None
            for sc in range(32):
                pv = S.emit("pe", lambda p, sc=sc: p.matmul(
                    S_ps[:, 4 * 512:5 * 512], vtm[:, sc * 128:(sc + 1) * 128],
                    PT[:, sc * 512:(sc + 1) * 512], start=(sc == 0), stop=(sc == 31)),
                    waits=[pt_evs[sc], bank_ev[4]], inc=(sc == 31))
            out_evac_ev = S.emit("dve", lambda v, qtg=qtg: v.tensor_copy(
                outstage[:, qtg * 512:(qtg + 1) * 512], S_ps[:, 4 * 512:5 * 512]),
                waits=[pv, store_prev.get(qtg)], inc=True)
            bank_ev[4] = out_evac_ev
            last_store = S.dma("sp", f"so{qtg}", lambda s, h=h, qtg=qtg: s.dma_start(
                out=outT[h * 128:(h + 1) * 128, qtg * 512:(qtg + 1) * 512],
                in_=outstage[:, qtg * 512:(qtg + 1) * 512]),
                waits=[out_evac_ev])
            store_prev[qtg] = last_store
            if dbg is not None and h == 0 and qtg == 0:
                dde = S.dma("sp", "dbg", lambda s: s.dma_start(out=dbg["P"][:, :], in_=P[:, :]),
                            waits=[norm_evs[3], out_evac_ev])
                dde = S.dma("sp", "dbg", lambda s: s.dma_start(out=dbg["PT"][:, :], in_=PT[:, :]),
                            waits=[out_evac_ev])
                dde = S.dma("sp", "dbg", lambda s: s.dma_start(out=dbg["nm"][:, :], in_=nm[:, :]))
                dde = S.dma("sp", "dbg", lambda s: s.dma_start(out=dbg["den"][:, :], in_=den[:, :]))
                dde = S.dma("sp", "dbg", lambda s: s.dma_start(out=dbg["vtm"][:, :], in_=vtm[:, :]))
                exp_prev[0] = dde
        kh_free = [mm]
        qm_free = [mm]
    return [S.ch_now("so0"), S.ch_now("so1")]


def wo_phase(L, t, oT_dram, Wo, bo_sb, resid_in, resid_out, wbuf, banks, ar1,
             rstage, wfree, pre=(), ar1_free=()):
    S = L.S
    ev_o = S.dma("sp", "ro", lambda s: s.dma_start(
        out=ar1.ap().rearrange("p (c n) -> p c n", n=TOK),
        in_=oT_dram.rearrange("(c p) n -> p c n", p=128)),
        waits=list(pre) + list(ar1_free))
    bank_free = {}
    rslot_free = [[], []]
    last = None
    for oc in range(NC):
        sl = oc % 2
        ev_w = S.dma("pool", f"wb{sl}", lambda g, oc=oc, sl=sl: g.dma_start(
            out=wbuf[sl].ap().rearrange("p (c m) -> p c m", m=128)[:, 0:16, :],
            in_=Wo[:, oc, :, :]),
            waits=[wfree[sl]])
        mm_last = [None, None]
        for c in range(NC):
            for tg in range(NTG):
                pb = banks[sl * 2 + tg]
                ev = S.emit("pe", lambda p, c=c, tg=tg, sl=sl, pb=pb: p.matmul(
                    pb[:, :], wbuf[sl][:, c * 128:(c + 1) * 128],
                    ar1[:, c * TOK + tg * TG: c * TOK + (tg + 1) * TG],
                    start=(c == 0), stop=(c == NC - 1)),
                    waits=[ev_w, ev_o, bank_free.get(sl * 2 + tg)],
                    inc=(c == NC - 1))
                mm_last[tg] = ev
        wfree[sl] = mm_last[1]
        for tg in range(NTG):
            pb = banks[sl * 2 + tg]
            rsl = (oc * NTG + tg) % 2
            rs = rstage[:, rsl * TG:(rsl + 1) * TG]
            ev_x = S.dma("sp", f"rs{rsl}", lambda s, oc=oc, tg=tg, rs=rs: s.dma_start(
                out=rs, in_=resid_in[oc * 128:(oc + 1) * 128, tg * TG:(tg + 1) * TG]),
                waits=rslot_free[rsl] + list(pre))
            if bo_sb is not None:
                ev = S.emit("dve", lambda v, pb=pb, rs=rs, oc=oc: v.scalar_tensor_tensor(
                    rs, pb[:, :], bo_sb[:, oc:oc + 1], rs, ALU.add, ALU.add),
                    waits=[mm_last[tg], ev_x], inc=True)
            else:
                ev = S.emit("dve", lambda v, pb=pb, rs=rs: v.tensor_add(rs, pb[:, :], rs),
                            waits=[mm_last[tg], ev_x], inc=True)
            bank_free[sl * 2 + tg] = ev
            last = S.dma("sp", f"wr{rsl}", lambda s, oc=oc, tg=tg, rs=rs: s.dma_start(
                out=resid_out[oc * 128:(oc + 1) * 128, tg * TG:(tg + 1) * TG], in_=rs),
                waits=[ev])
            rslot_free[rsl] = [last]
    return [S.ch_now("wr0"), S.ch_now("wr1")]


def build_launch_B1(debug=False):
    L = LaunchBuilder("B1")
    resid1 = L.inp("resid1", [D, TOK], F32)
    qmine = L.inp("qmine", [16, 128, TOK], BF16)
    kfull = L.inp("kfull", [D, T], BF16)
    vfull = L.inp("vfull", [D, T], BF16)
    iota = L.inp("iota", [128, T], F32)
    thr = L.inp("thr", [128, 8], F32)
    ident = L.inp("ident", [128, 128], BF16)
    Wo = L.inp("Wo", [128, 16, 16, 128], F32)
    bo = L.inp("bo", [128, NC], F32)
    resid2 = L.out("resid2", [D, TOK], F32)
    outT = (L.out if debug else L.scratch)("outT", [D, TOK], BF16)

    L.begin()
    S = L.S
    t = alloc_common(L)
    S_ps = L.es.enter_context(L.nc.psum_tensor("S_ps", [128, 4096], F32))
    banks = [S_ps[:, b * 512:(b + 1) * 512] for b in range(8)]
    AR1 = L.sbuf("AR1", [128, NC * TOK], BF16)
    AR2 = L.sbuf("AR2", [128, 2 * NC * TOK], BF16)
    AR3 = L.sbuf("AR3", [128, 2 * NC * TOK], BF16)
    wbuf = [L.sbuf("wb0", [128, 4096], BF16), L.sbuf("wb1", [128, 4096], BF16)]
    xstage = L.sbuf("xstage", [128, 2 * TG], F32)
    kh = AR3[:, 0:4096]
    vh = AR3[:, 4096:8192]
    vtm = AR3[:, 8192:12288]
    qm = AR3[:, 12288:13312]
    outstage = AR3[:, 13312:14336]
    iota_sb = AR3[:, 16384:24576].bitcast(F32)
    thr_sb = L.sbuf("thr_sb", [128, 8], F32)
    ident_sb = AR3[:, 14336:14464]
    bo_sb = L.sbuf("bo_sb", [128, NC], F32)
    nm = L.sbuf("nm", [128, 16], F32)
    nm8 = L.sbuf("nm8", [128, 8], F32)
    nm1 = L.sbuf("nm1", [128, 1], F32)
    den = L.sbuf("den", [128, 16], F32)
    P = AR2[:, 0:NC * TOK]
    PT = AR2[:, NC * TOK:2 * NC * TOK]

    S.dma("sp", "cst", lambda s: s.dma_start(out=iota_sb, in_=iota[:, :]))
    S.dma("sp", "cst", lambda s: s.dma_start(out=thr_sb[:, :], in_=thr[:, :]))
    ev3 = S.dma("sp", "cst", lambda s: s.dma_start(out=ident_sb[:, :], in_=ident[:, :]))
    t["iota_ev"] = ev3
    t["thr_ev"] = ev3
    t["ident_ev"] = ev3
    ev2 = ev3
    ev4 = S.dma("pool", "cb", lambda g: g.dma_start(out=bo_sb[:, :], in_=bo[:, :]))

    scr = dict(nm=nm, den=den, outstage=outstage, qm=qm, nm8=nm8, nm1=nm1)
    dbg = None
    if debug:
        dbg = {"P": L.out("dbg_P", [128, NC * TOK], BF16),
               "PT": L.out("dbg_PT", [128, NC * TOK], BF16),
               "nm": L.out("dbg_nm", [128, 16], F32),
               "den": L.out("dbg_den", [128, 16], F32),
               "vtm": L.out("dbg_vtm", [128, 4096], BF16)}
    ev = attn_phase(L, t, S_ps, None, kh, vh, vtm, P, PT, iota_sb, thr_sb, ident_sb,
                    kfull, vfull, outT.ap(), scr, pre=[ev2], qmine_in=qmine.ap(), dbg=dbg)
    wfree = [None, None]
    ev = wo_phase(L, t, outT.ap(), Wo, bo_sb, resid1.ap(), resid2.ap(), wbuf, banks,
                  AR1, xstage, wfree, pre=ev + [ev4])
    S.dma("sp", "fin", lambda s: s.nop(), waits=ev)
    return L




def build_launch_B2():
    L = LaunchBuilder("B2")
    resid2 = L.inp("resid2", [D, TOK], F32)
    wgT0 = L.inp("wgT0", [128, 64, 16, 128], F32)
    wuT0 = L.inp("wuT0", [128, 64, 16, 128], F32)
    wdT0 = L.inp("wdT0", [128, 16, 64, 128], F32)
    wgT1 = L.inp("wgT1", [128, 64, 16, 128], I32)
    wuT1 = L.inp("wuT1", [128, 64, 16, 128], I32)
    wdT1 = L.inp("wdT1", [128, 16, 64, 128], I32)
    sq1 = L.inp("sq1", [128, 128], F32)
    sd1 = L.inp("sd1", [128, NC], F32)
    Wq1 = L.inp("Wq1", [128, 16, 16, 128], F32)
    Wk1 = L.inp("Wk1", [128, 16, 16, 128], F32)
    Wv1 = L.inp("Wv1", [128, 16, 16, 128], F32)
    resid4 = L.out("resid4", [D, TOK], F32)
    qT1 = L.out("qT1", [D, TOK], BF16)
    kT1 = L.out("kT1", [D, TOK], BF16)
    vT1 = L.out("vT1", [D, TOK], BF16)
    resid3 = L.scratch("resid3", [D, TOK], F32)

    L.begin()
    S = L.S
    t = alloc_common(L)
    S_ps = L.es.enter_context(L.nc.psum_tensor("S_ps", [128, 4096], F32))
    banks = [S_ps[:, b * 512:(b + 1) * 512] for b in range(8)]
    xh = L.sbuf("xh", [128, NC * TOK], BF16)
    h = L.sbuf("h", [128, 32 * TOK], BF16)
    yp = L.sbuf("yp", [128, NC * TOK], F32)
    wbuf = [L.sbuf("wb0", [128, 4096], BF16), L.sbuf("wb1", [128, 4096], BF16)]
    xstage = L.sbuf("xstage", [128, 2 * TG], F32)
    sqstage = yp[:, 0:2 * TG]
    sq_sb = L.sbuf("sq_sb", [128, 128], F32)
    sd05_sb = L.sbuf("sd05_sb", [128, NC], F32)
    ev_s1 = S.dma("pool", "cb", lambda g: g.dma_start(out=sq_sb[:, :], in_=sq1[:, :]))
    ev_s2 = S.dma("pool", "cb", lambda g: g.dma_start(out=sd05_sb[:, :], in_=sd1[:, :]))
    ev_sd = S.emit("dve", lambda v: v.tensor_scalar_mul(sd05_sb[:, :], sd05_sb[:, :], 0.5),
                   waits=[S.ch_now("cb")], inc=True)

    wfree = [None, None]
    ev = ln_phase(L, t, resid2.ap(), xh, banks, xstage, sqstage)
    ev = ffn_phase(L, t, xh, h, yp, wbuf, banks, xstage,
                   wgT0, wuT0, wdT0, quant=False, sq_sb=None, sd05_ap=lambda dc: 0.5,
                   xold=resid2.ap(), resid_out=resid3.ap(), xh_ready=ev, wfree=wfree)
    ev = ln_phase(L, t, resid3.ap(), xh, banks, xstage, sqstage, pre=[ev], sx=True)
    ev = ffn_phase(L, t, xh, h, yp, wbuf, banks, xstage,
                   wgT1, wuT1, wdT1, quant=True, sq_sb=sq_sb,
                   sd05_ap=lambda dc: sd05_sb[:, dc:dc + 1],
                   xold=resid3.ap(), resid_out=resid4.ap(), xh_ready=ev, wfree=wfree)
    ev = ln_phase(L, t, resid4.ap(), xh, banks, xstage, sqstage, pre=[ev])
    pq = h[:, 0:2 * TOK]
    pk = h[:, 2 * TOK:4 * TOK]
    pv = h[:, 4 * TOK:6 * TOK]
    e1 = proj_phase(L, t, xh, pq, wbuf, banks, Wq1, None, qT1.ap(), ev, wfree=wfree)
    e2 = proj_phase(L, t, xh, pk, wbuf, banks, Wk1, None, kT1.ap(), e1, wfree=wfree)
    e3 = proj_phase(L, t, xh, pv, wbuf, banks, Wv1, None, vT1.ap(), e2, wfree=wfree)
    S.dma("sp", "fin", lambda s: s.nop(),
          waits=[e3, S.ch_now("po0"), S.ch_now("po1"), S.ch_now("sr")])
    return L


def build_launch_C1(debug=False):
    L = LaunchBuilder("C1")
    resid4 = L.inp("resid4", [D, TOK], F32)
    q1 = L.inp("q1", [D, TOK], BF16)
    kfull = L.inp("kfull", [D, T], BF16)
    vfull = L.inp("vfull", [D, T], BF16)
    iota = L.inp("iota", [128, T], F32)
    thr = L.inp("thr", [128, 8], F32)
    ident = L.inp("ident", [128, 128], BF16)
    gmask = L.inp("gmask", [8, 128, NB], F32)
    Wo = L.inp("Wo", [128, 16, 16, 128], F32)
    resid5 = L.out("resid5", [D, TOK], F32)
    outT = L.scratch("outT", [D, TOK], BF16)

    L.begin()
    S = L.S
    t = alloc_common(L)
    S_ps = L.es.enter_context(L.nc.psum_tensor("S_ps", [128, 4096], F32))
    banks = [S_ps[:, b * 512:(b + 1) * 512] for b in range(8)]
    AR1 = L.sbuf("AR1", [128, NC * TOK], BF16)
    AR2 = L.sbuf("AR2", [128, 2 * NC * TOK], BF16)
    AR3 = L.sbuf("AR3", [128, 2 * NC * TOK], BF16)
    wbuf = [L.sbuf("wb0", [128, 4096], BF16), L.sbuf("wb1", [128, 4096], BF16)]
    xstage = L.sbuf("xstage", [128, 2 * TG], F32)
    kh = AR3[:, 0:4096]
    vh = AR3[:, 4096:8192]
    vtm = AR3[:, 8192:12288]
    outstage = AR3[:, 13312:14336]
    iota_sb = AR3[:, 16384:24576].bitcast(F32)
    b01 = AR3[:, 24576:32768]
    gate_all = AR2[:, 0:NC * TOK].bitcast(F32)
    gwork = AR2[:, NC * TOK:2 * NC * TOK].bitcast(F32)
    thr_sb = L.sbuf("thr_sb", [128, 8], F32)
    ident_sb = L.sbuf("ident_sb", [128, 128], BF16)
    gm_sb = L.sbuf("gm_sb", [128, 8 * NB], F32)
    kg_sb = L.sbuf("kg_sb", [128, 16 * NB], BF16)
    eqm = AR3[:, 14336:16384].bitcast(F32)
    tau = L.sbuf("tau", [128, 16], F32)
    nm = L.sbuf("nm", [128, 16], F32)
    nm8 = L.sbuf("nm8", [128, 8], F32)
    nm1 = L.sbuf("nm1", [128, 1], F32)
    den = L.sbuf("den", [128, 16], F32)

    S.dma("sp", "cst", lambda s: s.dma_start(out=iota_sb, in_=iota[:, :]))
    S.dma("sp", "cst", lambda s: s.dma_start(out=thr_sb[:, :], in_=thr[:, :]))
    S.dma("sp", "cst", lambda s: s.dma_start(
        out=gm_sb.rearrange("p (q n) -> p q n", n=NB),
        in_=gmask.ap().rearrange("q p n -> p q n")))
    ev3 = S.dma("sp", "cst", lambda s: s.dma_start(out=ident_sb[:, :], in_=ident[:, :]))
    t["iota_ev"] = ev3
    t["thr_ev"] = ev3
    t["ident_ev"] = ev3
    ev_q1 = S.dma("sp", "q1", lambda s: s.dma_start(
        out=AR1.ap().rearrange("p (c n) -> p c n", n=TOK),
        in_=q1.ap().rearrange("(c p) n -> p c n", p=128)))

    kh_free = []
    ge_evs = []
    for hh in range(16):
        ev_k = S.dma("sp", "kh", lambda s, hh=hh: s.dma_start(
            out=kh[:, :], in_=kfull[hh * 128:(hh + 1) * 128, :]), waits=kh_free)
        ev_kg = S.emit("dve", lambda v, hh=hh: v.tensor_reduce(
            eqm.rearrange("p (a n) -> p a n", n=1)[:, 0:NB, :],
            kh.rearrange("p (a n) -> p a n", n=BLK).transpose([0, 2, 1])
            if False else kh.rearrange("p (a n) -> p a n", n=BLK),
            AX.X, ALU.add), waits=[ev_k], inc=True)
        ev_kg = S.emit("dve", lambda v, hh=hh: v.tensor_scalar_mul(
            kg_sb[:, hh * NB:(hh + 1) * NB], eqm[:, 0:NB],
            1.0 / BLK), inc=True)
        mm = None
        for qt in range(8):
            mm = S.emit("pe", lambda p, hh=hh, qt=qt: p.matmul(
                S_ps[:, 5 * 512 + qt * NB: 5 * 512 + (qt + 1) * NB],
                AR1[:, hh * TOK + qt * 128: hh * TOK + (qt + 1) * 128],
                kg_sb[:, hh * NB:(hh + 1) * NB], start=True, stop=True),
                waits=[ev_kg, ev_q1, ge_evs[-1] if ge_evs else None],
                inc=(qt == 7))
        kh_free = [ev_kg]
        ev_g = S.emit("dve", lambda v, hh=hh: v.tensor_tensor(
            gate_all.rearrange("p (q a n) -> p (q a) n", a=16, n=NB)
            .rearrange("p qa n -> p qa n")[:, hh::16, :],
            S_ps[:, 5 * 512:6 * 512].rearrange("p (q n) -> p q n", n=NB),
            gm_sb.rearrange("p (q n) -> p q n", n=NB), ALU.add),
            waits=[mm, S.ch_now("cst")], inc=True)
        ge_evs.append(ev_g)

    b01_evs = []
    for qt in range(8):
        W3 = gwork.rearrange("p (q x) -> p q x", x=TOK)[:, qt, :].rearrange(
            "p (a n) -> p a n", n=NB)
        G3 = gate_all.rearrange("p (q x) -> p q x", x=TOK)[:, qt, :].rearrange(
            "p (a n) -> p a n", n=NB)
        ev = S.emit("dve", lambda v, W3=W3, G3=G3: v.tensor_copy(W3, G3),
                    waits=ge_evs, inc=True)
        for it in range(7):
            ev = S.emit("dve", lambda v, W3=W3: v.tensor_reduce(
                tau.ap().rearrange("p (a n) -> p a n", n=1), W3, AX.X, ALU.max),
                inc=True)
            ev = S.emit("dve", lambda v, W3=W3: v.tensor_tensor(
                eqm.rearrange("p (a n) -> p a n", n=NB), W3,
                tau.ap().unsqueeze(2).broadcast_to([128, 16, NB]), ALU.is_equal),
                inc=True)
            ev = S.emit("dve", lambda v, W3=W3: v.scalar_tensor_tensor(
                W3, eqm.rearrange("p (a n) -> p a n", n=NB), -2.0e30, W3,
                ALU.mult, ALU.add), inc=True)
        ev = S.emit("dve", lambda v, W3=W3: v.tensor_reduce(
            tau.ap().rearrange("p (a n) -> p a n", n=1), W3, AX.X, ALU.max),
            inc=True)
        ev = S.emit("dve", lambda v, G3=G3, qt=qt: v.tensor_tensor(
            b01.rearrange("p (q a n) -> p q a n", a=16, n=NB)[:, qt, :, :], G3,
            tau.ap().unsqueeze(2).broadcast_to([128, 16, NB]), ALU.is_ge),
            inc=True)
        b01_evs.append(ev)

    scr = dict(nm=nm, den=den, outstage=outstage, qm=None, nm8=nm8, nm1=nm1)
    dbg = None
    if debug:
        dbg = {"P": L.out("dbg_P", [128, NC * TOK], BF16),
               "PT": L.out("dbg_PT", [128, NC * TOK], BF16),
               "nm": L.out("dbg_nm", [128, 16], F32),
               "den": L.out("dbg_den", [128, 16], F32),
               "vtm": L.out("dbg_vtm", [128, 4096], BF16)}
        dbg["gate"] = L.out("dbg_gate", [128, 8192], F32)
        dbg["b01"] = L.out("dbg_b01", [128, 8192], BF16)
        S.dma("sp", "dbg", lambda s: s.dma_start(out=dbg["gate"][:, :], in_=gate_all),
              waits=b01_evs)
        S.dma("sp", "dbg", lambda s: s.dma_start(out=dbg["b01"][:, :], in_=b01),
              waits=b01_evs)
    ev = attn_phase(L, t, S_ps, None, kh, vh, vtm, AR2[:, 0:NC * TOK],
                    AR2[:, NC * TOK:2 * NC * TOK], iota_sb, thr_sb, ident_sb,
                    kfull, vfull, outT.ap(), scr, pre=b01_evs + [S.ch_now("dbg")] if debug else b01_evs,
                    ar1_q=AR1, b01=b01, dbg=dbg)
    wfree = [None, None]
    ev = wo_phase(L, t, outT.ap(), Wo, None, resid4.ap(), resid5.ap(), wbuf, banks,
                  AR1, xstage, wfree, pre=ev, ar1_free=[S.now("pe")])
    S.dma("sp", "fin", lambda s: s.nop(), waits=ev)
    return L


def build_launch_C2():
    L = LaunchBuilder("C2")
    resid5 = L.inp("resid5", [D, TOK], F32)
    wgT1 = L.inp("wgT1", [128, 64, 16, 128], I32)
    wuT1 = L.inp("wuT1", [128, 64, 16, 128], I32)
    wdT1 = L.inp("wdT1", [128, 16, 64, 128], I32)
    sq1 = L.inp("sq1", [128, 128], F32)
    sd1 = L.inp("sd1", [128, NC], F32)
    yT = L.out("yT", [D, TOK], F32)

    L.begin()
    S = L.S
    t = alloc_common(L)
    S_ps = L.es.enter_context(L.nc.psum_tensor("S_ps", [128, 4096], F32))
    banks = [S_ps[:, b * 512:(b + 1) * 512] for b in range(8)]
    xh = L.sbuf("xh", [128, NC * TOK], BF16)
    h = L.sbuf("h", [128, 32 * TOK], BF16)
    yp = L.sbuf("yp", [128, NC * TOK], F32)
    wbuf = [L.sbuf("wb0", [128, 4096], BF16), L.sbuf("wb1", [128, 4096], BF16)]
    xstage = L.sbuf("xstage", [128, 2 * TG], F32)
    sqstage = yp[:, 0:2 * TG]
    sq_sb = L.sbuf("sq_sb", [128, 128], F32)
    sd05_sb = L.sbuf("sd05_sb", [128, NC], F32)
    S.dma("pool", "cb", lambda g: g.dma_start(out=sq_sb[:, :], in_=sq1[:, :]))
    S.dma("pool", "cb", lambda g: g.dma_start(out=sd05_sb[:, :], in_=sd1[:, :]))
    S.emit("dve", lambda v: v.tensor_scalar_mul(sd05_sb[:, :], sd05_sb[:, :], 0.5),
           waits=[S.ch_now("cb")], inc=True)
    wfree = [None, None]
    ev = ln_phase(L, t, resid5.ap(), xh, banks, xstage, sqstage, sx=True)
    ev = ffn_phase(L, t, xh, h, yp, wbuf, banks, xstage,
                   wgT1, wuT1, wdT1, quant=True, sq_sb=sq_sb,
                   sd05_ap=lambda dc: sd05_sb[:, dc:dc + 1],
                   xold=resid5.ap(), resid_out=yT.ap(), xh_ready=ev, wfree=wfree)
    S.dma("sp", "fin", lambda s: s.nop(), waits=[ev, S.ch_now("sr")])
    return L




def build_launch_B():
    L = LaunchBuilder("B")
    resid1 = L.inp("resid1", [D, TOK], F32)
    qmine = L.inp("qmine", [16, 128, TOK], BF16)
    kfull = L.inp("kfull", [D, T], BF16)
    vfull = L.inp("vfull", [D, T], BF16)
    iota = L.inp("iota", [128, T], F32)
    thr = L.inp("thr", [128, 8], F32)
    ident = L.inp("ident", [128, 128], BF16)
    Wo = L.inp("Wo", [128, 16, 16, 128], F32)
    bo = L.inp("bo", [128, NC], F32)
    wgT0 = L.inp("wgT0", [128, 64, 16, 128], F32)
    wuT0 = L.inp("wuT0", [128, 64, 16, 128], F32)
    wdT0 = L.inp("wdT0", [128, 16, 64, 128], F32)
    wgT1 = L.inp("wgT1", [128, 64, 16, 128], I32)
    wuT1 = L.inp("wuT1", [128, 64, 16, 128], I32)
    wdT1 = L.inp("wdT1", [128, 16, 64, 128], I32)
    sq1 = L.inp("sq1", [128, 128], F32)
    sd1 = L.inp("sd1", [128, NC], F32)
    Wq1 = L.inp("Wq1", [128, 16, 16, 128], F32)
    Wk1 = L.inp("Wk1", [128, 16, 16, 128], F32)
    Wv1 = L.inp("Wv1", [128, 16, 16, 128], F32)
    resid4 = L.out("resid4", [D, TOK], F32)
    qT1 = L.out("qT1", [D, TOK], BF16)
    kT1 = L.out("kT1", [D, TOK], BF16)
    vT1 = L.out("vT1", [D, TOK], BF16)
    outT = L.scratch("outT", [D, TOK], BF16)
    resid2 = L.scratch("resid2", [D, TOK], F32)
    resid3 = L.scratch("resid3", [D, TOK], F32)

    L.begin()
    S = L.S
    t = alloc_common(L)
    S_ps = L.es.enter_context(L.nc.psum_tensor("S_ps", [128, 4096], F32))
    banks = [S_ps[:, b * 512:(b + 1) * 512] for b in range(8)]
    AR1 = L.sbuf("AR1", [128, NC * TOK], BF16)
    AR2 = L.sbuf("AR2", [128, 2 * NC * TOK], BF16)
    AR3 = L.sbuf("AR3", [128, 2 * NC * TOK], BF16)
    wbuf = [L.sbuf("wb0", [128, 4096], BF16), L.sbuf("wb1", [128, 4096], BF16)]
    xstage = L.sbuf("xstage", [128, 2 * TG], F32)
    kh = AR3[:, 0:4096]
    vh = AR3[:, 4096:8192]
    vtm = AR3[:, 8192:12288]
    qm = AR3[:, 12288:13312]
    outstage = AR3[:, 13312:14336]
    iota_sb = AR3[:, 16384:24576].bitcast(F32)
    thr_sb = L.sbuf("thr_sb", [128, 8], F32)
    ident_sb = AR3[:, 14336:14464]
    bo_sb = L.sbuf("bo_sb", [128, NC], F32)
    nm = L.sbuf("nm", [128, 16], F32)
    nm8 = L.sbuf("nm8", [128, 8], F32)
    nm1 = L.sbuf("nm1", [128, 1], F32)
    den = L.sbuf("den", [128, 16], F32)
    sq_sb = L.sbuf("sq_sb", [128, 128], F32)
    sd05_sb = L.sbuf("sd05_sb", [128, NC], F32)
    P = AR2[:, 0:NC * TOK]
    PT = AR2[:, NC * TOK:2 * NC * TOK]
    xh = AR1
    hbuf = AR2
    yp = AR3[:, 0:2 * NC * TOK].bitcast(F32)
    sqstage = yp[:, 0:2 * TG]

    S.dma("sp", "cst", lambda s: s.dma_start(out=iota_sb, in_=iota[:, :]))
    S.dma("sp", "cst", lambda s: s.dma_start(out=thr_sb[:, :], in_=thr[:, :]))
    ev3 = S.dma("sp", "cst", lambda s: s.dma_start(out=ident_sb[:, :], in_=ident[:, :]))
    t["iota_ev"] = ev3
    t["thr_ev"] = ev3
    t["ident_ev"] = ev3
    ev4 = S.dma("pool", "cb", lambda g: g.dma_start(out=bo_sb[:, :], in_=bo[:, :]))
    S.dma("pool", "cb", lambda g: g.dma_start(out=sq_sb[:, :], in_=sq1[:, :]))
    ev_s2 = S.dma("pool", "cb", lambda g: g.dma_start(out=sd05_sb[:, :], in_=sd1[:, :]))
    ev_sd = S.emit("dve", lambda v: v.tensor_scalar_mul(sd05_sb[:, :], sd05_sb[:, :], 0.5),
                   waits=[S.ch_now("cb")], inc=True)

    scr = dict(nm=nm, den=den, outstage=outstage, qm=qm, nm8=nm8, nm1=nm1)
    ev = attn_phase(L, t, S_ps, None, kh, vh, vtm, P, PT, iota_sb, thr_sb, ident_sb,
                    kfull, vfull, outT.ap(), scr, pre=[ev3], qmine_in=qmine.ap())
    wfree = [None, None]
    ev = wo_phase(L, t, outT.ap(), Wo, bo_sb, resid1.ap(), resid2.ap(), wbuf, banks,
                  AR1, xstage, wfree, pre=ev + [ev4])
    ev2 = ln_phase(L, t, resid2.ap(), xh, banks, xstage, sqstage, pre=ev)
    ev2 = ffn_phase(L, t, xh, hbuf, yp, wbuf, banks, xstage,
                    wgT0, wuT0, wdT0, quant=False, sq_sb=None, sd05_ap=lambda dc: 0.5,
                    xold=resid2.ap(), resid_out=resid3.ap(), xh_ready=ev2, wfree=wfree)
    ev2 = ln_phase(L, t, resid3.ap(), xh, banks, xstage, sqstage, pre=[ev2], sx=True)
    ev2 = ffn_phase(L, t, xh, hbuf, yp, wbuf, banks, xstage,
                    wgT1, wuT1, wdT1, quant=True, sq_sb=sq_sb,
                    sd05_ap=lambda dc: sd05_sb[:, dc:dc + 1],
                    xold=resid3.ap(), resid_out=resid4.ap(), xh_ready=ev2, wfree=wfree)
    ev2 = ln_phase(L, t, resid4.ap(), xh, banks, xstage, sqstage, pre=[ev2])
    pq = hbuf[:, 0:2 * TOK]
    pk = hbuf[:, 2 * TOK:4 * TOK]
    pv = hbuf[:, 4 * TOK:6 * TOK]
    e1 = proj_phase(L, t, xh, pq, wbuf, banks, Wq1, None, qT1.ap(), ev2, wfree=wfree)
    e2 = proj_phase(L, t, xh, pk, wbuf, banks, Wk1, None, kT1.ap(), e1, wfree=wfree)
    e3 = proj_phase(L, t, xh, pv, wbuf, banks, Wv1, None, vT1.ap(), e2, wfree=wfree)
    S.dma("sp", "fin", lambda s: s.nop(),
          waits=[e3, S.ch_now("po0"), S.ch_now("po1"), S.ch_now("sr")])
    return L


def build_launch_C():
    L = LaunchBuilder("C")
    resid4 = L.inp("resid4", [D, TOK], F32)
    q1 = L.inp("q1", [D, TOK], BF16)
    kfull = L.inp("kfull", [D, T], BF16)
    vfull = L.inp("vfull", [D, T], BF16)
    iota = L.inp("iota", [128, T], F32)
    thr = L.inp("thr", [128, 8], F32)
    ident = L.inp("ident", [128, 128], BF16)
    gmask = L.inp("gmask", [8, 128, NB], F32)
    Wo = L.inp("Wo", [128, 16, 16, 128], F32)
    wgT1 = L.inp("wgT1", [128, 64, 16, 128], I32)
    wuT1 = L.inp("wuT1", [128, 64, 16, 128], I32)
    wdT1 = L.inp("wdT1", [128, 16, 64, 128], I32)
    sq1 = L.inp("sq1", [128, 128], F32)
    sd1 = L.inp("sd1", [128, NC], F32)
    yT = L.out("yT", [D, TOK], F32)
    outT = L.scratch("outT", [D, TOK], BF16)
    resid5 = L.scratch("resid5", [D, TOK], F32)

    L.begin()
    S = L.S
    t = alloc_common(L)
    S_ps = L.es.enter_context(L.nc.psum_tensor("S_ps", [128, 4096], F32))
    banks = [S_ps[:, b * 512:(b + 1) * 512] for b in range(8)]
    AR1 = L.sbuf("AR1", [128, NC * TOK], BF16)
    AR2 = L.sbuf("AR2", [128, 2 * NC * TOK], BF16)
    AR3 = L.sbuf("AR3", [128, 2 * NC * TOK], BF16)
    wbuf = [L.sbuf("wb0", [128, 4096], BF16), L.sbuf("wb1", [128, 4096], BF16)]
    xstage = L.sbuf("xstage", [128, 2 * TG], F32)
    kh = AR3[:, 0:4096]
    vh = AR3[:, 4096:8192]
    vtm = AR3[:, 8192:12288]
    outstage = AR3[:, 13312:14336]
    iota_sb = AR3[:, 16384:24576].bitcast(F32)
    b01 = AR3[:, 24576:32768]
    gate_all = AR2[:, 0:NC * TOK].bitcast(F32)
    gwork = AR2[:, NC * TOK:2 * NC * TOK].bitcast(F32)
    eqm = AR3[:, 14336:16384].bitcast(F32)
    thr_sb = L.sbuf("thr_sb", [128, 8], F32)
    ident_sb = AR3[:, 12288:12416]
    gm_sb = AR3[:, 4096:5120].bitcast(F32)
    kg_sb = AR3[:, 6144:7168]
    tau = L.sbuf("tau", [128, 16], F32)
    nm = L.sbuf("nm", [128, 16], F32)
    nm8 = L.sbuf("nm8", [128, 8], F32)
    nm1 = L.sbuf("nm1", [128, 1], F32)
    den = L.sbuf("den", [128, 16], F32)
    sq_sb = L.sbuf("sq_sb", [128, 128], F32)
    sd05_sb = L.sbuf("sd05_sb", [128, NC], F32)
    xh = AR1
    hbuf = AR2
    yp = AR3[:, 0:2 * NC * TOK].bitcast(F32)
    sqstage = yp[:, 0:2 * TG]

    S.dma("sp", "cst", lambda s: s.dma_start(out=iota_sb, in_=iota[:, :]))
    S.dma("sp", "cst", lambda s: s.dma_start(out=thr_sb[:, :], in_=thr[:, :]))
    S.dma("sp", "cst", lambda s: s.dma_start(
        out=gm_sb.rearrange("p (q n) -> p q n", n=NB),
        in_=gmask.ap().rearrange("q p n -> p q n")))
    ev3 = S.dma("sp", "cst", lambda s: s.dma_start(out=ident_sb[:, :], in_=ident[:, :]))
    t["iota_ev"] = ev3
    t["thr_ev"] = ev3
    t["ident_ev"] = ev3
    S.dma("pool", "cb", lambda g: g.dma_start(out=sq_sb[:, :], in_=sq1[:, :]))
    S.dma("pool", "cb", lambda g: g.dma_start(out=sd05_sb[:, :], in_=sd1[:, :]))
    S.emit("dve", lambda v: v.tensor_scalar_mul(sd05_sb[:, :], sd05_sb[:, :], 0.5),
           waits=[S.ch_now("cb")], inc=True)
    ev_q1 = S.dma("sp", "q1", lambda s: s.dma_start(
        out=AR1.ap().rearrange("p (c n) -> p c n", n=TOK),
        in_=q1.ap().rearrange("(c p) n -> p c n", p=128)))

    kh_free = []
    ge_evs = []
    for hh in range(16):
        ev_k = S.dma("sp", "kh", lambda s, hh=hh: s.dma_start(
            out=kh[:, :], in_=kfull[hh * 128:(hh + 1) * 128, :]), waits=kh_free)
        ev_kg = S.emit("dve", lambda v, hh=hh: v.tensor_reduce(
            eqm.rearrange("p (a n) -> p a n", n=1)[:, 0:NB, :],
            kh.rearrange("p (a n) -> p a n", n=BLK),
            AX.X, ALU.add), waits=[ev_k], inc=True)
        ev_kg = S.emit("dve", lambda v, hh=hh: v.tensor_scalar_mul(
            kg_sb[:, hh * NB:(hh + 1) * NB], eqm[:, 0:NB], 1.0 / BLK), inc=True)
        mm = None
        for qt in range(8):
            mm = S.emit("pe", lambda p, hh=hh, qt=qt: p.matmul(
                S_ps[:, 5 * 512 + qt * NB: 5 * 512 + (qt + 1) * NB],
                AR1[:, hh * TOK + qt * 128: hh * TOK + (qt + 1) * 128],
                kg_sb[:, hh * NB:(hh + 1) * NB], start=True, stop=True),
                waits=[ev_kg, ev_q1, ge_evs[-1] if ge_evs else None],
                inc=(qt == 7))
        kh_free = [ev_kg]
        ev_g = S.emit("dve", lambda v, hh=hh: v.tensor_tensor(
            gate_all.rearrange("p (q a n) -> p (q a) n", a=16, n=NB)[:, hh::16, :],
            S_ps[:, 5 * 512:6 * 512].rearrange("p (q n) -> p q n", n=NB),
            gm_sb.rearrange("p (q n) -> p q n", n=NB), ALU.add),
            waits=[mm, S.ch_now("cst")], inc=True)
        ge_evs.append(ev_g)
    b01_evs = []
    for qt in range(8):
        W3 = gwork.rearrange("p (q x) -> p q x", x=TOK)[:, qt, :].rearrange(
            "p (a n) -> p a n", n=NB)
        G3 = gate_all.rearrange("p (q x) -> p q x", x=TOK)[:, qt, :].rearrange(
            "p (a n) -> p a n", n=NB)
        ev = S.emit("dve", lambda v, W3=W3, G3=G3: v.tensor_copy(W3, G3),
                    waits=ge_evs, inc=True)
        for it in range(7):
            ev = S.emit("dve", lambda v, W3=W3: v.tensor_reduce(
                tau.ap().rearrange("p (a n) -> p a n", n=1), W3, AX.X, ALU.max),
                inc=True)
            ev = S.emit("dve", lambda v, W3=W3: v.tensor_tensor(
                eqm.rearrange("p (a n) -> p a n", n=NB), W3,
                tau.ap().unsqueeze(2).broadcast_to([128, 16, NB]), ALU.is_equal),
                inc=True)
            ev = S.emit("dve", lambda v, W3=W3: v.scalar_tensor_tensor(
                W3, eqm.rearrange("p (a n) -> p a n", n=NB), -2.0e30, W3,
                ALU.mult, ALU.add), inc=True)
        ev = S.emit("dve", lambda v, W3=W3: v.tensor_reduce(
            tau.ap().rearrange("p (a n) -> p a n", n=1), W3, AX.X, ALU.max),
            inc=True)
        ev = S.emit("dve", lambda v, G3=G3, qt=qt: v.tensor_tensor(
            b01.rearrange("p (q a n) -> p q a n", a=16, n=NB)[:, qt, :, :], G3,
            tau.ap().unsqueeze(2).broadcast_to([128, 16, NB]), ALU.is_ge),
            inc=True)
        b01_evs.append(ev)

    scr = dict(nm=nm, den=den, outstage=outstage, qm=None, nm8=nm8, nm1=nm1)
    ev = attn_phase(L, t, S_ps, None, kh, vh, vtm, AR2[:, 0:NC * TOK],
                    AR2[:, NC * TOK:2 * NC * TOK], iota_sb, thr_sb, ident_sb,
                    kfull, vfull, outT.ap(), scr, pre=b01_evs,
                    ar1_q=AR1, b01=b01)
    wfree = [None, None]
    ev = wo_phase(L, t, outT.ap(), Wo, None, resid4.ap(), resid5.ap(), wbuf, banks,
                  AR1, xstage, wfree, pre=ev, ar1_free=[S.now("pe")])
    ev2 = ln_phase(L, t, resid5.ap(), xh, banks, xstage, sqstage, pre=ev, sx=True)
    ev2 = ffn_phase(L, t, xh, hbuf, yp, wbuf, banks, xstage,
                    wgT1, wuT1, wdT1, quant=True, sq_sb=sq_sb,
                    sd05_ap=lambda dc: sd05_sb[:, dc:dc + 1],
                    xold=resid5.ap(), resid_out=yT.ap(), xh_ready=ev2, wfree=wfree)
    S.dma("sp", "fin", lambda s: s.nop(), waits=[ev2, S.ch_now("sr")])
    return L


_NCORES = 8
_BF = ml_dtypes.bfloat16
_PROG_CACHE = {}


def _prog(name, builder):
    if name not in _PROG_CACHE:
        L = builder()
        _PROG_CACHE[name] = L.finish()
    return _PROG_CACHE[name]


def _build_qmine(qflatT_b, qr):
    A = qflatT_b.reshape(16, 128, 4096)
    out = np.empty((16, 128, 1024), dtype=qflatT_b.dtype)
    for hh in range(16):
        sub = A[:, :, hh * 256 + 64 * qr: hh * 256 + 64 * qr + 64]
        out[hh] = np.transpose(sub, (1, 2, 0)).reshape(128, 1024)
    return out


def kernel(L0_attn_x, L0_attn_Wq, L0_attn_bq, L0_attn_Wk, L0_attn_bk,
           L0_attn_Wv, L0_attn_bv, L0_attn_Wo, L0_attn_bo,
           L0_ffn_w_gate, L0_ffn_w_up, L0_ffn_w_down,
           L1_attn_Wq, L1_attn_Wk, L1_attn_Wv, L1_attn_Wo,
           L1_ffn_gate_w, L1_ffn_gate_scale, L1_ffn_up_w, L1_ffn_up_scale,
           L1_ffn_down_w, L1_ffn_down_scale):
    C = np.ascontiguousarray

    def gu_img(w):
        return C(np.asarray(w).reshape(64, 128, 16, 128).transpose(3, 0, 2, 1))

    def dn_img(w):
        return C(np.asarray(w).reshape(16, 128, 64, 128).transpose(3, 0, 2, 1))

    def pj_img(w):
        return C(np.asarray(w, np.float32).reshape(16, 128, 16, 128).transpose(1, 2, 0, 3))

    x = np.asarray(L0_attn_x, np.float32)
    wgT0 = gu_img(np.asarray(L0_ffn_w_gate, np.float32))
    wuT0 = gu_img(np.asarray(L0_ffn_w_up, np.float32))
    wdT0 = dn_img(np.asarray(L0_ffn_w_down, np.float32))
    b_qkv = C(np.concatenate([np.asarray(b, np.float32).reshape(16, 128).T
                              for b in (L0_attn_bq, L0_attn_bk, L0_attn_bv)], axis=1))
    bo = C(np.asarray(L0_attn_bo, np.float32).reshape(16, 128).T)
    wgT1 = gu_img(np.asarray(L1_ffn_gate_w, np.int32))
    wuT1 = gu_img(np.asarray(L1_ffn_up_w, np.int32))
    wdT1 = dn_img(np.asarray(L1_ffn_down_w, np.int32))
    sq1 = C(np.concatenate([np.asarray(L1_ffn_gate_scale, np.float32).reshape(64, 128).T,
                            np.asarray(L1_ffn_up_scale, np.float32).reshape(64, 128).T],
                           axis=1))
    sd1 = C(np.asarray(L1_ffn_down_scale, np.float32).reshape(16, 128).T)
    iota = np.broadcast_to(np.arange(T, dtype=np.float32), (128, T)).copy()
    ident = np.eye(128, dtype=_BF)
    thr = [(1024 * qr + np.arange(8)[None, :] * 128 +
            np.arange(128)[:, None]).astype(np.float32) for qr in range(4)]
    gmask = []
    for qr in range(4):
        s_abs = qr * 1024 + np.arange(8)[:, None] * 128 + np.arange(128)[None, :]
        blk = np.arange(NB)[None, None, :]
        self_m = (s_abs[:, :, None] // BLK) == blk
        notpast = (s_abs[:, :, None] < (blk + 1) * BLK) & ~self_m
        gmask.append(C(np.where(self_m, BIGM, np.where(notpast, -BIGM, 0.0))
                     .astype(np.float32)))

    im = []
    for c in range(_NCORES):
        b, qr = c // 4, c % 4
        im.append(dict(xT=C(x[b, qr * TOK:(qr + 1) * TOK].T), wgT=wgT0, wuT=wuT0,
                       wdT=wdT0, Wq=pj_img(L0_attn_Wq), Wk=pj_img(L0_attn_Wk),
                       Wv=pj_img(L0_attn_Wv), b_qkv=b_qkv))
    rA = run_bass_kernel_spmd(_prog("A", build_launch_A), im,
                              core_ids=list(range(_NCORES))).results
    qfT = [np.concatenate([rA[b * 4 + qr]["qT"] for qr in range(4)], axis=1)
           for b in range(2)]
    kfT = [C(np.concatenate([rA[b * 4 + qr]["kT"] for qr in range(4)], axis=1))
           for b in range(2)]
    vfT = [C(np.concatenate([rA[b * 4 + qr]["vT"] for qr in range(4)], axis=1))
           for b in range(2)]

    im = []
    for c in range(_NCORES):
        b, qr = c // 4, c % 4
        im.append(dict(resid1=rA[c]["resid1"], qmine=C(_build_qmine(qfT[b], qr)),
                       kfull=kfT[b], vfull=vfT[b], iota=iota, thr=thr[qr],
                       ident=ident, Wo=pj_img(L0_attn_Wo), bo=bo))
    for c in range(_NCORES):
        im[c].update(dict(wgT0=wgT0, wuT0=wuT0, wdT0=wdT0,
                          wgT1=wgT1, wuT1=wuT1, wdT1=wdT1, sq1=sq1, sd1=sd1,
                          Wq1=pj_img(L1_attn_Wq), Wk1=pj_img(L1_attn_Wk),
                          Wv1=pj_img(L1_attn_Wv)))
    rB2 = run_bass_kernel_spmd(_prog("B", build_launch_B), im,
                               core_ids=list(range(_NCORES))).results
    k1T = [C(np.concatenate([rB2[b * 4 + qr]["kT1"] for qr in range(4)], axis=1))
           for b in range(2)]
    v1T = [C(np.concatenate([rB2[b * 4 + qr]["vT1"] for qr in range(4)], axis=1))
           for b in range(2)]

    im = []
    for c in range(_NCORES):
        b, qr = c // 4, c % 4
        im.append(dict(resid4=rB2[c]["resid4"], q1=rB2[c]["qT1"], kfull=k1T[b],
                       vfull=v1T[b], iota=iota, thr=thr[qr], ident=ident,
                       gmask=gmask[qr], Wo=pj_img(L1_attn_Wo)))
    for c in range(_NCORES):
        im[c].update(dict(wgT1=wgT1, wuT1=wuT1, wdT1=wdT1, sq1=sq1, sd1=sd1))
    rC2 = run_bass_kernel_spmd(_prog("C", build_launch_C), im,
                               core_ids=list(range(_NCORES))).results

    out = np.empty((2, T, D), np.float32)
    for c in range(_NCORES):
        b, qr = c // 4, c % 4
        out[b, qr * TOK:(qr + 1) * TOK] = rC2[c]["yT"].T
    return out


# version 06
# speedup vs baseline: 1.0280x; 1.0280x over previous
import contextlib
import numpy as np
import ml_dtypes

import concourse.bass as bass
import concourse.mybir as mybir
from concourse.bass_utils import run_bass_kernel_spmd

F32 = mybir.dt.float32
BF16 = mybir.dt.bfloat16
I32 = mybir.dt.int32
AF = mybir.ActivationFunctionType
ALU = mybir.AluOpType
AX = mybir.AxisListType

D = 2048
NC = 16
TOK = 1024
NTG = 2
TG = 512
HID = 8192
T = 4096
H = 16
DH = 128
SM = float(1.0 / np.sqrt(128.0))
EPS = 1e-5
NB = 64
BLK = 64
TOPK = 8
BIGM = 1.0e30


class Sch:

    ENGS = ("pe", "dve", "act", "sp", "pool")
    SERIAL = ("dve", "act")

    def __init__(self, nc):
        self.nc = nc
        self.q = {e: [] for e in self.ENGS}
        self.cnt = {e: 0 for e in self.ENGS}
        self.ch_cnt = {}
        self._lastw = {}

    def emit(self, eng, fn, waits=(), inc=False):
        assert eng in ("pe", "dve", "act")
        waits = list(waits)
        if eng in self.SERIAL:
            waits.append((eng, self.cnt[eng]))
            inc = True
        ws = self._filter(eng, waits)
        if inc:
            self.cnt[eng] += 1
            out = (eng, self.cnt[eng])
        else:
            out = None
        self.q[eng].append((ws, fn, eng if inc else None))
        return out

    def dma(self, queue, channel, fn, waits=()):
        assert queue in ("sp", "pool")
        ws = self._filter(queue, list(waits))
        self.ch_cnt[channel] = self.ch_cnt.get(channel, 0) + 16
        self.q[queue].append((ws, fn, channel))
        return (channel, self.ch_cnt[channel])

    def _filter(self, eng, waits):
        ws = []
        for ev in waits:
            if ev is None:
                continue
            tgt, val = ev
            if tgt == eng and tgt in ("pe",):
                continue
            key = (eng, tgt)
            if self._lastw.get(key, -1) >= val:
                continue
            self._lastw[key] = val
            ws.append((tgt, val))
        return ws

    def now(self, eng):
        return (eng, self.cnt[eng])

    def ch_now(self, channel):
        return (channel, self.ch_cnt.get(channel, 0))

    def build(self, block, mksem):
        engmap = dict(pe="tensor", dve="vector", act="scalar", sp="sync", pool="gpsimd")
        sems = {}

        def sem(name):
            if name not in sems:
                sems[name] = mksem(name)
            return sems[name]

        for eng in self.ENGS:
            for ws, fn, inc_t in self.q[eng]:
                for tgt, val in ws:
                    sem(tgt)
                if inc_t is not None:
                    sem(inc_t)

        def reg(eng):
            items = self.q[eng]

            def body(e):
                for ws, fn, inc_t in items:
                    for tgt, val in ws:
                        e.wait_ge(sems[tgt], val)
                    inst = fn(e)
                    if inc_t is not None:
                        inst.then_inc(sems[inc_t], 16 if eng in ("sp", "pool") else 1)
            body.__name__ = f"body_{eng}"
            getattr(block, engmap[eng])(body)

        for eng in self.ENGS:
            if self.q[eng]:
                reg(eng)


class LaunchBuilder:
    def __init__(self, name):
        self.name = name
        self.nc = bass.Bass("TRN2", target_bir_lowering=False, debug=False)
        self.es = contextlib.ExitStack()
        self.sb_bytes = 0
        self.dram = {}

    def inp(self, name, shape, dt):
        t = self.nc.dram_tensor(name, list(shape), dt, kind="ExternalInput")
        self.dram[name] = t
        return t

    def out(self, name, shape, dt):
        t = self.nc.dram_tensor(name, list(shape), dt, kind="ExternalOutput")
        self.dram[name] = t
        return t

    def scratch(self, name, shape, dt):
        t = self.nc.dram_tensor(name, list(shape), dt)
        self.dram[name] = t
        return t

    def begin(self):
        self.block = self.es.enter_context(self.nc.Block())
        self.S = Sch(self.nc)

    def sbuf(self, name, shape, dt):
        n = 1
        for s in shape[1:]:
            n *= s
        self.sb_bytes += n * {F32: 4, BF16: 2, I32: 4}[dt] * 128
        return self.es.enter_context(self.nc.sbuf_tensor(name, list(shape), dt))

    def psum_banks(self, n=8):
        return [self.es.enter_context(self.nc.psum_tensor(f"bank{i}", [128, 512], F32))
                for i in range(n)]

    def finish(self):
        self.S.build(self.block,
                     lambda name: self.es.enter_context(self.nc.semaphore(f"s_{name}")))
        self.es.close()
        return self.nc


def alloc_common(L):
    t = {}
    t["ones_col_f"] = L.sbuf("ones_col_f", [128, 1], F32)
    t["ones_col_b"] = L.sbuf("ones_col_b", [128, 1], BF16)
    t["ones_row_f"] = L.sbuf("ones_row_f", [1, 128], F32)
    t["negk_row_f"] = L.sbuf("negk_row_f", [1, 128], F32)
    t["negk_row_b"] = L.sbuf("negk_row_b", [1, 128], BF16)
    t["corr"] = L.sbuf("corr", [1, 3 * TOK], BF16)
    t["stats"] = L.sbuf("stats", [1, 4 * TOK], F32)
    t["eps"] = L.sbuf("eps", [1, 1], F32)
    S = L.S
    S.emit("dve", lambda v: v.memset(t["ones_col_f"][:, :], 1.0))
    S.emit("dve", lambda v: v.memset(t["ones_col_b"][:, :], 1.0))
    S.emit("dve", lambda v: v.memset(t["ones_row_f"][:, :], 1.0))
    S.emit("dve", lambda v: v.memset(t["negk_row_f"][:, :], -128.0))
    S.emit("dve", lambda v: v.memset(t["negk_row_b"][:, :], -128.0))
    ev = S.emit("dve", lambda v: v.memset(t["eps"][:, :], EPS), inc=True)
    t["consts_ev"] = ev
    return t


def ln_phase(L, t, xsrc, xh_sb, banks, xstage, sqstage, pre=(), sx=False):
    S = L.S
    st = t["stats"]
    R = lambda r, tg: st[0:1, r * TOK + tg * TG: r * TOK + (tg + 1) * TG]
    xslot_free = [None, None]
    sqslot_free = [None, None]
    pe_ev = None
    for c in range(NC):
        for tg in range(NTG):
            sl = (c * NTG + tg) % 2
            xs = xstage[:, sl * TG:(sl + 1) * TG]
            qs = sqstage[:, sl * TG:(sl + 1) * TG]
            ev_ld = S.dma("sp", f"xs{sl}", lambda s, c=c, tg=tg, xs=xs: s.dma_start(
                out=xs, in_=xsrc[c * 128:(c + 1) * 128, tg * TG:(tg + 1) * TG]),
                waits=(xslot_free[sl] or []) + list(pre))
            ev_sq = S.emit("act", lambda a, xs=xs, qs=qs: a.activation(qs, xs, AF.Square),
                           waits=[ev_ld] + (sqslot_free[sl] or []), inc=True)
            ev_m1 = S.emit("pe", lambda p, c=c, tg=tg, xs=xs: p.matmul(
                banks[tg][0:1, :], t["ones_col_f"][:, :], xs,
                start=(c == 0), stop=(c == NC - 1)),
                waits=[ev_ld, t["consts_ev"]], inc=True)
            pe_ev = S.emit("pe", lambda p, c=c, tg=tg, qs=qs: p.matmul(
                banks[2 + tg][0:1, :], t["ones_col_f"][:, :], qs,
                start=(c == 0), stop=(c == NC - 1)),
                waits=[ev_sq], inc=True)
            xslot_free[sl] = [ev_m1, ev_sq]
            sqslot_free[sl] = [pe_ev]
    for tg in range(NTG):
        S.emit("dve", lambda v, tg=tg: v.tensor_copy(R(0, tg), banks[tg][0:1, :]), waits=[pe_ev])
        S.emit("dve", lambda v, tg=tg: v.tensor_copy(R(1, tg), banks[2 + tg][0:1, :]))
    S.emit("dve", lambda v: v.tensor_scalar_mul(st[0:1, 2 * TOK:3 * TOK], st[0:1, 0:TOK], 1.0 / D))
    S.emit("dve", lambda v: v.tensor_scalar_mul(st[0:1, 1 * TOK:2 * TOK], st[0:1, 1 * TOK:2 * TOK], 1.0 / D))
    S.emit("dve", lambda v: v.tensor_tensor(st[0:1, 0:TOK], st[0:1, 2 * TOK:3 * TOK],
                                            st[0:1, 2 * TOK:3 * TOK], ALU.mult))
    ev = S.emit("dve", lambda v: v.tensor_tensor(st[0:1, 1 * TOK:2 * TOK], st[0:1, 1 * TOK:2 * TOK],
                                                 st[0:1, 0:TOK], ALU.subtract), inc=True)
    ev = S.emit("act", lambda a: a.activation(st[0:1, 1 * TOK:2 * TOK], st[0:1, 1 * TOK:2 * TOK],
                                              AF.Sqrt, bias=t["eps"][:, :]), waits=[ev], inc=True)
    ev = S.emit("dve", lambda v: v.reciprocal(st[0:1, 3 * TOK:4 * TOK], st[0:1, 1 * TOK:2 * TOK]),
                waits=[ev], inc=True)
    ev_mr = S.emit("dve", lambda v: v.tensor_tensor(st[0:1, 2 * TOK:3 * TOK], st[0:1, 2 * TOK:3 * TOK],
                                                    st[0:1, 3 * TOK:4 * TOK], ALU.mult), inc=True)
    bev = None
    for tg in range(NTG):
        S.emit("pe", lambda p, tg=tg: p.matmul(banks[4 + tg][:, :], t["ones_row_f"][:, :],
                                               R(3, tg), start=True, stop=True), waits=[ev_mr])
        bev = S.emit("pe", lambda p, tg=tg: p.matmul(banks[6 + tg][:, :], t["ones_row_f"][:, :],
                                                     R(2, tg), start=True, stop=True), inc=True)
    last = None
    for c in range(NC):
        for tg in range(NTG):
            sl = (c * NTG + tg) % 2
            xs = xstage[:, sl * TG:(sl + 1) * TG]
            qs = sqstage[:, sl * TG:(sl + 1) * TG]
            ev_ld = S.dma("sp", f"xs{sl}", lambda s, c=c, tg=tg, xs=xs: s.dma_start(
                out=xs, in_=xsrc[c * 128:(c + 1) * 128, tg * TG:(tg + 1) * TG]),
                waits=(xslot_free[sl] or []))
            S.emit("dve", lambda v, tg=tg, xs=xs, qs=qs: v.tensor_tensor(
                qs, xs, banks[4 + tg][:, :], ALU.mult),
                waits=[ev_ld, bev] + (sqslot_free[sl] or []))
            last = S.emit("dve", lambda v, c=c, tg=tg, qs=qs: v.tensor_tensor(
                xh_sb[:, c * TOK + tg * TG: c * TOK + (tg + 1) * TG],
                qs, banks[6 + tg][:, :], ALU.subtract), inc=True)
            xslot_free[sl] = [last]
            sqslot_free[sl] = [last]
    if sx:
        for c in range(NC):
            for tg in range(NTG):
                pe_ev = S.emit("pe", lambda p, c=c, tg=tg: p.matmul(
                    banks[tg][0:1, :], t["ones_col_b"][:, :],
                    xh_sb[:, c * TOK + tg * TG: c * TOK + (tg + 1) * TG],
                    start=(c == 0), stop=(c == NC - 1)),
                    waits=[last], inc=(c == NC - 1))
        for tg in range(NTG):
            S.emit("dve", lambda v, tg=tg: v.tensor_copy(R(0, tg), banks[tg][0:1, :]),
                   waits=[pe_ev], inc=True)
        last = S.emit("dve", lambda v: v.tensor_copy(t["corr"][0:1, 0:TOK], st[0:1, 0:TOK]),
                      inc=True)
    return last


def ffn_phase(L, t, xh_sb, h_sb, yp_sb, wbuf, banks, rstage,
              wgT, wuT, wdT, quant, sq_sb, sd05_ap,
              xold, resid_out, xh_ready, wfree=None):
    S = L.S
    st = t["stats"]
    bank_free = {}
    wslot_free = wfree if wfree is not None else [None, None]
    h_ev = {}
    y_ev = {}
    rslot_free = [[xh_ready], [xh_ready]]
    res_done = []

    for half in range(2):
        for which in ("g", "u"):
            W = wgT if which == "g" else wuT
            for icp in range(16):
                sl = icp % 2
                ev_w = S.dma("pool", f"wb{sl}", lambda g, icp=icp, half=half, sl=sl, W=W: g.dma_start(
                    out=wbuf[sl].ap().rearrange("p (c m) -> p c m", m=128),
                    in_=W.ap().rearrange("p a c m -> p (a c) m")[
                        :, (half * 32 + icp * 2) * 16:(half * 32 + icp * 2 + 2) * 16, :]),
                    waits=[wslot_free[sl]])
                for sub in range(2):
                    ic = icp * 2 + sub
                    gi = half * 32 + ic
                    mm_last = [None, None]
                    for c in range(NC):
                        for tg in range(NTG):
                            pb = banks[sl * 2 + tg]
                            ev = S.emit("pe", lambda p, c=c, tg=tg, sl=sl, pb=pb, sub=sub: p.matmul(
                                pb[:, :], wbuf[sl][:, sub * 2048 + c * 128: sub * 2048 + (c + 1) * 128],
                                xh_sb[:, c * TOK + tg * TG: c * TOK + (tg + 1) * TG],
                                start=(c == 0), stop=(not quant and c == NC - 1)),
                                waits=[ev_w, xh_ready, bank_free.get(sl * 2 + tg)],
                                inc=(not quant and c == NC - 1))
                            mm_last[tg] = ev
                    if quant:
                        for tg in range(NTG):
                            pb = banks[sl * 2 + tg]
                            mm_last[tg] = S.emit("pe", lambda p, tg=tg, pb=pb: p.matmul(
                                pb[:, :], t["negk_row_b"][:, :],
                                t["corr"][0:1, tg * TG:(tg + 1) * TG],
                                start=False, stop=True), inc=True)
                    for tg in range(NTG):
                        pb = banks[sl * 2 + tg]
                        hsl = h_sb[:, ic * TOK + tg * TG: ic * TOK + (tg + 1) * TG]
                        if which == "g":
                            if quant:
                                ev = S.emit("act", lambda a, pb=pb, hsl=hsl, gi=gi: a.activation(
                                    hsl, pb[:, :], AF.Silu, scale=sq_sb[:, gi:gi + 1]),
                                    waits=[mm_last[tg]], inc=True)
                            else:
                                ev = S.emit("act", lambda a, pb=pb, hsl=hsl: a.activation(
                                    hsl, pb[:, :], AF.Silu), waits=[mm_last[tg]], inc=True)
                        else:
                            if quant:
                                ev = S.emit("dve", lambda v, pb=pb, hsl=hsl, gi=gi: v.scalar_tensor_tensor(
                                    hsl, pb[:, :], sq_sb[:, 64 + gi:64 + gi + 1], hsl,
                                    ALU.mult, ALU.mult),
                                    waits=[mm_last[tg], h_ev.get((ic, tg))], inc=True)
                            else:
                                ev = S.emit("dve", lambda v, pb=pb, hsl=hsl: v.tensor_mul(
                                    hsl, pb[:, :], hsl),
                                    waits=[mm_last[tg], h_ev.get((ic, tg))], inc=True)
                        bank_free[sl * 2 + tg] = ev
                        h_ev[(ic, tg)] = ev
                wslot_free[sl] = mm_last[1]
        if quant:
            pe_ev = None
            for ic in range(32):
                for tg in range(NTG):
                    pe_ev = S.emit("pe", lambda p, ic=ic, tg=tg: p.matmul(
                        banks[4 + tg][0:1, :], t["ones_col_b"][:, :],
                        h_sb[:, ic * TOK + tg * TG: ic * TOK + (tg + 1) * TG],
                        start=(ic == 0), stop=(ic == 31)),
                        waits=[h_ev[(ic, tg)], bank_free.get(4 + tg)],
                        inc=(ic == 31))
            for tg in range(NTG):
                shev = S.emit("dve", lambda v, tg=tg: v.tensor_copy(
                    st[0:1, 1 * TOK + tg * TG: 1 * TOK + (tg + 1) * TG],
                    banks[4 + tg][0:1, :]), waits=[pe_ev], inc=True)
                bank_free[4 + tg] = shev
            shev = S.emit("dve", lambda v: v.tensor_copy(
                t["corr"][0:1, TOK:2 * TOK], st[0:1, TOK:2 * TOK]), inc=True)
            shev = S.emit("dve", lambda v: v.tensor_sub(
                t["corr"][0:1, 2 * TOK:3 * TOK], st[0:1, TOK:2 * TOK],
                t["corr"][0:1, TOK:2 * TOK]), inc=True)
        for dc in range(NC):
            sl = dc % 2
            ev_w = S.dma("pool", f"wb{sl}", lambda g, dc=dc, half=half, sl=sl: g.dma_start(
                out=wbuf[sl].ap().rearrange("p (c m) -> p c m", m=128),
                in_=wdT[:, dc, half * 32:(half + 1) * 32, :]),
                waits=[wslot_free[sl]])
            mm_last = [None, None]
            for c in range(32):
                for tg in range(NTG):
                    pb = banks[sl * 2 + tg]
                    ev = S.emit("pe", lambda p, c=c, tg=tg, sl=sl, pb=pb: p.matmul(
                        pb[:, :], wbuf[sl][:, c * 128:(c + 1) * 128],
                        h_sb[:, c * TOK + tg * TG: c * TOK + (tg + 1) * TG],
                        start=(c == 0), stop=(not quant and c == 31)),
                        waits=[ev_w, h_ev[(c, tg)], bank_free.get(sl * 2 + tg)],
                        inc=(not quant and c == 31))
                    mm_last[tg] = ev
            if quant:
                for tg in range(NTG):
                    pb = banks[sl * 2 + tg]
                    S.emit("pe", lambda p, tg=tg, pb=pb: p.matmul(
                        pb[:, :], t["negk_row_b"][:, :],
                        t["corr"][0:1, TOK + tg * TG: TOK + (tg + 1) * TG],
                        start=False, stop=False), waits=[shev])
                    mm_last[tg] = S.emit("pe", lambda p, tg=tg, pb=pb: p.matmul(
                        pb[:, :], t["negk_row_b"][:, :],
                        t["corr"][0:1, 2 * TOK + tg * TG: 2 * TOK + (tg + 1) * TG],
                        start=False, stop=True), inc=True)
            wslot_free[sl] = mm_last[1]
            for tg in range(NTG):
                pb = banks[sl * 2 + tg]
                ysl = yp_sb[:, dc * TOK + tg * TG: dc * TOK + (tg + 1) * TG]
                if half == 0:
                    ev = S.emit("dve", lambda v, pb=pb, ysl=ysl: v.tensor_copy(ysl, pb[:, :]),
                                waits=[mm_last[tg]], inc=True)
                    bank_free[sl * 2 + tg] = ev
                    y_ev[(dc, tg)] = ev
                else:
                    rsl = (dc * NTG + tg) % 2
                    rs = rstage[:, rsl * TG:(rsl + 1) * TG]
                    ev_x = S.dma("sp", f"rs{rsl}", lambda s, dc=dc, tg=tg, rs=rs: s.dma_start(
                        out=rs, in_=xold[dc * 128:(dc + 1) * 128, tg * TG:(tg + 1) * TG]),
                        waits=list(rslot_free[rsl]))
                    ev = S.emit("dve", lambda v, pb=pb, ysl=ysl: v.tensor_add(ysl, ysl, pb[:, :]),
                                waits=[mm_last[tg], y_ev[(dc, tg)]], inc=True)
                    bank_free[sl * 2 + tg] = ev
                    ev = S.emit("dve", lambda v, ysl=ysl, rs=rs, dc=dc: v.scalar_tensor_tensor(
                        ysl, ysl, sd05_ap(dc), rs, ALU.mult, ALU.add),
                        waits=[ev_x], inc=True)
                    rslot_free[rsl] = [ev]
                    ev2 = S.dma("sp", "sr", lambda s, dc=dc, tg=tg, ysl=ysl: s.dma_start(
                        out=resid_out[dc * 128:(dc + 1) * 128, tg * TG:(tg + 1) * TG],
                        in_=ysl), waits=[ev])
                    res_done.append(ev2)
    return res_done[-1]


def proj_phase(L, t, xh_sb, pbuf, wbuf, banks, Wt, bias_sb, out_hbm, xh_ready,
               extra=(), nout=NC, wfree=None, oscale=None):
    S = L.S
    wslot_free = wfree if wfree is not None else [None, None]
    pslot_free = [None, None]
    bank_free = {}
    last_st = None
    for oc in range(nout):
        sl = oc % 2
        ev_w = S.dma("pool", f"wb{sl}", lambda g, oc=oc, sl=sl: g.dma_start(
            out=wbuf[sl].ap().rearrange("p (c m) -> p c m", m=128)[:, 0:16, :],
            in_=Wt[:, oc, :, :]),
            waits=[wslot_free[sl]])
        mm_last = [None, None]
        for c in range(NC):
            for tg in range(NTG):
                pb = banks[sl * 2 + tg]
                ev = S.emit("pe", lambda p, c=c, tg=tg, sl=sl, pb=pb: p.matmul(
                    pb[:, :], wbuf[sl][:, c * 128:(c + 1) * 128],
                    xh_sb[:, c * TOK + tg * TG: c * TOK + (tg + 1) * TG],
                    start=(c == 0), stop=(c == NC - 1)),
                    waits=[ev_w, xh_ready, bank_free.get(sl * 2 + tg)],
                    inc=(c == NC - 1))
                mm_last[tg] = ev
        wslot_free[sl] = mm_last[1]
        for tg in range(NTG):
            pb = banks[sl * 2 + tg]
            dst = pbuf[:, sl * TOK + tg * TG: sl * TOK + (tg + 1) * TG]
            if bias_sb is not None and oscale is not None:
                ev = S.emit("dve", lambda v, pb=pb, dst=dst, oc=oc: v.tensor_scalar(
                    dst, pb[:, :], bias_sb[:, oc:oc + 1], oscale, ALU.add, ALU.mult),
                    waits=[mm_last[tg], pslot_free[sl]] + list(extra), inc=True)
            elif bias_sb is not None:
                ev = S.emit("dve", lambda v, pb=pb, dst=dst, oc=oc: v.tensor_scalar_add(
                    dst, pb[:, :], bias_sb[:, oc:oc + 1]),
                    waits=[mm_last[tg], pslot_free[sl]] + list(extra), inc=True)
            elif oscale is not None:
                ev = S.emit("dve", lambda v, pb=pb, dst=dst: v.tensor_scalar_mul(
                    dst, pb[:, :], oscale),
                    waits=[mm_last[tg], pslot_free[sl]] + list(extra), inc=True)
            else:
                ev = S.emit("dve", lambda v, pb=pb, dst=dst: v.tensor_copy(dst, pb[:, :]),
                            waits=[mm_last[tg], pslot_free[sl]] + list(extra), inc=True)
            bank_free[sl * 2 + tg] = ev
            last_st = S.dma("sp", f"po{sl}", lambda s, oc=oc, tg=tg, dst=dst: s.dma_start(
                out=out_hbm[oc * 128:(oc + 1) * 128, tg * TG:(tg + 1) * TG], in_=dst),
                waits=[ev])
            pslot_free[sl] = last_st
    return last_st


def build_launch_A(debug=False):
    L = LaunchBuilder("A")
    xT = L.inp("xT", [D, TOK], F32)
    wgT = L.inp("wgT", [128, 64, 16, 128], F32)
    wuT = L.inp("wuT", [128, 64, 16, 128], F32)
    wdT = L.inp("wdT", [128, 16, 64, 128], F32)
    Wq = L.inp("Wq", [128, 16, 16, 128], F32)
    Wk = L.inp("Wk", [128, 16, 16, 128], F32)
    Wv = L.inp("Wv", [128, 16, 16, 128], F32)
    b_qkv = L.inp("b_qkv", [128, 3 * NC], F32)
    resid1 = L.out("resid1", [D, TOK], F32)
    dbg = {}
    qT = L.out("qT", [D, TOK], BF16)
    kT = L.out("kT", [D, TOK], BF16)
    vT = L.out("vT", [D, TOK], BF16)

    L.begin()
    S = L.S
    t = alloc_common(L)
    banks = L.psum_banks()
    xh = L.sbuf("xh", [128, NC * TOK], BF16)
    h = L.sbuf("h", [128, 32 * TOK], BF16)
    yp = L.sbuf("yp", [128, NC * TOK], F32)
    wbuf = [L.sbuf("wb0", [128, 4096], BF16), L.sbuf("wb1", [128, 4096], BF16)]
    xstage = L.sbuf("xstage", [128, 2 * TG], F32)
    sqstage = yp[:, 0:2 * TG]
    bqkv_sb = L.sbuf("bqkv_sb", [128, 3 * NC], F32)
    ev_b = S.dma("pool", "cb", lambda g: g.dma_start(out=bqkv_sb[:, :], in_=b_qkv[:, :]))

    if debug:
        dbg["xh1"] = L.out("dbg_xh1", [128, NC * TOK], BF16)
        dbg["h"] = L.out("dbg_h", [128, 32 * TOK], BF16)
        dbg["xh2"] = L.out("dbg_xh2", [128, NC * TOK], BF16)
    wfree = [None, None]
    ev = ln_phase(L, t, xT.ap(), xh, banks, xstage, sqstage)
    if debug:
        ev = S.dma("sp", "dbg", lambda s: s.dma_start(out=dbg["xh1"][:, :], in_=xh[:, :]),
                   waits=[ev])
    ev = ffn_phase(L, t, xh, h, yp, wbuf, banks, xstage,
                   wgT, wuT, wdT, quant=False, sq_sb=None, sd05_ap=lambda dc: 0.5,
                   xold=xT.ap(), resid_out=resid1.ap(), xh_ready=ev, wfree=wfree)
    if debug:
        ev = S.dma("sp", "dbg", lambda s: s.dma_start(out=dbg["h"][:, :], in_=h[:, :]),
                   waits=[ev])
    ev = ln_phase(L, t, resid1.ap(), xh, banks, xstage, sqstage, pre=[ev])
    if debug:
        ev = S.dma("sp", "dbg", lambda s: s.dma_start(out=dbg["xh2"][:, :], in_=xh[:, :]),
                   waits=[ev])
    pq = h[:, 0:2 * TOK]
    pk = h[:, 2 * TOK:4 * TOK]
    pv = h[:, 4 * TOK:6 * TOK]
    e1 = proj_phase(L, t, xh, pq, wbuf, banks, Wq, bqkv_sb[:, 0:NC], qT.ap(), ev, extra=[ev_b], wfree=wfree, oscale=SM)
    e2 = proj_phase(L, t, xh, pk, wbuf, banks, Wk, bqkv_sb[:, NC:2 * NC], kT.ap(), e1, extra=[ev_b], wfree=wfree)
    e3 = proj_phase(L, t, xh, pv, wbuf, banks, Wv, bqkv_sb[:, 2 * NC:3 * NC], vT.ap(), e2, extra=[ev_b], wfree=wfree)
    S.dma("sp", "fin", lambda s: s.nop(), waits=[e3, ev_b, S.ch_now("po0"), S.ch_now("po1")])
    return L




def attn_phase(L, t, S_ps, qsrc, kh, vh, vtm, P, PT, iota_sb, thr_sb, ident_sb,
               kfull, vfull, outT, scr, pre=(), qmine_in=None, ar1_q=None,
               b01=None, extra_qk=(), dbg=None):
    S = L.S
    nm, den, outstage = scr["nm"], scr["den"], scr["outstage"]
    qm = scr.get("qm")
    kh_free = list(pre)
    vh_free = list(pre)
    qm_free = []
    exp_prev = {}
    store_prev = {}
    bank_ev = [None] * 8
    bank_bf = [S_ps[:, b * 512:(b + 1) * 512].bitcast(BF16) for b in range(8)]
    last_store = None
    out_evac_ev = None

    for h in range(16):
        ev_k = S.dma("sp", "kh", lambda s, h=h: s.dma_start(
            out=kh[:, :], in_=kfull[h * 128:(h + 1) * 128, :]),
            waits=kh_free)
        ev_v = S.dma("sp", "vh", lambda s, h=h: s.dma_start(
            out=vh[:, :], in_=vfull[h * 128:(h + 1) * 128, :]),
            waits=vh_free)
        if qmine_in is not None:
            ev_q = S.dma("sp", "qm", lambda s, h=h: s.dma_start(
                out=qm[:, :], in_=qmine_in[h, :, :]), waits=qm_free)
            qv = qm
            qoff = 0
        else:
            ev_q = None
            qv = ar1_q
            qoff = h * 1024
        tr_ev = None
        vtm_evs = []
        for grp in range(4):
            bb = 6 + (grp % 2)
            for j in range(8):
                sc = grp * 8 + j
                tr_ev = S.emit("pe", lambda p, bb=bb, j=j, sc=sc: p.transpose(
                    bank_bf[bb][:, j * 128:(j + 1) * 128], vh[:, sc * 128:(sc + 1) * 128],
                    ident_sb[:, :]),
                    waits=[ev_v, bank_ev[bb], t["ident_ev"]], inc=(j == 7))
            ev = S.emit("dve", lambda v, grp=grp, bb=bb: v.tensor_copy(
                vtm[:, grp * 1024:(grp + 1) * 1024], bank_bf[bb][:, :]),
                waits=[tr_ev], inc=True)
            bank_ev[bb] = ev
            vtm_evs.append(ev)
        vh_free = [tr_ev]
        exp_evs = []
        norm_evs = {}
        for qtg in range(2):
            for qt4 in range(4):
                qtidx = qtg * 4 + qt4
                mm = None
                for kt in range(8):
                    mm = S.emit("pe", lambda p, kt=kt, qtidx=qtidx, qv=qv, qoff=qoff: p.matmul(
                        S_ps[:, kt * 512:(kt + 1) * 512],
                        qv[:, qoff + qtidx * 128: qoff + (qtidx + 1) * 128],
                        kh[:, kt * 512:(kt + 1) * 512], start=True, stop=True),
                        waits=[ev_k, ev_q, bank_ev[kt]] + vtm_evs + list(extra_qk),
                        inc=(kt == 7))
                ev = S.emit("dve", lambda v, qtidx=qtidx: v.tensor_reduce(
                    scr["nm8"].ap().unsqueeze(2),
                    S_ps.ap().rearrange("p (b n) -> p b n", n=512), AX.X, ALU.max),
                    waits=[mm, exp_prev.get(qtidx)], inc=False)
                ev = S.emit("dve", lambda v: v.reduce_max(
                    scr["nm1"][:, :], scr["nm8"][:, :], AX.X), inc=False)
                ev = S.emit("dve", lambda v, qtidx=qtidx: v.tensor_scalar_mul(
                    nm[:, qtidx:qtidx + 1], scr["nm1"][:, :], -1.0), inc=True)
                pdst = P[:, qt4 * 4096:(qt4 + 1) * 4096]
                eexp = S.emit("act", lambda a, pdst=pdst, qtidx=qtidx: a.activation(
                    pdst, S_ps[:, :], AF.Exp, bias=nm[:, qtidx:qtidx + 1]),
                    waits=[ev, out_evac_ev], inc=True)
                exp_evs.append(eexp)
                exp_prev[qtidx] = eexp
                for b in range(8):
                    bank_ev[b] = eexp
                if b01 is not None:
                    ev = S.emit("dve", lambda v, pdst=pdst, qtidx=qtidx, h=h: v.tensor_tensor(
                        pdst.rearrange("p (a b) -> p a b", b=64), pdst.rearrange("p (a b) -> p a b", b=64),
                        b01[:, (qtidx * 16 + h) * 64:(qtidx * 16 + h + 1) * 64]
                        .unsqueeze(2).broadcast_to([128, 64, 64]), ALU.mult),
                        waits=[eexp], inc=True)
                ev = S.emit("dve", lambda v, pdst=pdst, qtidx=qtidx: v.scalar_tensor_tensor(
                    pdst, iota_sb[:, :], thr_sb[:, qtidx:qtidx + 1], pdst,
                    ALU.is_le, ALU.mult, accum_out=den[:, qtidx:qtidx + 1]),
                    waits=[eexp, ev if b01 is not None else None,
                           t["iota_ev"], t.get("thr_ev")], inc=True)
                ev = S.emit("dve", lambda v, qtidx=qtidx: v.reciprocal(
                    den[:, qtidx:qtidx + 1], den[:, qtidx:qtidx + 1]), inc=True)
                ev = S.emit("act", lambda a, pdst=pdst, qtidx=qtidx: a.activation(
                    pdst, pdst, AF.Copy, scale=den[:, qtidx:qtidx + 1]),
                    waits=[ev], inc=True)
                norm_evs[qt4] = ev
            pt_evs = {}
            for rnd in range(4):
                for j in range(4):
                    bb = j
                    tr = None
                    for half in range(2):
                        sc = rnd * 8 + j * 2 + half
                        for qt4 in range(4):
                            tr = S.emit("pe", lambda p, bb=bb, half=half, qt4=qt4, sc=sc: p.transpose(
                                bank_bf[bb][:, half * 512 + qt4 * 128: half * 512 + (qt4 + 1) * 128],
                                P[:, qt4 * 4096 + sc * 128: qt4 * 4096 + (sc + 1) * 128],
                                ident_sb[:, :]),
                                waits=[norm_evs[qt4], bank_ev[bb]],
                                inc=(qt4 == 3 and half == 1))
                    sc0 = rnd * 8 + j * 2
                    ev = S.emit("dve", lambda v, sc0=sc0, bb=bb: v.tensor_copy(
                        PT[:, sc0 * 512:(sc0 + 2) * 512], bank_bf[bb][:, :]),
                        waits=[tr], inc=True)
                    bank_ev[bb] = ev
                    pt_evs[sc0] = ev
                    pt_evs[sc0 + 1] = ev
            pv = None
            for sc in range(32):
                pv = S.emit("pe", lambda p, sc=sc: p.matmul(
                    S_ps[:, 4 * 512:5 * 512], vtm[:, sc * 128:(sc + 1) * 128],
                    PT[:, sc * 512:(sc + 1) * 512], start=(sc == 0), stop=(sc == 31)),
                    waits=[pt_evs[sc], bank_ev[4]], inc=(sc == 31))
            out_evac_ev = S.emit("dve", lambda v, qtg=qtg: v.tensor_copy(
                outstage[:, qtg * 512:(qtg + 1) * 512], S_ps[:, 4 * 512:5 * 512]),
                waits=[pv, store_prev.get(qtg)], inc=True)
            bank_ev[4] = out_evac_ev
            last_store = S.dma("sp", f"so{qtg}", lambda s, h=h, qtg=qtg: s.dma_start(
                out=outT[h * 128:(h + 1) * 128, qtg * 512:(qtg + 1) * 512],
                in_=outstage[:, qtg * 512:(qtg + 1) * 512]),
                waits=[out_evac_ev])
            store_prev[qtg] = last_store
            if dbg is not None and h == 0 and qtg == 0:
                dde = S.dma("sp", "dbg", lambda s: s.dma_start(out=dbg["P"][:, :], in_=P[:, :]),
                            waits=[norm_evs[3], out_evac_ev])
                dde = S.dma("sp", "dbg", lambda s: s.dma_start(out=dbg["PT"][:, :], in_=PT[:, :]),
                            waits=[out_evac_ev])
                dde = S.dma("sp", "dbg", lambda s: s.dma_start(out=dbg["nm"][:, :], in_=nm[:, :]))
                dde = S.dma("sp", "dbg", lambda s: s.dma_start(out=dbg["den"][:, :], in_=den[:, :]))
                dde = S.dma("sp", "dbg", lambda s: s.dma_start(out=dbg["vtm"][:, :], in_=vtm[:, :]))
                exp_prev[0] = dde
        kh_free = [mm]
        qm_free = [mm]
    return [S.ch_now("so0"), S.ch_now("so1")]


def wo_phase(L, t, oT_dram, Wo, bo_sb, resid_in, resid_out, wbuf, banks, ar1,
             rstage, wfree, pre=(), ar1_free=()):
    S = L.S
    ev_o = S.dma("sp", "ro", lambda s: s.dma_start(
        out=ar1.ap().rearrange("p (c n) -> p c n", n=TOK),
        in_=oT_dram.rearrange("(c p) n -> p c n", p=128)),
        waits=list(pre) + list(ar1_free))
    bank_free = {}
    rslot_free = [[], []]
    last = None
    for oc in range(NC):
        sl = oc % 2
        ev_w = S.dma("pool", f"wb{sl}", lambda g, oc=oc, sl=sl: g.dma_start(
            out=wbuf[sl].ap().rearrange("p (c m) -> p c m", m=128)[:, 0:16, :],
            in_=Wo[:, oc, :, :]),
            waits=[wfree[sl]])
        mm_last = [None, None]
        for c in range(NC):
            for tg in range(NTG):
                pb = banks[sl * 2 + tg]
                ev = S.emit("pe", lambda p, c=c, tg=tg, sl=sl, pb=pb: p.matmul(
                    pb[:, :], wbuf[sl][:, c * 128:(c + 1) * 128],
                    ar1[:, c * TOK + tg * TG: c * TOK + (tg + 1) * TG],
                    start=(c == 0), stop=(c == NC - 1)),
                    waits=[ev_w, ev_o, bank_free.get(sl * 2 + tg)],
                    inc=(c == NC - 1))
                mm_last[tg] = ev
        wfree[sl] = mm_last[1]
        for tg in range(NTG):
            pb = banks[sl * 2 + tg]
            rsl = (oc * NTG + tg) % 2
            rs = rstage[:, rsl * TG:(rsl + 1) * TG]
            ev_x = S.dma("sp", f"rs{rsl}", lambda s, oc=oc, tg=tg, rs=rs: s.dma_start(
                out=rs, in_=resid_in[oc * 128:(oc + 1) * 128, tg * TG:(tg + 1) * TG]),
                waits=rslot_free[rsl] + list(pre))
            if bo_sb is not None:
                ev = S.emit("dve", lambda v, pb=pb, rs=rs, oc=oc: v.scalar_tensor_tensor(
                    rs, pb[:, :], bo_sb[:, oc:oc + 1], rs, ALU.add, ALU.add),
                    waits=[mm_last[tg], ev_x], inc=True)
            else:
                ev = S.emit("dve", lambda v, pb=pb, rs=rs: v.tensor_add(rs, pb[:, :], rs),
                            waits=[mm_last[tg], ev_x], inc=True)
            bank_free[sl * 2 + tg] = ev
            last = S.dma("sp", f"wr{rsl}", lambda s, oc=oc, tg=tg, rs=rs: s.dma_start(
                out=resid_out[oc * 128:(oc + 1) * 128, tg * TG:(tg + 1) * TG], in_=rs),
                waits=[ev])
            rslot_free[rsl] = [last]
    return [S.ch_now("wr0"), S.ch_now("wr1")]


def build_launch_B1(debug=False):
    L = LaunchBuilder("B1")
    resid1 = L.inp("resid1", [D, TOK], F32)
    qmine = L.inp("qmine", [16, 128, TOK], BF16)
    kfull = L.inp("kfull", [D, T], BF16)
    vfull = L.inp("vfull", [D, T], BF16)
    iota = L.inp("iota", [128, T], F32)
    thr = L.inp("thr", [128, 8], F32)
    ident = L.inp("ident", [128, 128], BF16)
    Wo = L.inp("Wo", [128, 16, 16, 128], F32)
    bo = L.inp("bo", [128, NC], F32)
    resid2 = L.out("resid2", [D, TOK], F32)
    outT = (L.out if debug else L.scratch)("outT", [D, TOK], BF16)

    L.begin()
    S = L.S
    t = alloc_common(L)
    S_ps = L.es.enter_context(L.nc.psum_tensor("S_ps", [128, 4096], F32))
    banks = [S_ps[:, b * 512:(b + 1) * 512] for b in range(8)]
    AR1 = L.sbuf("AR1", [128, NC * TOK], BF16)
    AR2 = L.sbuf("AR2", [128, 2 * NC * TOK], BF16)
    AR3 = L.sbuf("AR3", [128, 2 * NC * TOK], BF16)
    wbuf = [L.sbuf("wb0", [128, 4096], BF16), L.sbuf("wb1", [128, 4096], BF16)]
    xstage = L.sbuf("xstage", [128, 2 * TG], F32)
    kh = AR3[:, 0:4096]
    vh = AR3[:, 4096:8192]
    vtm = AR3[:, 8192:12288]
    qm = AR3[:, 12288:13312]
    outstage = AR3[:, 13312:14336]
    iota_sb = AR3[:, 16384:24576].bitcast(F32)
    thr_sb = L.sbuf("thr_sb", [128, 8], F32)
    ident_sb = AR3[:, 14336:14464]
    bo_sb = L.sbuf("bo_sb", [128, NC], F32)
    nm = L.sbuf("nm", [128, 16], F32)
    nm8 = L.sbuf("nm8", [128, 8], F32)
    nm1 = L.sbuf("nm1", [128, 1], F32)
    den = L.sbuf("den", [128, 16], F32)
    P = AR2[:, 0:NC * TOK]
    PT = AR2[:, NC * TOK:2 * NC * TOK]

    S.dma("sp", "cst", lambda s: s.dma_start(out=iota_sb, in_=iota[:, :]))
    S.dma("sp", "cst", lambda s: s.dma_start(out=thr_sb[:, :], in_=thr[:, :]))
    ev3 = S.dma("sp", "cst", lambda s: s.dma_start(out=ident_sb[:, :], in_=ident[:, :]))
    t["iota_ev"] = ev3
    t["thr_ev"] = ev3
    t["ident_ev"] = ev3
    ev2 = ev3
    ev4 = S.dma("pool", "cb", lambda g: g.dma_start(out=bo_sb[:, :], in_=bo[:, :]))

    scr = dict(nm=nm, den=den, outstage=outstage, qm=qm, nm8=nm8, nm1=nm1)
    dbg = None
    if debug:
        dbg = {"P": L.out("dbg_P", [128, NC * TOK], BF16),
               "PT": L.out("dbg_PT", [128, NC * TOK], BF16),
               "nm": L.out("dbg_nm", [128, 16], F32),
               "den": L.out("dbg_den", [128, 16], F32),
               "vtm": L.out("dbg_vtm", [128, 4096], BF16)}
    ev = attn_phase(L, t, S_ps, None, kh, vh, vtm, P, PT, iota_sb, thr_sb, ident_sb,
                    kfull, vfull, outT.ap(), scr, pre=[ev2], qmine_in=qmine.ap(), dbg=dbg)
    wfree = [None, None]
    ev = wo_phase(L, t, outT.ap(), Wo, bo_sb, resid1.ap(), resid2.ap(), wbuf, banks,
                  AR1, xstage, wfree, pre=ev + [ev4])
    S.dma("sp", "fin", lambda s: s.nop(), waits=ev)
    return L




def build_launch_B2():
    L = LaunchBuilder("B2")
    resid2 = L.inp("resid2", [D, TOK], F32)
    wgT0 = L.inp("wgT0", [128, 64, 16, 128], F32)
    wuT0 = L.inp("wuT0", [128, 64, 16, 128], F32)
    wdT0 = L.inp("wdT0", [128, 16, 64, 128], F32)
    wgT1 = L.inp("wgT1", [128, 64, 16, 128], I32)
    wuT1 = L.inp("wuT1", [128, 64, 16, 128], I32)
    wdT1 = L.inp("wdT1", [128, 16, 64, 128], I32)
    sq1 = L.inp("sq1", [128, 128], F32)
    sd1 = L.inp("sd1", [128, NC], F32)
    Wq1 = L.inp("Wq1", [128, 16, 16, 128], F32)
    Wk1 = L.inp("Wk1", [128, 16, 16, 128], F32)
    Wv1 = L.inp("Wv1", [128, 16, 16, 128], F32)
    resid4 = L.out("resid4", [D, TOK], F32)
    qT1 = L.out("qT1", [D, TOK], BF16)
    kT1 = L.out("kT1", [D, TOK], BF16)
    vT1 = L.out("vT1", [D, TOK], BF16)
    resid3 = L.scratch("resid3", [D, TOK], F32)

    L.begin()
    S = L.S
    t = alloc_common(L)
    S_ps = L.es.enter_context(L.nc.psum_tensor("S_ps", [128, 4096], F32))
    banks = [S_ps[:, b * 512:(b + 1) * 512] for b in range(8)]
    xh = L.sbuf("xh", [128, NC * TOK], BF16)
    h = L.sbuf("h", [128, 32 * TOK], BF16)
    yp = L.sbuf("yp", [128, NC * TOK], F32)
    wbuf = [L.sbuf("wb0", [128, 4096], BF16), L.sbuf("wb1", [128, 4096], BF16)]
    xstage = L.sbuf("xstage", [128, 2 * TG], F32)
    sqstage = yp[:, 0:2 * TG]
    sq_sb = L.sbuf("sq_sb", [128, 128], F32)
    sd05_sb = L.sbuf("sd05_sb", [128, NC], F32)
    ev_s1 = S.dma("pool", "cb", lambda g: g.dma_start(out=sq_sb[:, :], in_=sq1[:, :]))
    ev_s2 = S.dma("pool", "cb", lambda g: g.dma_start(out=sd05_sb[:, :], in_=sd1[:, :]))
    ev_sd = S.emit("dve", lambda v: v.tensor_scalar_mul(sd05_sb[:, :], sd05_sb[:, :], 0.5),
                   waits=[S.ch_now("cb")], inc=True)

    wfree = [None, None]
    ev = ln_phase(L, t, resid2.ap(), xh, banks, xstage, sqstage)
    ev = ffn_phase(L, t, xh, h, yp, wbuf, banks, xstage,
                   wgT0, wuT0, wdT0, quant=False, sq_sb=None, sd05_ap=lambda dc: 0.5,
                   xold=resid2.ap(), resid_out=resid3.ap(), xh_ready=ev, wfree=wfree)
    ev = ln_phase(L, t, resid3.ap(), xh, banks, xstage, sqstage, pre=[ev], sx=True)
    ev = ffn_phase(L, t, xh, h, yp, wbuf, banks, xstage,
                   wgT1, wuT1, wdT1, quant=True, sq_sb=sq_sb,
                   sd05_ap=lambda dc: sd05_sb[:, dc:dc + 1],
                   xold=resid3.ap(), resid_out=resid4.ap(), xh_ready=ev, wfree=wfree)
    ev = ln_phase(L, t, resid4.ap(), xh, banks, xstage, sqstage, pre=[ev])
    pq = h[:, 0:2 * TOK]
    pk = h[:, 2 * TOK:4 * TOK]
    pv = h[:, 4 * TOK:6 * TOK]
    e1 = proj_phase(L, t, xh, pq, wbuf, banks, Wq1, None, qT1.ap(), ev, wfree=wfree)
    e2 = proj_phase(L, t, xh, pk, wbuf, banks, Wk1, None, kT1.ap(), e1, wfree=wfree)
    e3 = proj_phase(L, t, xh, pv, wbuf, banks, Wv1, None, vT1.ap(), e2, wfree=wfree)
    S.dma("sp", "fin", lambda s: s.nop(),
          waits=[e3, S.ch_now("po0"), S.ch_now("po1"), S.ch_now("sr")])
    return L


def build_launch_C1(debug=False):
    L = LaunchBuilder("C1")
    resid4 = L.inp("resid4", [D, TOK], F32)
    q1 = L.inp("q1", [D, TOK], BF16)
    kfull = L.inp("kfull", [D, T], BF16)
    vfull = L.inp("vfull", [D, T], BF16)
    iota = L.inp("iota", [128, T], F32)
    thr = L.inp("thr", [128, 8], F32)
    ident = L.inp("ident", [128, 128], BF16)
    gmask = L.inp("gmask", [8, 128, NB], F32)
    Wo = L.inp("Wo", [128, 16, 16, 128], F32)
    resid5 = L.out("resid5", [D, TOK], F32)
    outT = L.scratch("outT", [D, TOK], BF16)

    L.begin()
    S = L.S
    t = alloc_common(L)
    S_ps = L.es.enter_context(L.nc.psum_tensor("S_ps", [128, 4096], F32))
    banks = [S_ps[:, b * 512:(b + 1) * 512] for b in range(8)]
    AR1 = L.sbuf("AR1", [128, NC * TOK], BF16)
    AR2 = L.sbuf("AR2", [128, 2 * NC * TOK], BF16)
    AR3 = L.sbuf("AR3", [128, 2 * NC * TOK], BF16)
    wbuf = [L.sbuf("wb0", [128, 4096], BF16), L.sbuf("wb1", [128, 4096], BF16)]
    xstage = L.sbuf("xstage", [128, 2 * TG], F32)
    kh = AR3[:, 0:4096]
    vh = AR3[:, 4096:8192]
    vtm = AR3[:, 8192:12288]
    outstage = AR3[:, 13312:14336]
    iota_sb = AR3[:, 16384:24576].bitcast(F32)
    b01 = AR3[:, 24576:32768]
    gate_all = AR2[:, 0:NC * TOK].bitcast(F32)
    gwork = AR2[:, NC * TOK:2 * NC * TOK].bitcast(F32)
    thr_sb = L.sbuf("thr_sb", [128, 8], F32)
    ident_sb = L.sbuf("ident_sb", [128, 128], BF16)
    gm_sb = L.sbuf("gm_sb", [128, 8 * NB], F32)
    kg_sb = L.sbuf("kg_sb", [128, 16 * NB], BF16)
    eqm = AR3[:, 14336:16384].bitcast(F32)
    tau = L.sbuf("tau", [128, 16], F32)
    nm = L.sbuf("nm", [128, 16], F32)
    nm8 = L.sbuf("nm8", [128, 8], F32)
    nm1 = L.sbuf("nm1", [128, 1], F32)
    den = L.sbuf("den", [128, 16], F32)

    S.dma("sp", "cst", lambda s: s.dma_start(out=iota_sb, in_=iota[:, :]))
    S.dma("sp", "cst", lambda s: s.dma_start(out=thr_sb[:, :], in_=thr[:, :]))
    S.dma("sp", "cst", lambda s: s.dma_start(
        out=gm_sb.rearrange("p (q n) -> p q n", n=NB),
        in_=gmask.ap().rearrange("q p n -> p q n")))
    ev3 = S.dma("sp", "cst", lambda s: s.dma_start(out=ident_sb[:, :], in_=ident[:, :]))
    t["iota_ev"] = ev3
    t["thr_ev"] = ev3
    t["ident_ev"] = ev3
    ev_q1 = S.dma("sp", "q1", lambda s: s.dma_start(
        out=AR1.ap().rearrange("p (c n) -> p c n", n=TOK),
        in_=q1.ap().rearrange("(c p) n -> p c n", p=128)))

    kh_free = []
    ge_evs = []
    for hh in range(16):
        ev_k = S.dma("sp", "kh", lambda s, hh=hh: s.dma_start(
            out=kh[:, :], in_=kfull[hh * 128:(hh + 1) * 128, :]), waits=kh_free)
        ev_kg = S.emit("dve", lambda v, hh=hh: v.tensor_reduce(
            eqm.rearrange("p (a n) -> p a n", n=1)[:, 0:NB, :],
            kh.rearrange("p (a n) -> p a n", n=BLK).transpose([0, 2, 1])
            if False else kh.rearrange("p (a n) -> p a n", n=BLK),
            AX.X, ALU.add), waits=[ev_k], inc=True)
        ev_kg = S.emit("dve", lambda v, hh=hh: v.tensor_scalar_mul(
            kg_sb[:, hh * NB:(hh + 1) * NB], eqm[:, 0:NB],
            1.0 / BLK), inc=True)
        mm = None
        for qt in range(8):
            mm = S.emit("pe", lambda p, hh=hh, qt=qt: p.matmul(
                S_ps[:, 5 * 512 + qt * NB: 5 * 512 + (qt + 1) * NB],
                AR1[:, hh * TOK + qt * 128: hh * TOK + (qt + 1) * 128],
                kg_sb[:, hh * NB:(hh + 1) * NB], start=True, stop=True),
                waits=[ev_kg, ev_q1, ge_evs[-1] if ge_evs else None],
                inc=(qt == 7))
        kh_free = [ev_kg]
        ev_g = S.emit("dve", lambda v, hh=hh: v.tensor_tensor(
            gate_all.rearrange("p (q a n) -> p (q a) n", a=16, n=NB)
            .rearrange("p qa n -> p qa n")[:, hh::16, :],
            S_ps[:, 5 * 512:6 * 512].rearrange("p (q n) -> p q n", n=NB),
            gm_sb.rearrange("p (q n) -> p q n", n=NB), ALU.add),
            waits=[mm, S.ch_now("cst")], inc=True)
        ge_evs.append(ev_g)

    b01_evs = []
    for qt in range(8):
        W3 = gwork.rearrange("p (q x) -> p q x", x=TOK)[:, qt, :].rearrange(
            "p (a n) -> p a n", n=NB)
        G3 = gate_all.rearrange("p (q x) -> p q x", x=TOK)[:, qt, :].rearrange(
            "p (a n) -> p a n", n=NB)
        ev = S.emit("dve", lambda v, W3=W3, G3=G3: v.tensor_copy(W3, G3),
                    waits=ge_evs, inc=True)
        for it in range(7):
            ev = S.emit("dve", lambda v, W3=W3: v.tensor_reduce(
                tau.ap().rearrange("p (a n) -> p a n", n=1), W3, AX.X, ALU.max),
                inc=True)
            ev = S.emit("dve", lambda v, W3=W3: v.tensor_tensor(
                eqm.rearrange("p (a n) -> p a n", n=NB), W3,
                tau.ap().unsqueeze(2).broadcast_to([128, 16, NB]), ALU.is_equal),
                inc=True)
            ev = S.emit("dve", lambda v, W3=W3: v.scalar_tensor_tensor(
                W3, eqm.rearrange("p (a n) -> p a n", n=NB), -2.0e30, W3,
                ALU.mult, ALU.add), inc=True)
        ev = S.emit("dve", lambda v, W3=W3: v.tensor_reduce(
            tau.ap().rearrange("p (a n) -> p a n", n=1), W3, AX.X, ALU.max),
            inc=True)
        ev = S.emit("dve", lambda v, G3=G3, qt=qt: v.tensor_tensor(
            b01.rearrange("p (q a n) -> p q a n", a=16, n=NB)[:, qt, :, :], G3,
            tau.ap().unsqueeze(2).broadcast_to([128, 16, NB]), ALU.is_ge),
            inc=True)
        b01_evs.append(ev)

    scr = dict(nm=nm, den=den, outstage=outstage, qm=None, nm8=nm8, nm1=nm1)
    dbg = None
    if debug:
        dbg = {"P": L.out("dbg_P", [128, NC * TOK], BF16),
               "PT": L.out("dbg_PT", [128, NC * TOK], BF16),
               "nm": L.out("dbg_nm", [128, 16], F32),
               "den": L.out("dbg_den", [128, 16], F32),
               "vtm": L.out("dbg_vtm", [128, 4096], BF16)}
        dbg["gate"] = L.out("dbg_gate", [128, 8192], F32)
        dbg["b01"] = L.out("dbg_b01", [128, 8192], BF16)
        S.dma("sp", "dbg", lambda s: s.dma_start(out=dbg["gate"][:, :], in_=gate_all),
              waits=b01_evs)
        S.dma("sp", "dbg", lambda s: s.dma_start(out=dbg["b01"][:, :], in_=b01),
              waits=b01_evs)
    ev = attn_phase(L, t, S_ps, None, kh, vh, vtm, AR2[:, 0:NC * TOK],
                    AR2[:, NC * TOK:2 * NC * TOK], iota_sb, thr_sb, ident_sb,
                    kfull, vfull, outT.ap(), scr, pre=b01_evs + [S.ch_now("dbg")] if debug else b01_evs,
                    ar1_q=AR1, b01=b01, dbg=dbg)
    wfree = [None, None]
    ev = wo_phase(L, t, outT.ap(), Wo, None, resid4.ap(), resid5.ap(), wbuf, banks,
                  AR1, xstage, wfree, pre=ev, ar1_free=[S.now("pe")])
    S.dma("sp", "fin", lambda s: s.nop(), waits=ev)
    return L


def build_launch_C2():
    L = LaunchBuilder("C2")
    resid5 = L.inp("resid5", [D, TOK], F32)
    wgT1 = L.inp("wgT1", [128, 64, 16, 128], I32)
    wuT1 = L.inp("wuT1", [128, 64, 16, 128], I32)
    wdT1 = L.inp("wdT1", [128, 16, 64, 128], I32)
    sq1 = L.inp("sq1", [128, 128], F32)
    sd1 = L.inp("sd1", [128, NC], F32)
    yT = L.out("yT", [D, TOK], F32)

    L.begin()
    S = L.S
    t = alloc_common(L)
    S_ps = L.es.enter_context(L.nc.psum_tensor("S_ps", [128, 4096], F32))
    banks = [S_ps[:, b * 512:(b + 1) * 512] for b in range(8)]
    xh = L.sbuf("xh", [128, NC * TOK], BF16)
    h = L.sbuf("h", [128, 32 * TOK], BF16)
    yp = L.sbuf("yp", [128, NC * TOK], F32)
    wbuf = [L.sbuf("wb0", [128, 4096], BF16), L.sbuf("wb1", [128, 4096], BF16)]
    xstage = L.sbuf("xstage", [128, 2 * TG], F32)
    sqstage = yp[:, 0:2 * TG]
    sq_sb = L.sbuf("sq_sb", [128, 128], F32)
    sd05_sb = L.sbuf("sd05_sb", [128, NC], F32)
    S.dma("pool", "cb", lambda g: g.dma_start(out=sq_sb[:, :], in_=sq1[:, :]))
    S.dma("pool", "cb", lambda g: g.dma_start(out=sd05_sb[:, :], in_=sd1[:, :]))
    S.emit("dve", lambda v: v.tensor_scalar_mul(sd05_sb[:, :], sd05_sb[:, :], 0.5),
           waits=[S.ch_now("cb")], inc=True)
    wfree = [None, None]
    ev = ln_phase(L, t, resid5.ap(), xh, banks, xstage, sqstage, sx=True)
    ev = ffn_phase(L, t, xh, h, yp, wbuf, banks, xstage,
                   wgT1, wuT1, wdT1, quant=True, sq_sb=sq_sb,
                   sd05_ap=lambda dc: sd05_sb[:, dc:dc + 1],
                   xold=resid5.ap(), resid_out=yT.ap(), xh_ready=ev, wfree=wfree)
    S.dma("sp", "fin", lambda s: s.nop(), waits=[ev, S.ch_now("sr")])
    return L




def build_launch_B():
    L = LaunchBuilder("B")
    resid1 = L.inp("resid1", [D, TOK], F32)
    qmine = L.inp("qmine", [16, 128, TOK], BF16)
    kfull = L.inp("kfull", [D, T], BF16)
    vfull = L.inp("vfull", [D, T], BF16)
    iota = L.inp("iota", [128, T], F32)
    thr = L.inp("thr", [128, 8], F32)
    ident = L.inp("ident", [128, 128], BF16)
    Wo = L.inp("Wo", [128, 16, 16, 128], F32)
    bo = L.inp("bo", [128, NC], F32)
    wgT0 = L.inp("wgT0", [128, 64, 16, 128], F32)
    wuT0 = L.inp("wuT0", [128, 64, 16, 128], F32)
    wdT0 = L.inp("wdT0", [128, 16, 64, 128], F32)
    wgT1 = L.inp("wgT1", [128, 64, 16, 128], I32)
    wuT1 = L.inp("wuT1", [128, 64, 16, 128], I32)
    wdT1 = L.inp("wdT1", [128, 16, 64, 128], I32)
    sq1 = L.inp("sq1", [128, 128], F32)
    sd1 = L.inp("sd1", [128, NC], F32)
    Wq1 = L.inp("Wq1", [128, 16, 16, 128], F32)
    Wk1 = L.inp("Wk1", [128, 16, 16, 128], F32)
    Wv1 = L.inp("Wv1", [128, 16, 16, 128], F32)
    resid4 = L.out("resid4", [D, TOK], F32)
    qT1 = L.out("qT1", [D, TOK], BF16)
    kT1 = L.out("kT1", [D, TOK], BF16)
    vT1 = L.out("vT1", [D, TOK], BF16)
    outT = L.scratch("outT", [D, TOK], BF16)
    resid2 = L.scratch("resid2", [D, TOK], F32)
    resid3 = L.scratch("resid3", [D, TOK], F32)

    L.begin()
    S = L.S
    t = alloc_common(L)
    S_ps = L.es.enter_context(L.nc.psum_tensor("S_ps", [128, 4096], F32))
    banks = [S_ps[:, b * 512:(b + 1) * 512] for b in range(8)]
    AR1 = L.sbuf("AR1", [128, NC * TOK], BF16)
    AR2 = L.sbuf("AR2", [128, 2 * NC * TOK], BF16)
    AR3 = L.sbuf("AR3", [128, 2 * NC * TOK], BF16)
    wbuf = [L.sbuf("wb0", [128, 4096], BF16), L.sbuf("wb1", [128, 4096], BF16)]
    xstage = L.sbuf("xstage", [128, 2 * TG], F32)
    kh = AR3[:, 0:4096]
    vh = AR3[:, 4096:8192]
    vtm = AR3[:, 8192:12288]
    qm = AR3[:, 12288:13312]
    outstage = AR3[:, 13312:14336]
    iota_sb = AR3[:, 16384:24576].bitcast(F32)
    thr_sb = L.sbuf("thr_sb", [128, 8], F32)
    ident_sb = AR3[:, 14336:14464]
    bo_sb = L.sbuf("bo_sb", [128, NC], F32)
    nm = L.sbuf("nm", [128, 16], F32)
    nm8 = L.sbuf("nm8", [128, 8], F32)
    nm1 = L.sbuf("nm1", [128, 1], F32)
    den = L.sbuf("den", [128, 16], F32)
    sq_sb = L.sbuf("sq_sb", [128, 128], F32)
    sd05_sb = L.sbuf("sd05_sb", [128, NC], F32)
    P = AR2[:, 0:NC * TOK]
    PT = AR2[:, NC * TOK:2 * NC * TOK]
    xh = AR1
    hbuf = AR2
    yp = AR3[:, 0:2 * NC * TOK].bitcast(F32)
    sqstage = yp[:, 0:2 * TG]

    S.dma("sp", "cst", lambda s: s.dma_start(out=iota_sb, in_=iota[:, :]))
    S.dma("sp", "cst", lambda s: s.dma_start(out=thr_sb[:, :], in_=thr[:, :]))
    ev3 = S.dma("sp", "cst", lambda s: s.dma_start(out=ident_sb[:, :], in_=ident[:, :]))
    t["iota_ev"] = ev3
    t["thr_ev"] = ev3
    t["ident_ev"] = ev3
    ev4 = S.dma("pool", "cb", lambda g: g.dma_start(out=bo_sb[:, :], in_=bo[:, :]))
    S.dma("pool", "cb", lambda g: g.dma_start(out=sq_sb[:, :], in_=sq1[:, :]))
    ev_s2 = S.dma("pool", "cb", lambda g: g.dma_start(out=sd05_sb[:, :], in_=sd1[:, :]))
    ev_sd = S.emit("dve", lambda v: v.tensor_scalar_mul(sd05_sb[:, :], sd05_sb[:, :], 0.5),
                   waits=[S.ch_now("cb")], inc=True)

    scr = dict(nm=nm, den=den, outstage=outstage, qm=qm, nm8=nm8, nm1=nm1)
    ev = attn_phase(L, t, S_ps, None, kh, vh, vtm, P, PT, iota_sb, thr_sb, ident_sb,
                    kfull, vfull, outT.ap(), scr, pre=[ev3], qmine_in=qmine.ap())
    wfree = [None, None]
    ev = wo_phase(L, t, outT.ap(), Wo, bo_sb, resid1.ap(), resid2.ap(), wbuf, banks,
                  AR1, xstage, wfree, pre=ev + [ev4])
    ev2 = ln_phase(L, t, resid2.ap(), xh, banks, xstage, sqstage, pre=ev)
    ev2 = ffn_phase(L, t, xh, hbuf, yp, wbuf, banks, xstage,
                    wgT0, wuT0, wdT0, quant=False, sq_sb=None, sd05_ap=lambda dc: 0.5,
                    xold=resid2.ap(), resid_out=resid3.ap(), xh_ready=ev2, wfree=wfree)
    ev2 = ln_phase(L, t, resid3.ap(), xh, banks, xstage, sqstage, pre=[ev2], sx=True)
    ev2 = ffn_phase(L, t, xh, hbuf, yp, wbuf, banks, xstage,
                    wgT1, wuT1, wdT1, quant=True, sq_sb=sq_sb,
                    sd05_ap=lambda dc: sd05_sb[:, dc:dc + 1],
                    xold=resid3.ap(), resid_out=resid4.ap(), xh_ready=ev2, wfree=wfree)
    ev2 = ln_phase(L, t, resid4.ap(), xh, banks, xstage, sqstage, pre=[ev2])
    pq = hbuf[:, 0:2 * TOK]
    pk = hbuf[:, 2 * TOK:4 * TOK]
    pv = hbuf[:, 4 * TOK:6 * TOK]
    e1 = proj_phase(L, t, xh, pq, wbuf, banks, Wq1, None, qT1.ap(), ev2, wfree=wfree, oscale=SM)
    e2 = proj_phase(L, t, xh, pk, wbuf, banks, Wk1, None, kT1.ap(), e1, wfree=wfree)
    e3 = proj_phase(L, t, xh, pv, wbuf, banks, Wv1, None, vT1.ap(), e2, wfree=wfree)
    S.dma("sp", "fin", lambda s: s.nop(),
          waits=[e3, S.ch_now("po0"), S.ch_now("po1"), S.ch_now("sr")])
    return L


def build_launch_C():
    L = LaunchBuilder("C")
    resid4 = L.inp("resid4", [D, TOK], F32)
    q1 = L.inp("q1", [D, TOK], BF16)
    kfull = L.inp("kfull", [D, T], BF16)
    vfull = L.inp("vfull", [D, T], BF16)
    iota = L.inp("iota", [128, T], F32)
    thr = L.inp("thr", [128, 8], F32)
    ident = L.inp("ident", [128, 128], BF16)
    gmask = L.inp("gmask", [8, 128, NB], F32)
    Wo = L.inp("Wo", [128, 16, 16, 128], F32)
    wgT1 = L.inp("wgT1", [128, 64, 16, 128], I32)
    wuT1 = L.inp("wuT1", [128, 64, 16, 128], I32)
    wdT1 = L.inp("wdT1", [128, 16, 64, 128], I32)
    sq1 = L.inp("sq1", [128, 128], F32)
    sd1 = L.inp("sd1", [128, NC], F32)
    yT = L.out("yT", [D, TOK], F32)
    outT = L.scratch("outT", [D, TOK], BF16)
    resid5 = L.scratch("resid5", [D, TOK], F32)

    L.begin()
    S = L.S
    t = alloc_common(L)
    S_ps = L.es.enter_context(L.nc.psum_tensor("S_ps", [128, 4096], F32))
    banks = [S_ps[:, b * 512:(b + 1) * 512] for b in range(8)]
    AR1 = L.sbuf("AR1", [128, NC * TOK], BF16)
    AR2 = L.sbuf("AR2", [128, 2 * NC * TOK], BF16)
    AR3 = L.sbuf("AR3", [128, 2 * NC * TOK], BF16)
    wbuf = [L.sbuf("wb0", [128, 4096], BF16), L.sbuf("wb1", [128, 4096], BF16)]
    xstage = L.sbuf("xstage", [128, 2 * TG], F32)
    kh = AR3[:, 0:4096]
    vh = AR3[:, 4096:8192]
    vtm = AR3[:, 8192:12288]
    outstage = AR3[:, 13312:14336]
    iota_sb = AR3[:, 16384:24576].bitcast(F32)
    b01 = AR3[:, 24576:32768]
    gate_all = AR2[:, 0:NC * TOK].bitcast(F32)
    gwork = AR2[:, NC * TOK:2 * NC * TOK].bitcast(F32)
    eqm = AR3[:, 14336:16384].bitcast(F32)
    thr_sb = L.sbuf("thr_sb", [128, 8], F32)
    ident_sb = AR3[:, 12288:12416]
    gm_sb = AR3[:, 4096:5120].bitcast(F32)
    kg_sb = AR3[:, 6144:7168]
    tau = L.sbuf("tau", [128, 16], F32)
    nm = L.sbuf("nm", [128, 16], F32)
    nm8 = L.sbuf("nm8", [128, 8], F32)
    nm1 = L.sbuf("nm1", [128, 1], F32)
    den = L.sbuf("den", [128, 16], F32)
    sq_sb = L.sbuf("sq_sb", [128, 128], F32)
    sd05_sb = L.sbuf("sd05_sb", [128, NC], F32)
    xh = AR1
    hbuf = AR2
    yp = AR3[:, 0:2 * NC * TOK].bitcast(F32)
    sqstage = yp[:, 0:2 * TG]

    S.dma("sp", "cst", lambda s: s.dma_start(out=iota_sb, in_=iota[:, :]))
    S.dma("sp", "cst", lambda s: s.dma_start(out=thr_sb[:, :], in_=thr[:, :]))
    S.dma("sp", "cst", lambda s: s.dma_start(
        out=gm_sb.rearrange("p (q n) -> p q n", n=NB),
        in_=gmask.ap().rearrange("q p n -> p q n")))
    ev3 = S.dma("sp", "cst", lambda s: s.dma_start(out=ident_sb[:, :], in_=ident[:, :]))
    t["iota_ev"] = ev3
    t["thr_ev"] = ev3
    t["ident_ev"] = ev3
    S.dma("pool", "cb", lambda g: g.dma_start(out=sq_sb[:, :], in_=sq1[:, :]))
    S.dma("pool", "cb", lambda g: g.dma_start(out=sd05_sb[:, :], in_=sd1[:, :]))
    S.emit("dve", lambda v: v.tensor_scalar_mul(sd05_sb[:, :], sd05_sb[:, :], 0.5),
           waits=[S.ch_now("cb")], inc=True)
    ev_q1 = S.dma("sp", "q1", lambda s: s.dma_start(
        out=AR1.ap().rearrange("p (c n) -> p c n", n=TOK),
        in_=q1.ap().rearrange("(c p) n -> p c n", p=128)))

    kh_free = []
    ge_evs = []
    for hh in range(16):
        ev_k = S.dma("sp", "kh", lambda s, hh=hh: s.dma_start(
            out=kh[:, :], in_=kfull[hh * 128:(hh + 1) * 128, :]), waits=kh_free)
        ev_kg = S.emit("dve", lambda v, hh=hh: v.tensor_reduce(
            eqm.rearrange("p (a n) -> p a n", n=1)[:, 0:NB, :],
            kh.rearrange("p (a n) -> p a n", n=BLK),
            AX.X, ALU.add), waits=[ev_k], inc=True)
        ev_kg = S.emit("dve", lambda v, hh=hh: v.tensor_scalar_mul(
            kg_sb[:, hh * NB:(hh + 1) * NB], eqm[:, 0:NB], (1.0 / BLK) / SM), inc=True)
        mm = None
        for qt in range(8):
            mm = S.emit("pe", lambda p, hh=hh, qt=qt: p.matmul(
                S_ps[:, 5 * 512 + qt * NB: 5 * 512 + (qt + 1) * NB],
                AR1[:, hh * TOK + qt * 128: hh * TOK + (qt + 1) * 128],
                kg_sb[:, hh * NB:(hh + 1) * NB], start=True, stop=True),
                waits=[ev_kg, ev_q1, ge_evs[-1] if ge_evs else None],
                inc=(qt == 7))
        kh_free = [ev_kg]
        ev_g = S.emit("dve", lambda v, hh=hh: v.tensor_tensor(
            gate_all.rearrange("p (q a n) -> p (q a) n", a=16, n=NB)[:, hh::16, :],
            S_ps[:, 5 * 512:6 * 512].rearrange("p (q n) -> p q n", n=NB),
            gm_sb.rearrange("p (q n) -> p q n", n=NB), ALU.add),
            waits=[mm, S.ch_now("cst")], inc=True)
        ge_evs.append(ev_g)
    b01_evs = []
    for qt in range(8):
        W3 = gwork.rearrange("p (q x) -> p q x", x=TOK)[:, qt, :].rearrange(
            "p (a n) -> p a n", n=NB)
        G3 = gate_all.rearrange("p (q x) -> p q x", x=TOK)[:, qt, :].rearrange(
            "p (a n) -> p a n", n=NB)
        ev = S.emit("dve", lambda v, W3=W3, G3=G3: v.tensor_copy(W3, G3),
                    waits=ge_evs, inc=True)
        for it in range(7):
            ev = S.emit("dve", lambda v, W3=W3: v.tensor_reduce(
                tau.ap().rearrange("p (a n) -> p a n", n=1), W3, AX.X, ALU.max),
                inc=True)
            ev = S.emit("dve", lambda v, W3=W3: v.tensor_tensor(
                eqm.rearrange("p (a n) -> p a n", n=NB), W3,
                tau.ap().unsqueeze(2).broadcast_to([128, 16, NB]), ALU.is_equal),
                inc=True)
            ev = S.emit("dve", lambda v, W3=W3: v.scalar_tensor_tensor(
                W3, eqm.rearrange("p (a n) -> p a n", n=NB), -2.0e30, W3,
                ALU.mult, ALU.add), inc=True)
        ev = S.emit("dve", lambda v, W3=W3: v.tensor_reduce(
            tau.ap().rearrange("p (a n) -> p a n", n=1), W3, AX.X, ALU.max),
            inc=True)
        ev = S.emit("dve", lambda v, G3=G3, qt=qt: v.tensor_tensor(
            b01.rearrange("p (q a n) -> p q a n", a=16, n=NB)[:, qt, :, :], G3,
            tau.ap().unsqueeze(2).broadcast_to([128, 16, NB]), ALU.is_ge),
            inc=True)
        b01_evs.append(ev)

    scr = dict(nm=nm, den=den, outstage=outstage, qm=None, nm8=nm8, nm1=nm1)
    ev = attn_phase(L, t, S_ps, None, kh, vh, vtm, AR2[:, 0:NC * TOK],
                    AR2[:, NC * TOK:2 * NC * TOK], iota_sb, thr_sb, ident_sb,
                    kfull, vfull, outT.ap(), scr, pre=b01_evs,
                    ar1_q=AR1, b01=b01)
    wfree = [None, None]
    ev = wo_phase(L, t, outT.ap(), Wo, None, resid4.ap(), resid5.ap(), wbuf, banks,
                  AR1, xstage, wfree, pre=ev, ar1_free=[S.now("pe")])
    ev2 = ln_phase(L, t, resid5.ap(), xh, banks, xstage, sqstage, pre=ev, sx=True)
    ev2 = ffn_phase(L, t, xh, hbuf, yp, wbuf, banks, xstage,
                    wgT1, wuT1, wdT1, quant=True, sq_sb=sq_sb,
                    sd05_ap=lambda dc: sd05_sb[:, dc:dc + 1],
                    xold=resid5.ap(), resid_out=yT.ap(), xh_ready=ev2, wfree=wfree)
    S.dma("sp", "fin", lambda s: s.nop(), waits=[ev2, S.ch_now("sr")])
    return L


_NCORES = 8
_BF = ml_dtypes.bfloat16
_PROG_CACHE = {}


def _prog(name, builder):
    if name not in _PROG_CACHE:
        L = builder()
        _PROG_CACHE[name] = L.finish()
    return _PROG_CACHE[name]


def _build_qmine(qflatT_b, qr):
    A = qflatT_b.reshape(16, 128, 4096)
    out = np.empty((16, 128, 1024), dtype=qflatT_b.dtype)
    for hh in range(16):
        sub = A[:, :, hh * 256 + 64 * qr: hh * 256 + 64 * qr + 64]
        out[hh] = np.transpose(sub, (1, 2, 0)).reshape(128, 1024)
    return out


def kernel(L0_attn_x, L0_attn_Wq, L0_attn_bq, L0_attn_Wk, L0_attn_bk,
           L0_attn_Wv, L0_attn_bv, L0_attn_Wo, L0_attn_bo,
           L0_ffn_w_gate, L0_ffn_w_up, L0_ffn_w_down,
           L1_attn_Wq, L1_attn_Wk, L1_attn_Wv, L1_attn_Wo,
           L1_ffn_gate_w, L1_ffn_gate_scale, L1_ffn_up_w, L1_ffn_up_scale,
           L1_ffn_down_w, L1_ffn_down_scale):
    C = np.ascontiguousarray

    def gu_img(w):
        return C(np.asarray(w).reshape(64, 128, 16, 128).transpose(3, 0, 2, 1))

    def dn_img(w):
        return C(np.asarray(w).reshape(16, 128, 64, 128).transpose(3, 0, 2, 1))

    def pj_img(w):
        return C(np.asarray(w, np.float32).reshape(16, 128, 16, 128).transpose(1, 2, 0, 3))

    x = np.asarray(L0_attn_x, np.float32)
    wgT0 = gu_img(np.asarray(L0_ffn_w_gate, np.float32))
    wuT0 = gu_img(np.asarray(L0_ffn_w_up, np.float32))
    wdT0 = dn_img(np.asarray(L0_ffn_w_down, np.float32))
    b_qkv = C(np.concatenate([np.asarray(b, np.float32).reshape(16, 128).T
                              for b in (L0_attn_bq, L0_attn_bk, L0_attn_bv)], axis=1))
    bo = C(np.asarray(L0_attn_bo, np.float32).reshape(16, 128).T)
    wgT1 = gu_img(np.asarray(L1_ffn_gate_w, np.int32))
    wuT1 = gu_img(np.asarray(L1_ffn_up_w, np.int32))
    wdT1 = dn_img(np.asarray(L1_ffn_down_w, np.int32))
    sq1 = C(np.concatenate([np.asarray(L1_ffn_gate_scale, np.float32).reshape(64, 128).T,
                            np.asarray(L1_ffn_up_scale, np.float32).reshape(64, 128).T],
                           axis=1))
    sd1 = C(np.asarray(L1_ffn_down_scale, np.float32).reshape(16, 128).T)
    iota = np.broadcast_to(np.arange(T, dtype=np.float32), (128, T)).copy()
    ident = np.eye(128, dtype=_BF)
    thr = [(1024 * qr + np.arange(8)[None, :] * 128 +
            np.arange(128)[:, None]).astype(np.float32) for qr in range(4)]
    gmask = []
    for qr in range(4):
        s_abs = qr * 1024 + np.arange(8)[:, None] * 128 + np.arange(128)[None, :]
        blk = np.arange(NB)[None, None, :]
        self_m = (s_abs[:, :, None] // BLK) == blk
        notpast = (s_abs[:, :, None] < (blk + 1) * BLK) & ~self_m
        gmask.append(C(np.where(self_m, BIGM, np.where(notpast, -BIGM, 0.0))
                     .astype(np.float32)))

    im = []
    for c in range(_NCORES):
        b, qr = c // 4, c % 4
        im.append(dict(xT=C(x[b, qr * TOK:(qr + 1) * TOK].T), wgT=wgT0, wuT=wuT0,
                       wdT=wdT0, Wq=pj_img(L0_attn_Wq), Wk=pj_img(L0_attn_Wk),
                       Wv=pj_img(L0_attn_Wv), b_qkv=b_qkv))
    rA = run_bass_kernel_spmd(_prog("A", build_launch_A), im,
                              core_ids=list(range(_NCORES))).results
    qfT = [np.concatenate([rA[b * 4 + qr]["qT"] for qr in range(4)], axis=1)
           for b in range(2)]
    kfT = [C(np.concatenate([rA[b * 4 + qr]["kT"] for qr in range(4)], axis=1))
           for b in range(2)]
    vfT = [C(np.concatenate([rA[b * 4 + qr]["vT"] for qr in range(4)], axis=1))
           for b in range(2)]

    im = []
    for c in range(_NCORES):
        b, qr = c // 4, c % 4
        im.append(dict(resid1=rA[c]["resid1"], qmine=C(_build_qmine(qfT[b], qr)),
                       kfull=kfT[b], vfull=vfT[b], iota=iota, thr=thr[qr],
                       ident=ident, Wo=pj_img(L0_attn_Wo), bo=bo))
    for c in range(_NCORES):
        im[c].update(dict(wgT0=wgT0, wuT0=wuT0, wdT0=wdT0,
                          wgT1=wgT1, wuT1=wuT1, wdT1=wdT1, sq1=sq1, sd1=sd1,
                          Wq1=pj_img(L1_attn_Wq), Wk1=pj_img(L1_attn_Wk),
                          Wv1=pj_img(L1_attn_Wv)))
    rB2 = run_bass_kernel_spmd(_prog("B", build_launch_B), im,
                               core_ids=list(range(_NCORES))).results
    k1T = [C(np.concatenate([rB2[b * 4 + qr]["kT1"] for qr in range(4)], axis=1))
           for b in range(2)]
    v1T = [C(np.concatenate([rB2[b * 4 + qr]["vT1"] for qr in range(4)], axis=1))
           for b in range(2)]

    im = []
    for c in range(_NCORES):
        b, qr = c // 4, c % 4
        im.append(dict(resid4=rB2[c]["resid4"], q1=rB2[c]["qT1"], kfull=k1T[b],
                       vfull=v1T[b], iota=iota, thr=thr[qr], ident=ident,
                       gmask=gmask[qr], Wo=pj_img(L1_attn_Wo)))
    for c in range(_NCORES):
        im[c].update(dict(wgT1=wgT1, wuT1=wuT1, wdT1=wdT1, sq1=sq1, sd1=sd1))
    rC2 = run_bass_kernel_spmd(_prog("C", build_launch_C), im,
                               core_ids=list(range(_NCORES))).results

    out = np.empty((2, T, D), np.float32)
    for c in range(_NCORES):
        b, qr = c // 4, c % 4
        out[b, qr * TOK:(qr + 1) * TOK] = rC2[c]["yT"].T
    return out


# version 08 (final)
# speedup vs baseline: 1.0567x; 1.0279x over previous
import contextlib
import numpy as np
import ml_dtypes

import concourse.bass as bass
import concourse.mybir as mybir
from concourse.bass_utils import run_bass_kernel_spmd

F32 = mybir.dt.float32
BF16 = mybir.dt.bfloat16
I32 = mybir.dt.int32
AF = mybir.ActivationFunctionType
ALU = mybir.AluOpType
AX = mybir.AxisListType

D = 2048
NC = 16
TOK = 1024
NTG = 2
TG = 512
HID = 8192
T = 4096
H = 16
DH = 128
SM = float(1.0 / np.sqrt(128.0))
EPS = 1e-5
NB = 64
BLK = 64
TOPK = 8
BIGM = 1.0e30


class Sch:

    ENGS = ("pe", "dve", "act", "sp", "pool")
    SERIAL = ("dve", "act")

    def __init__(self, nc):
        self.nc = nc
        self.q = {e: [] for e in self.ENGS}
        self.cnt = {e: 0 for e in self.ENGS}
        self.ch_cnt = {}
        self._lastw = {}

    def emit(self, eng, fn, waits=(), inc=False):
        assert eng in ("pe", "dve", "act")
        waits = list(waits)
        if eng in self.SERIAL:
            waits.append((eng, self.cnt[eng]))
            inc = True
        ws = self._filter(eng, waits)
        if inc:
            self.cnt[eng] += 1
            out = (eng, self.cnt[eng])
        else:
            out = None
        self.q[eng].append((ws, fn, eng if inc else None))
        return out

    def dma(self, queue, channel, fn, waits=()):
        assert queue in ("sp", "pool")
        ws = self._filter(queue, list(waits))
        self.ch_cnt[channel] = self.ch_cnt.get(channel, 0) + 16
        self.q[queue].append((ws, fn, channel))
        return (channel, self.ch_cnt[channel])

    def _filter(self, eng, waits):
        ws = []
        for ev in waits:
            if ev is None:
                continue
            tgt, val = ev
            if tgt == eng and tgt in ("pe",):
                continue
            key = (eng, tgt)
            if self._lastw.get(key, -1) >= val:
                continue
            self._lastw[key] = val
            ws.append((tgt, val))
        return ws

    def now(self, eng):
        return (eng, self.cnt[eng])

    def ch_now(self, channel):
        return (channel, self.ch_cnt.get(channel, 0))

    def build(self, block, mksem):
        engmap = dict(pe="tensor", dve="vector", act="scalar", sp="sync", pool="gpsimd")
        sems = {}

        def sem(name):
            if name not in sems:
                sems[name] = mksem(name)
            return sems[name]

        for eng in self.ENGS:
            for ws, fn, inc_t in self.q[eng]:
                for tgt, val in ws:
                    sem(tgt)
                if inc_t is not None:
                    sem(inc_t)

        def reg(eng):
            items = self.q[eng]

            def body(e):
                for ws, fn, inc_t in items:
                    for tgt, val in ws:
                        e.wait_ge(sems[tgt], val)
                    inst = fn(e)
                    if inc_t is not None:
                        inst.then_inc(sems[inc_t], 16 if eng in ("sp", "pool") else 1)
            body.__name__ = f"body_{eng}"
            getattr(block, engmap[eng])(body)

        for eng in self.ENGS:
            if self.q[eng]:
                reg(eng)


class LaunchBuilder:
    def __init__(self, name):
        self.name = name
        self.nc = bass.Bass("TRN2", target_bir_lowering=False, debug=False)
        self.es = contextlib.ExitStack()
        self.sb_bytes = 0
        self.dram = {}

    def inp(self, name, shape, dt):
        t = self.nc.dram_tensor(name, list(shape), dt, kind="ExternalInput")
        self.dram[name] = t
        return t

    def out(self, name, shape, dt):
        t = self.nc.dram_tensor(name, list(shape), dt, kind="ExternalOutput")
        self.dram[name] = t
        return t

    def scratch(self, name, shape, dt):
        t = self.nc.dram_tensor(name, list(shape), dt)
        self.dram[name] = t
        return t

    def begin(self):
        self.block = self.es.enter_context(self.nc.Block())
        self.S = Sch(self.nc)

    def sbuf(self, name, shape, dt):
        n = 1
        for s in shape[1:]:
            n *= s
        self.sb_bytes += n * {F32: 4, BF16: 2, I32: 4}[dt] * 128
        return self.es.enter_context(self.nc.sbuf_tensor(name, list(shape), dt))

    def psum_banks(self, n=8):
        return [self.es.enter_context(self.nc.psum_tensor(f"bank{i}", [128, 512], F32))
                for i in range(n)]

    def finish(self):
        self.S.build(self.block,
                     lambda name: self.es.enter_context(self.nc.semaphore(f"s_{name}")))
        self.es.close()
        return self.nc


def alloc_common(L):
    t = {}
    t["ones_col_f"] = L.sbuf("ones_col_f", [128, 1], F32)
    t["ones_col_b"] = L.sbuf("ones_col_b", [128, 1], BF16)
    t["ones_row_f"] = L.sbuf("ones_row_f", [1, 128], F32)
    t["negk_row_f"] = L.sbuf("negk_row_f", [1, 128], F32)
    t["negk_row_b"] = L.sbuf("negk_row_b", [1, 128], BF16)
    t["corr"] = L.sbuf("corr", [1, 3 * TOK], BF16)
    t["stats"] = L.sbuf("stats", [1, 4 * TOK], F32)
    t["eps"] = L.sbuf("eps", [1, 1], F32)
    S = L.S
    S.emit("dve", lambda v: v.memset(t["ones_col_f"][:, :], 1.0))
    S.emit("dve", lambda v: v.memset(t["ones_col_b"][:, :], 1.0))
    S.emit("dve", lambda v: v.memset(t["ones_row_f"][:, :], 1.0))
    S.emit("dve", lambda v: v.memset(t["negk_row_f"][:, :], -128.0))
    S.emit("dve", lambda v: v.memset(t["negk_row_b"][:, :], -128.0))
    ev = S.emit("dve", lambda v: v.memset(t["eps"][:, :], EPS), inc=True)
    t["consts_ev"] = ev
    return t


def ln_phase(L, t, xsrc, xh_sb, banks, xstage, sqstage, pre=(), sx=False):
    S = L.S
    st = t["stats"]
    R = lambda r, tg: st[0:1, r * TOK + tg * TG: r * TOK + (tg + 1) * TG]
    xslot_free = [None, None]
    sqslot_free = [None, None]
    pe_ev = None
    for c in range(NC):
        for tg in range(NTG):
            sl = (c * NTG + tg) % 2
            xs = xstage[:, sl * TG:(sl + 1) * TG]
            qs = sqstage[:, sl * TG:(sl + 1) * TG]
            ev_ld = S.dma("sp", f"xs{sl}", lambda s, c=c, tg=tg, xs=xs: s.dma_start(
                out=xs, in_=xsrc[c * 128:(c + 1) * 128, tg * TG:(tg + 1) * TG]),
                waits=(xslot_free[sl] or []) + list(pre))
            ev_sq = S.emit("act", lambda a, xs=xs, qs=qs: a.activation(qs, xs, AF.Square),
                           waits=[ev_ld] + (sqslot_free[sl] or []), inc=True)
            ev_m1 = S.emit("pe", lambda p, c=c, tg=tg, xs=xs: p.matmul(
                banks[tg][0:1, :], t["ones_col_f"][:, :], xs,
                start=(c == 0), stop=(c == NC - 1)),
                waits=[ev_ld, t["consts_ev"]], inc=True)
            pe_ev = S.emit("pe", lambda p, c=c, tg=tg, qs=qs: p.matmul(
                banks[2 + tg][0:1, :], t["ones_col_f"][:, :], qs,
                start=(c == 0), stop=(c == NC - 1)),
                waits=[ev_sq], inc=True)
            xslot_free[sl] = [ev_m1, ev_sq]
            sqslot_free[sl] = [pe_ev]
    for tg in range(NTG):
        S.emit("dve", lambda v, tg=tg: v.tensor_copy(R(0, tg), banks[tg][0:1, :]), waits=[pe_ev])
        S.emit("dve", lambda v, tg=tg: v.tensor_copy(R(1, tg), banks[2 + tg][0:1, :]))
    S.emit("dve", lambda v: v.tensor_scalar_mul(st[0:1, 2 * TOK:3 * TOK], st[0:1, 0:TOK], 1.0 / D))
    S.emit("dve", lambda v: v.tensor_scalar_mul(st[0:1, 1 * TOK:2 * TOK], st[0:1, 1 * TOK:2 * TOK], 1.0 / D))
    S.emit("dve", lambda v: v.tensor_tensor(st[0:1, 0:TOK], st[0:1, 2 * TOK:3 * TOK],
                                            st[0:1, 2 * TOK:3 * TOK], ALU.mult))
    ev = S.emit("dve", lambda v: v.tensor_tensor(st[0:1, 1 * TOK:2 * TOK], st[0:1, 1 * TOK:2 * TOK],
                                                 st[0:1, 0:TOK], ALU.subtract), inc=True)
    ev = S.emit("act", lambda a: a.activation(st[0:1, 1 * TOK:2 * TOK], st[0:1, 1 * TOK:2 * TOK],
                                              AF.Sqrt, bias=t["eps"][:, :]), waits=[ev], inc=True)
    ev = S.emit("dve", lambda v: v.reciprocal(st[0:1, 3 * TOK:4 * TOK], st[0:1, 1 * TOK:2 * TOK]),
                waits=[ev], inc=True)
    ev_mr = S.emit("dve", lambda v: v.tensor_tensor(st[0:1, 2 * TOK:3 * TOK], st[0:1, 2 * TOK:3 * TOK],
                                                    st[0:1, 3 * TOK:4 * TOK], ALU.mult), inc=True)
    bev = None
    for tg in range(NTG):
        S.emit("pe", lambda p, tg=tg: p.matmul(banks[4 + tg][:, :], t["ones_row_f"][:, :],
                                               R(3, tg), start=True, stop=True), waits=[ev_mr])
        bev = S.emit("pe", lambda p, tg=tg: p.matmul(banks[6 + tg][:, :], t["ones_row_f"][:, :],
                                                     R(2, tg), start=True, stop=True), inc=True)
    last = None
    for c in range(NC):
        for tg in range(NTG):
            sl = (c * NTG + tg) % 2
            xs = xstage[:, sl * TG:(sl + 1) * TG]
            qs = sqstage[:, sl * TG:(sl + 1) * TG]
            ev_ld = S.dma("sp", f"xs{sl}", lambda s, c=c, tg=tg, xs=xs: s.dma_start(
                out=xs, in_=xsrc[c * 128:(c + 1) * 128, tg * TG:(tg + 1) * TG]),
                waits=(xslot_free[sl] or []))
            S.emit("dve", lambda v, tg=tg, xs=xs, qs=qs: v.tensor_tensor(
                qs, xs, banks[4 + tg][:, :], ALU.mult),
                waits=[ev_ld, bev] + (sqslot_free[sl] or []))
            last = S.emit("dve", lambda v, c=c, tg=tg, qs=qs: v.tensor_tensor(
                xh_sb[:, c * TOK + tg * TG: c * TOK + (tg + 1) * TG],
                qs, banks[6 + tg][:, :], ALU.subtract), inc=True)
            xslot_free[sl] = [last]
            sqslot_free[sl] = [last]
    if sx:
        for c in range(NC):
            for tg in range(NTG):
                pe_ev = S.emit("pe", lambda p, c=c, tg=tg: p.matmul(
                    banks[tg][0:1, :], t["ones_col_b"][:, :],
                    xh_sb[:, c * TOK + tg * TG: c * TOK + (tg + 1) * TG],
                    start=(c == 0), stop=(c == NC - 1)),
                    waits=[last], inc=(c == NC - 1))
        for tg in range(NTG):
            S.emit("dve", lambda v, tg=tg: v.tensor_copy(R(0, tg), banks[tg][0:1, :]),
                   waits=[pe_ev], inc=True)
        last = S.emit("dve", lambda v: v.tensor_copy(t["corr"][0:1, 0:TOK], st[0:1, 0:TOK]),
                      inc=True)
    return last


def ffn_phase(L, t, xh_sb, h_sb, yp_sb, wbuf, banks, rstage,
              wgT, wuT, wdT, quant, sq_sb, sd05_ap,
              xold, resid_out, xh_ready, wfree=None):
    S = L.S
    st = t["stats"]
    bank_free = {}
    wslot_free = wfree if wfree is not None else [None, None]
    h_ev = {}
    y_ev = {}
    rslot_free = [[xh_ready], [xh_ready]]
    res_done = []

    for half in range(2):
        for which in ("g", "u"):
            W = wgT if which == "g" else wuT
            for ic in range(32):
                gi = half * 32 + ic
                sl = ic % 2
                ev_w = S.dma("pool", f"wb{sl}", lambda g, gi=gi, sl=sl, W=W: g.dma_start(
                    out=wbuf[sl].ap().rearrange("p (c m) -> p c m", m=128)[:, 0:16, :],
                    in_=W[:, gi, :, :]),
                    waits=[wslot_free[sl]])
                mm_last = [None, None]
                for c in range(NC):
                    for tg in range(NTG):
                        pb = banks[sl * 2 + tg]
                        ev = S.emit("pe", lambda p, c=c, tg=tg, sl=sl, pb=pb: p.matmul(
                            pb[:, :], wbuf[sl][:, c * 128:(c + 1) * 128],
                            xh_sb[:, c * TOK + tg * TG: c * TOK + (tg + 1) * TG],
                            start=(c == 0), stop=(not quant and c == NC - 1)),
                            waits=[ev_w, xh_ready, bank_free.get(sl * 2 + tg)],
                            inc=(not quant and c == NC - 1))
                        mm_last[tg] = ev
                if quant:
                    for tg in range(NTG):
                        pb = banks[sl * 2 + tg]
                        mm_last[tg] = S.emit("pe", lambda p, tg=tg, pb=pb: p.matmul(
                            pb[:, :], t["negk_row_b"][:, :],
                            t["corr"][0:1, tg * TG:(tg + 1) * TG],
                            start=False, stop=True), inc=True)
                wslot_free[sl] = mm_last[1]
                for tg in range(NTG):
                    pb = banks[sl * 2 + tg]
                    hsl = h_sb[:, ic * TOK + tg * TG: ic * TOK + (tg + 1) * TG]
                    if which == "g":
                        if quant:
                            ev = S.emit("act", lambda a, pb=pb, hsl=hsl, gi=gi: a.activation(
                                hsl, pb[:, :], AF.Silu, scale=sq_sb[:, gi:gi + 1]),
                                waits=[mm_last[tg]], inc=True)
                        else:
                            ev = S.emit("act", lambda a, pb=pb, hsl=hsl: a.activation(
                                hsl, pb[:, :], AF.Silu), waits=[mm_last[tg]], inc=True)
                    else:
                        if quant:
                            ev = S.emit("dve", lambda v, pb=pb, hsl=hsl, gi=gi: v.scalar_tensor_tensor(
                                hsl, pb[:, :], sq_sb[:, 64 + gi:64 + gi + 1], hsl,
                                ALU.mult, ALU.mult),
                                waits=[mm_last[tg], h_ev.get((ic, tg))], inc=True)
                        else:
                            ev = S.emit("dve", lambda v, pb=pb, hsl=hsl: v.tensor_mul(
                                hsl, pb[:, :], hsl),
                                waits=[mm_last[tg], h_ev.get((ic, tg))], inc=True)
                    bank_free[sl * 2 + tg] = ev
                    h_ev[(ic, tg)] = ev
        if quant:
            pe_ev = None
            for ic in range(32):
                for tg in range(NTG):
                    pe_ev = S.emit("pe", lambda p, ic=ic, tg=tg: p.matmul(
                        banks[4 + tg][0:1, :], t["ones_col_b"][:, :],
                        h_sb[:, ic * TOK + tg * TG: ic * TOK + (tg + 1) * TG],
                        start=(ic == 0), stop=(ic == 31)),
                        waits=[h_ev[(ic, tg)], bank_free.get(4 + tg)],
                        inc=(ic == 31))
            for tg in range(NTG):
                shev = S.emit("dve", lambda v, tg=tg: v.tensor_copy(
                    st[0:1, 1 * TOK + tg * TG: 1 * TOK + (tg + 1) * TG],
                    banks[4 + tg][0:1, :]), waits=[pe_ev], inc=True)
                bank_free[4 + tg] = shev
            shev = S.emit("dve", lambda v: v.tensor_copy(
                t["corr"][0:1, TOK:2 * TOK], st[0:1, TOK:2 * TOK]), inc=True)
            shev = S.emit("dve", lambda v: v.tensor_sub(
                t["corr"][0:1, 2 * TOK:3 * TOK], st[0:1, TOK:2 * TOK],
                t["corr"][0:1, TOK:2 * TOK]), inc=True)
        for dc in range(NC):
            sl = dc % 2
            ev_w = S.dma("pool", f"wb{sl}", lambda g, dc=dc, half=half, sl=sl: g.dma_start(
                out=wbuf[sl].ap().rearrange("p (c m) -> p c m", m=128),
                in_=wdT[:, dc, half * 32:(half + 1) * 32, :]),
                waits=[wslot_free[sl]])
            mm_last = [None, None]
            for c in range(32):
                for tg in range(NTG):
                    pb = banks[sl * 2 + tg]
                    ev = S.emit("pe", lambda p, c=c, tg=tg, sl=sl, pb=pb: p.matmul(
                        pb[:, :], wbuf[sl][:, c * 128:(c + 1) * 128],
                        h_sb[:, c * TOK + tg * TG: c * TOK + (tg + 1) * TG],
                        start=(c == 0), stop=(not quant and c == 31)),
                        waits=[ev_w, h_ev[(c, tg)], bank_free.get(sl * 2 + tg)],
                        inc=(not quant and c == 31))
                    mm_last[tg] = ev
            if quant:
                for tg in range(NTG):
                    pb = banks[sl * 2 + tg]
                    S.emit("pe", lambda p, tg=tg, pb=pb: p.matmul(
                        pb[:, :], t["negk_row_b"][:, :],
                        t["corr"][0:1, TOK + tg * TG: TOK + (tg + 1) * TG],
                        start=False, stop=False), waits=[shev])
                    mm_last[tg] = S.emit("pe", lambda p, tg=tg, pb=pb: p.matmul(
                        pb[:, :], t["negk_row_b"][:, :],
                        t["corr"][0:1, 2 * TOK + tg * TG: 2 * TOK + (tg + 1) * TG],
                        start=False, stop=True), inc=True)
            wslot_free[sl] = mm_last[1]
            for tg in range(NTG):
                pb = banks[sl * 2 + tg]
                ysl = yp_sb[:, dc * TOK + tg * TG: dc * TOK + (tg + 1) * TG]
                if half == 0:
                    ev = S.emit("dve", lambda v, pb=pb, ysl=ysl: v.tensor_copy(ysl, pb[:, :]),
                                waits=[mm_last[tg]], inc=True)
                    bank_free[sl * 2 + tg] = ev
                    y_ev[(dc, tg)] = ev
                else:
                    rsl = (dc * NTG + tg) % 2
                    rs = rstage[:, rsl * TG:(rsl + 1) * TG]
                    ev_x = S.dma("sp", f"rs{rsl}", lambda s, dc=dc, tg=tg, rs=rs: s.dma_start(
                        out=rs, in_=xold[dc * 128:(dc + 1) * 128, tg * TG:(tg + 1) * TG]),
                        waits=list(rslot_free[rsl]))
                    ev = S.emit("dve", lambda v, pb=pb, ysl=ysl: v.tensor_add(ysl, ysl, pb[:, :]),
                                waits=[mm_last[tg], y_ev[(dc, tg)]], inc=True)
                    bank_free[sl * 2 + tg] = ev
                    ev = S.emit("dve", lambda v, ysl=ysl, rs=rs, dc=dc: v.scalar_tensor_tensor(
                        ysl, ysl, sd05_ap(dc), rs, ALU.mult, ALU.add),
                        waits=[ev_x], inc=True)
                    rslot_free[rsl] = [ev]
                    ev2 = S.dma("sp", "sr", lambda s, dc=dc, tg=tg, ysl=ysl: s.dma_start(
                        out=resid_out[dc * 128:(dc + 1) * 128, tg * TG:(tg + 1) * TG],
                        in_=ysl), waits=[ev])
                    res_done.append(ev2)
    return res_done[-1]


def proj_phase(L, t, xh_sb, pbuf, wbuf, banks, Wt, bias_sb, out_hbm, xh_ready,
               extra=(), nout=NC, wfree=None):
    S = L.S
    wslot_free = wfree if wfree is not None else [None, None]
    pslot_free = [None, None]
    bank_free = {}
    last_st = None
    for oc in range(nout):
        sl = oc % 2
        ev_w = S.dma("pool", f"wb{sl}", lambda g, oc=oc, sl=sl: g.dma_start(
            out=wbuf[sl].ap().rearrange("p (c m) -> p c m", m=128)[:, 0:16, :],
            in_=Wt[:, oc, :, :]),
            waits=[wslot_free[sl]])
        mm_last = [None, None]
        for c in range(NC):
            for tg in range(NTG):
                pb = banks[sl * 2 + tg]
                ev = S.emit("pe", lambda p, c=c, tg=tg, sl=sl, pb=pb: p.matmul(
                    pb[:, :], wbuf[sl][:, c * 128:(c + 1) * 128],
                    xh_sb[:, c * TOK + tg * TG: c * TOK + (tg + 1) * TG],
                    start=(c == 0), stop=(c == NC - 1)),
                    waits=[ev_w, xh_ready, bank_free.get(sl * 2 + tg)],
                    inc=(c == NC - 1))
                mm_last[tg] = ev
        wslot_free[sl] = mm_last[1]
        for tg in range(NTG):
            pb = banks[sl * 2 + tg]
            dst = pbuf[:, sl * TOK + tg * TG: sl * TOK + (tg + 1) * TG]
            if bias_sb is not None:
                ev = S.emit("dve", lambda v, pb=pb, dst=dst, oc=oc: v.tensor_scalar_add(
                    dst, pb[:, :], bias_sb[:, oc:oc + 1]),
                    waits=[mm_last[tg], pslot_free[sl]] + list(extra), inc=True)
            else:
                ev = S.emit("dve", lambda v, pb=pb, dst=dst: v.tensor_copy(dst, pb[:, :]),
                            waits=[mm_last[tg], pslot_free[sl]] + list(extra), inc=True)
            bank_free[sl * 2 + tg] = ev
            last_st = S.dma("sp", f"po{sl}", lambda s, oc=oc, tg=tg, dst=dst: s.dma_start(
                out=out_hbm[oc * 128:(oc + 1) * 128, tg * TG:(tg + 1) * TG], in_=dst),
                waits=[ev])
            pslot_free[sl] = last_st
    return last_st


def build_launch_A(debug=False):
    L = LaunchBuilder("A")
    xT = L.inp("xT", [D, TOK], F32)
    wgT = L.inp("wgT", [128, 64, 16, 128], F32)
    wuT = L.inp("wuT", [128, 64, 16, 128], F32)
    wdT = L.inp("wdT", [128, 16, 64, 128], F32)
    Wq = L.inp("Wq", [128, 16, 16, 128], F32)
    Wk = L.inp("Wk", [128, 16, 16, 128], F32)
    Wv = L.inp("Wv", [128, 16, 16, 128], F32)
    b_qkv = L.inp("b_qkv", [128, 3 * NC], F32)
    resid1 = L.out("resid1", [D, TOK], F32)
    dbg = {}
    qT = L.out("qT", [D, TOK], BF16)
    kT = L.out("kT", [D, TOK], BF16)
    vT = L.out("vT", [D, TOK], BF16)

    L.begin()
    S = L.S
    t = alloc_common(L)
    banks = L.psum_banks()
    xh = L.sbuf("xh", [128, NC * TOK], BF16)
    h = L.sbuf("h", [128, 32 * TOK], BF16)
    yp = L.sbuf("yp", [128, NC * TOK], F32)
    wbuf = [L.sbuf("wb0", [128, 4096], BF16), L.sbuf("wb1", [128, 4096], BF16)]
    xstage = L.sbuf("xstage", [128, 2 * TG], F32)
    sqstage = yp[:, 0:2 * TG]
    bqkv_sb = L.sbuf("bqkv_sb", [128, 3 * NC], F32)
    ev_b = S.dma("pool", "cb", lambda g: g.dma_start(out=bqkv_sb[:, :], in_=b_qkv[:, :]))

    if debug:
        dbg["xh1"] = L.out("dbg_xh1", [128, NC * TOK], BF16)
        dbg["h"] = L.out("dbg_h", [128, 32 * TOK], BF16)
        dbg["xh2"] = L.out("dbg_xh2", [128, NC * TOK], BF16)
    wfree = [None, None]
    ev = ln_phase(L, t, xT.ap(), xh, banks, xstage, sqstage)
    if debug:
        ev = S.dma("sp", "dbg", lambda s: s.dma_start(out=dbg["xh1"][:, :], in_=xh[:, :]),
                   waits=[ev])
    ev = ffn_phase(L, t, xh, h, yp, wbuf, banks, xstage,
                   wgT, wuT, wdT, quant=False, sq_sb=None, sd05_ap=lambda dc: 0.5,
                   xold=xT.ap(), resid_out=resid1.ap(), xh_ready=ev, wfree=wfree)
    if debug:
        ev = S.dma("sp", "dbg", lambda s: s.dma_start(out=dbg["h"][:, :], in_=h[:, :]),
                   waits=[ev])
    ev = ln_phase(L, t, resid1.ap(), xh, banks, xstage, sqstage, pre=[ev])
    if debug:
        ev = S.dma("sp", "dbg", lambda s: s.dma_start(out=dbg["xh2"][:, :], in_=xh[:, :]),
                   waits=[ev])
    pq = h[:, 0:2 * TOK]
    pk = h[:, 2 * TOK:4 * TOK]
    pv = h[:, 4 * TOK:6 * TOK]
    e1 = proj_phase(L, t, xh, pq, wbuf, banks, Wq, bqkv_sb[:, 0:NC], qT.ap(), ev, extra=[ev_b], wfree=wfree)
    e2 = proj_phase(L, t, xh, pk, wbuf, banks, Wk, bqkv_sb[:, NC:2 * NC], kT.ap(), e1, extra=[ev_b], wfree=wfree)
    e3 = proj_phase(L, t, xh, pv, wbuf, banks, Wv, bqkv_sb[:, 2 * NC:3 * NC], vT.ap(), e2, extra=[ev_b], wfree=wfree)
    S.dma("sp", "fin", lambda s: s.nop(), waits=[e3, ev_b, S.ch_now("po0"), S.ch_now("po1")])
    return L




def attn_phase(L, t, S_ps, qsrc, kh, vh, vtm, P, PT, iota_sb, thr_sb, ident_sb,
               kfull, vfull, outT, scr, pre=(), qmine_in=None, ar1_q=None,
               b01=None, extra_qk=(), dbg=None):
    S = L.S
    nm, den, outstage = scr["nm"], scr["den"], scr["outstage"]
    qm = scr.get("qm")
    kh_free = list(pre)
    vh_free = list(pre)
    qm_free = []
    exp_prev = {}
    store_prev = {}
    bank_ev = [None] * 8
    bank_bf = [S_ps[:, b * 512:(b + 1) * 512].bitcast(BF16) for b in range(8)]
    last_store = None
    out_evac_ev = None

    for h in range(16):
        ev_k = S.dma("sp", "kh", lambda s, h=h: s.dma_start(
            out=kh[:, :], in_=kfull[h * 128:(h + 1) * 128, :]),
            waits=kh_free)
        ev_v = S.dma("sp", "vh", lambda s, h=h: s.dma_start(
            out=vh[:, :], in_=vfull[h * 128:(h + 1) * 128, :]),
            waits=vh_free)
        if qmine_in is not None:
            ev_q = S.dma("sp", "qm", lambda s, h=h: s.dma_start(
                out=qm[:, :], in_=qmine_in[h, :, :]), waits=qm_free)
            qv = qm
            qoff = 0
        else:
            ev_q = None
            qv = ar1_q
            qoff = h * 1024
        tr_ev = None
        vtm_evs = []
        for grp in range(4):
            bb = 6 + (grp % 2)
            for j in range(8):
                sc = grp * 8 + j
                tr_ev = S.emit("pe", lambda p, bb=bb, j=j, sc=sc: p.transpose(
                    bank_bf[bb][:, j * 128:(j + 1) * 128], vh[:, sc * 128:(sc + 1) * 128],
                    ident_sb[:, :]),
                    waits=[ev_v, bank_ev[bb], t["ident_ev"]], inc=(j == 7))
            ev = S.emit("dve", lambda v, grp=grp, bb=bb: v.tensor_copy(
                vtm[:, grp * 1024:(grp + 1) * 1024], bank_bf[bb][:, :]),
                waits=[tr_ev], inc=True)
            bank_ev[bb] = ev
            vtm_evs.append(ev)
        vh_free = [tr_ev]
        exp_evs = []
        norm_evs = {}
        for qtg in range(2):
            for qt4 in range(4):
                qtidx = qtg * 4 + qt4
                mm = None
                for kt in range(8):
                    mm = S.emit("pe", lambda p, kt=kt, qtidx=qtidx, qv=qv, qoff=qoff: p.matmul(
                        S_ps[:, kt * 512:(kt + 1) * 512],
                        qv[:, qoff + qtidx * 128: qoff + (qtidx + 1) * 128],
                        kh[:, kt * 512:(kt + 1) * 512], start=True, stop=True),
                        waits=[ev_k, ev_q, bank_ev[kt]] + vtm_evs + list(extra_qk),
                        inc=(kt == 7))
                ev = S.emit("dve", lambda v, qtidx=qtidx: v.tensor_reduce(
                    scr["nm8"].ap().unsqueeze(2),
                    S_ps.ap().rearrange("p (b n) -> p b n", n=512), AX.X, ALU.max),
                    waits=[mm, exp_prev.get(qtidx)], inc=False)
                ev = S.emit("dve", lambda v: v.reduce_max(
                    scr["nm1"][:, :], scr["nm8"][:, :], AX.X), inc=False)
                ev = S.emit("dve", lambda v, qtidx=qtidx: v.tensor_scalar_mul(
                    nm[:, qtidx:qtidx + 1], scr["nm1"][:, :], -SM), inc=True)
                pdst = P[:, qt4 * 4096:(qt4 + 1) * 4096]
                eexp = S.emit("act", lambda a, pdst=pdst, qtidx=qtidx: a.activation(
                    pdst, S_ps[:, :], AF.Exp, bias=nm[:, qtidx:qtidx + 1], scale=SM),
                    waits=[ev, out_evac_ev], inc=True)
                exp_evs.append(eexp)
                exp_prev[qtidx] = eexp
                for b in range(8):
                    bank_ev[b] = eexp
                if b01 is not None:
                    ev = S.emit("dve", lambda v, pdst=pdst, qtidx=qtidx, h=h: v.tensor_tensor(
                        pdst.rearrange("p (a b) -> p a b", b=64), pdst.rearrange("p (a b) -> p a b", b=64),
                        b01[:, (qtidx * 16 + h) * 64:(qtidx * 16 + h + 1) * 64]
                        .unsqueeze(2).broadcast_to([128, 64, 64]), ALU.mult),
                        waits=[eexp], inc=False)
                ev = S.emit("dve", lambda v, pdst=pdst, qtidx=qtidx: v.scalar_tensor_tensor(
                    pdst, iota_sb[:, :], thr_sb[:, qtidx:qtidx + 1], pdst,
                    ALU.is_le, ALU.mult, accum_out=den[:, qtidx:qtidx + 1]),
                    waits=[eexp, t["iota_ev"], t.get("thr_ev")], inc=True)
                ev = S.emit("dve", lambda v, qtidx=qtidx: v.reciprocal(
                    den[:, qtidx:qtidx + 1], den[:, qtidx:qtidx + 1]), inc=True)
                ev = S.emit("act", lambda a, pdst=pdst, qtidx=qtidx: a.activation(
                    pdst, pdst, AF.Copy, scale=den[:, qtidx:qtidx + 1]),
                    waits=[ev], inc=True)
                norm_evs[qt4] = ev
            pt_evs = {}
            for rnd in range(4):
                for j in range(4):
                    bb = j
                    tr = None
                    for half in range(2):
                        sc = rnd * 8 + j * 2 + half
                        for qt4 in range(4):
                            tr = S.emit("pe", lambda p, bb=bb, half=half, qt4=qt4, sc=sc: p.transpose(
                                bank_bf[bb][:, half * 512 + qt4 * 128: half * 512 + (qt4 + 1) * 128],
                                P[:, qt4 * 4096 + sc * 128: qt4 * 4096 + (sc + 1) * 128],
                                ident_sb[:, :]),
                                waits=[norm_evs[qt4], bank_ev[bb]],
                                inc=(qt4 == 3 and half == 1))
                    sc0 = rnd * 8 + j * 2
                    ev = S.emit("dve", lambda v, sc0=sc0, bb=bb: v.tensor_copy(
                        PT[:, sc0 * 512:(sc0 + 2) * 512], bank_bf[bb][:, :]),
                        waits=[tr], inc=True)
                    bank_ev[bb] = ev
                    pt_evs[sc0] = ev
                    pt_evs[sc0 + 1] = ev
            pv = None
            for sc in range(32):
                pv = S.emit("pe", lambda p, sc=sc: p.matmul(
                    S_ps[:, 4 * 512:5 * 512], vtm[:, sc * 128:(sc + 1) * 128],
                    PT[:, sc * 512:(sc + 1) * 512], start=(sc == 0), stop=(sc == 31)),
                    waits=[pt_evs[sc], bank_ev[4]], inc=(sc == 31))
            out_evac_ev = S.emit("dve", lambda v, qtg=qtg: v.tensor_copy(
                outstage[:, qtg * 512:(qtg + 1) * 512], S_ps[:, 4 * 512:5 * 512]),
                waits=[pv, store_prev.get(qtg)], inc=True)
            bank_ev[4] = out_evac_ev
            last_store = S.dma("sp", f"so{qtg}", lambda s, h=h, qtg=qtg: s.dma_start(
                out=outT[h * 128:(h + 1) * 128, qtg * 512:(qtg + 1) * 512],
                in_=outstage[:, qtg * 512:(qtg + 1) * 512]),
                waits=[out_evac_ev])
            store_prev[qtg] = last_store
            if dbg is not None and h == 0 and qtg == 0:
                dde = S.dma("sp", "dbg", lambda s: s.dma_start(out=dbg["P"][:, :], in_=P[:, :]),
                            waits=[norm_evs[3], out_evac_ev])
                dde = S.dma("sp", "dbg", lambda s: s.dma_start(out=dbg["PT"][:, :], in_=PT[:, :]),
                            waits=[out_evac_ev])
                dde = S.dma("sp", "dbg", lambda s: s.dma_start(out=dbg["nm"][:, :], in_=nm[:, :]))
                dde = S.dma("sp", "dbg", lambda s: s.dma_start(out=dbg["den"][:, :], in_=den[:, :]))
                dde = S.dma("sp", "dbg", lambda s: s.dma_start(out=dbg["vtm"][:, :], in_=vtm[:, :]))
                exp_prev[0] = dde
        kh_free = [mm]
        qm_free = [mm]
    return [S.ch_now("so0"), S.ch_now("so1")]


def wo_phase(L, t, oT_dram, Wo, bo_sb, resid_in, resid_out, wbuf, banks, ar1,
             rstage, wfree, pre=(), ar1_free=()):
    S = L.S
    ev_o = S.dma("sp", "ro", lambda s: s.dma_start(
        out=ar1.ap().rearrange("p (c n) -> p c n", n=TOK),
        in_=oT_dram.rearrange("(c p) n -> p c n", p=128)),
        waits=list(pre) + list(ar1_free))
    bank_free = {}
    rslot_free = [[], []]
    last = None
    for oc in range(NC):
        sl = oc % 2
        ev_w = S.dma("pool", f"wb{sl}", lambda g, oc=oc, sl=sl: g.dma_start(
            out=wbuf[sl].ap().rearrange("p (c m) -> p c m", m=128)[:, 0:16, :],
            in_=Wo[:, oc, :, :]),
            waits=[wfree[sl]])
        mm_last = [None, None]
        for c in range(NC):
            for tg in range(NTG):
                pb = banks[sl * 2 + tg]
                ev = S.emit("pe", lambda p, c=c, tg=tg, sl=sl, pb=pb: p.matmul(
                    pb[:, :], wbuf[sl][:, c * 128:(c + 1) * 128],
                    ar1[:, c * TOK + tg * TG: c * TOK + (tg + 1) * TG],
                    start=(c == 0), stop=(c == NC - 1)),
                    waits=[ev_w, ev_o, bank_free.get(sl * 2 + tg)],
                    inc=(c == NC - 1))
                mm_last[tg] = ev
        wfree[sl] = mm_last[1]
        for tg in range(NTG):
            pb = banks[sl * 2 + tg]
            rsl = (oc * NTG + tg) % 2
            rs = rstage[:, rsl * TG:(rsl + 1) * TG]
            ev_x = S.dma("sp", f"rs{rsl}", lambda s, oc=oc, tg=tg, rs=rs: s.dma_start(
                out=rs, in_=resid_in[oc * 128:(oc + 1) * 128, tg * TG:(tg + 1) * TG]),
                waits=rslot_free[rsl] + list(pre))
            if bo_sb is not None:
                ev = S.emit("dve", lambda v, pb=pb, rs=rs, oc=oc: v.scalar_tensor_tensor(
                    rs, pb[:, :], bo_sb[:, oc:oc + 1], rs, ALU.add, ALU.add),
                    waits=[mm_last[tg], ev_x], inc=True)
            else:
                ev = S.emit("dve", lambda v, pb=pb, rs=rs: v.tensor_add(rs, pb[:, :], rs),
                            waits=[mm_last[tg], ev_x], inc=True)
            bank_free[sl * 2 + tg] = ev
            last = S.dma("sp", f"wr{rsl}", lambda s, oc=oc, tg=tg, rs=rs: s.dma_start(
                out=resid_out[oc * 128:(oc + 1) * 128, tg * TG:(tg + 1) * TG], in_=rs),
                waits=[ev])
            rslot_free[rsl] = [last]
    return [S.ch_now("wr0"), S.ch_now("wr1")]


def build_launch_B1(debug=False):
    L = LaunchBuilder("B1")
    resid1 = L.inp("resid1", [D, TOK], F32)
    qmine = L.inp("qmine", [16, 128, TOK], BF16)
    kfull = L.inp("kfull", [D, T], BF16)
    vfull = L.inp("vfull", [D, T], BF16)
    iota = L.inp("iota", [128, T], F32)
    thr = L.inp("thr", [128, 8], F32)
    ident = L.inp("ident", [128, 128], BF16)
    Wo = L.inp("Wo", [128, 16, 16, 128], F32)
    bo = L.inp("bo", [128, NC], F32)
    resid2 = L.out("resid2", [D, TOK], F32)
    outT = (L.out if debug else L.scratch)("outT", [D, TOK], BF16)

    L.begin()
    S = L.S
    t = alloc_common(L)
    S_ps = L.es.enter_context(L.nc.psum_tensor("S_ps", [128, 4096], F32))
    banks = [S_ps[:, b * 512:(b + 1) * 512] for b in range(8)]
    AR1 = L.sbuf("AR1", [128, NC * TOK], BF16)
    AR2 = L.sbuf("AR2", [128, 2 * NC * TOK], BF16)
    AR3 = L.sbuf("AR3", [128, 2 * NC * TOK], BF16)
    wbuf = [L.sbuf("wb0", [128, 4096], BF16), L.sbuf("wb1", [128, 4096], BF16)]
    xstage = L.sbuf("xstage", [128, 2 * TG], F32)
    kh = AR3[:, 0:4096]
    vh = AR3[:, 4096:8192]
    vtm = AR3[:, 8192:12288]
    qm = AR3[:, 12288:13312]
    outstage = AR3[:, 13312:14336]
    iota_sb = AR3[:, 16384:24576].bitcast(F32)
    thr_sb = L.sbuf("thr_sb", [128, 8], F32)
    ident_sb = AR3[:, 14336:14464]
    bo_sb = L.sbuf("bo_sb", [128, NC], F32)
    nm = L.sbuf("nm", [128, 16], F32)
    nm8 = L.sbuf("nm8", [128, 8], F32)
    nm1 = L.sbuf("nm1", [128, 1], F32)
    den = L.sbuf("den", [128, 16], F32)
    P = AR2[:, 0:NC * TOK]
    PT = AR2[:, NC * TOK:2 * NC * TOK]

    S.dma("sp", "cst", lambda s: s.dma_start(out=iota_sb, in_=iota[:, :]))
    S.dma("sp", "cst", lambda s: s.dma_start(out=thr_sb[:, :], in_=thr[:, :]))
    ev3 = S.dma("sp", "cst", lambda s: s.dma_start(out=ident_sb[:, :], in_=ident[:, :]))
    t["iota_ev"] = ev3
    t["thr_ev"] = ev3
    t["ident_ev"] = ev3
    ev2 = ev3
    ev4 = S.dma("pool", "cb", lambda g: g.dma_start(out=bo_sb[:, :], in_=bo[:, :]))

    scr = dict(nm=nm, den=den, outstage=outstage, qm=qm, nm8=nm8, nm1=nm1)
    dbg = None
    if debug:
        dbg = {"P": L.out("dbg_P", [128, NC * TOK], BF16),
               "PT": L.out("dbg_PT", [128, NC * TOK], BF16),
               "nm": L.out("dbg_nm", [128, 16], F32),
               "den": L.out("dbg_den", [128, 16], F32),
               "vtm": L.out("dbg_vtm", [128, 4096], BF16)}
    ev = attn_phase(L, t, S_ps, None, kh, vh, vtm, P, PT, iota_sb, thr_sb, ident_sb,
                    kfull, vfull, outT.ap(), scr, pre=[ev2], qmine_in=qmine.ap(), dbg=dbg)
    wfree = [None, None]
    ev = wo_phase(L, t, outT.ap(), Wo, bo_sb, resid1.ap(), resid2.ap(), wbuf, banks,
                  AR1, xstage, wfree, pre=ev + [ev4])
    S.dma("sp", "fin", lambda s: s.nop(), waits=ev)
    return L




def build_launch_B2():
    L = LaunchBuilder("B2")
    resid2 = L.inp("resid2", [D, TOK], F32)
    wgT0 = L.inp("wgT0", [128, 64, 16, 128], F32)
    wuT0 = L.inp("wuT0", [128, 64, 16, 128], F32)
    wdT0 = L.inp("wdT0", [128, 16, 64, 128], F32)
    wgT1 = L.inp("wgT1", [128, 64, 16, 128], I32)
    wuT1 = L.inp("wuT1", [128, 64, 16, 128], I32)
    wdT1 = L.inp("wdT1", [128, 16, 64, 128], I32)
    sq1 = L.inp("sq1", [128, 128], F32)
    sd1 = L.inp("sd1", [128, NC], F32)
    Wq1 = L.inp("Wq1", [128, 16, 16, 128], F32)
    Wk1 = L.inp("Wk1", [128, 16, 16, 128], F32)
    Wv1 = L.inp("Wv1", [128, 16, 16, 128], F32)
    resid4 = L.out("resid4", [D, TOK], F32)
    qT1 = L.out("qT1", [D, TOK], BF16)
    kT1 = L.out("kT1", [D, TOK], BF16)
    vT1 = L.out("vT1", [D, TOK], BF16)
    resid3 = L.scratch("resid3", [D, TOK], F32)

    L.begin()
    S = L.S
    t = alloc_common(L)
    S_ps = L.es.enter_context(L.nc.psum_tensor("S_ps", [128, 4096], F32))
    banks = [S_ps[:, b * 512:(b + 1) * 512] for b in range(8)]
    xh = L.sbuf("xh", [128, NC * TOK], BF16)
    h = L.sbuf("h", [128, 32 * TOK], BF16)
    yp = L.sbuf("yp", [128, NC * TOK], F32)
    wbuf = [L.sbuf("wb0", [128, 4096], BF16), L.sbuf("wb1", [128, 4096], BF16)]
    xstage = L.sbuf("xstage", [128, 2 * TG], F32)
    sqstage = yp[:, 0:2 * TG]
    sq_sb = L.sbuf("sq_sb", [128, 128], F32)
    sd05_sb = L.sbuf("sd05_sb", [128, NC], F32)
    ev_s1 = S.dma("pool", "cb", lambda g: g.dma_start(out=sq_sb[:, :], in_=sq1[:, :]))
    ev_s2 = S.dma("pool", "cb", lambda g: g.dma_start(out=sd05_sb[:, :], in_=sd1[:, :]))
    ev_sd = S.emit("dve", lambda v: v.tensor_scalar_mul(sd05_sb[:, :], sd05_sb[:, :], 0.5),
                   waits=[S.ch_now("cb")], inc=True)

    wfree = [None, None]
    ev = ln_phase(L, t, resid2.ap(), xh, banks, xstage, sqstage)
    ev = ffn_phase(L, t, xh, h, yp, wbuf, banks, xstage,
                   wgT0, wuT0, wdT0, quant=False, sq_sb=None, sd05_ap=lambda dc: 0.5,
                   xold=resid2.ap(), resid_out=resid3.ap(), xh_ready=ev, wfree=wfree)
    ev = ln_phase(L, t, resid3.ap(), xh, banks, xstage, sqstage, pre=[ev], sx=True)
    ev = ffn_phase(L, t, xh, h, yp, wbuf, banks, xstage,
                   wgT1, wuT1, wdT1, quant=True, sq_sb=sq_sb,
                   sd05_ap=lambda dc: sd05_sb[:, dc:dc + 1],
                   xold=resid3.ap(), resid_out=resid4.ap(), xh_ready=ev, wfree=wfree)
    ev = ln_phase(L, t, resid4.ap(), xh, banks, xstage, sqstage, pre=[ev])
    pq = h[:, 0:2 * TOK]
    pk = h[:, 2 * TOK:4 * TOK]
    pv = h[:, 4 * TOK:6 * TOK]
    e1 = proj_phase(L, t, xh, pq, wbuf, banks, Wq1, None, qT1.ap(), ev, wfree=wfree)
    e2 = proj_phase(L, t, xh, pk, wbuf, banks, Wk1, None, kT1.ap(), e1, wfree=wfree)
    e3 = proj_phase(L, t, xh, pv, wbuf, banks, Wv1, None, vT1.ap(), e2, wfree=wfree)
    S.dma("sp", "fin", lambda s: s.nop(),
          waits=[e3, S.ch_now("po0"), S.ch_now("po1"), S.ch_now("sr")])
    return L


def build_launch_C1(debug=False):
    L = LaunchBuilder("C1")
    resid4 = L.inp("resid4", [D, TOK], F32)
    q1 = L.inp("q1", [D, TOK], BF16)
    kfull = L.inp("kfull", [D, T], BF16)
    vfull = L.inp("vfull", [D, T], BF16)
    iota = L.inp("iota", [128, T], F32)
    thr = L.inp("thr", [128, 8], F32)
    ident = L.inp("ident", [128, 128], BF16)
    gmask = L.inp("gmask", [8, 128, NB], F32)
    Wo = L.inp("Wo", [128, 16, 16, 128], F32)
    resid5 = L.out("resid5", [D, TOK], F32)
    outT = L.scratch("outT", [D, TOK], BF16)

    L.begin()
    S = L.S
    t = alloc_common(L)
    S_ps = L.es.enter_context(L.nc.psum_tensor("S_ps", [128, 4096], F32))
    banks = [S_ps[:, b * 512:(b + 1) * 512] for b in range(8)]
    AR1 = L.sbuf("AR1", [128, NC * TOK], BF16)
    AR2 = L.sbuf("AR2", [128, 2 * NC * TOK], BF16)
    AR3 = L.sbuf("AR3", [128, 2 * NC * TOK], BF16)
    wbuf = [L.sbuf("wb0", [128, 4096], BF16), L.sbuf("wb1", [128, 4096], BF16)]
    xstage = L.sbuf("xstage", [128, 2 * TG], F32)
    kh = AR3[:, 0:4096]
    vh = AR3[:, 4096:8192]
    vtm = AR3[:, 8192:12288]
    outstage = AR3[:, 13312:14336]
    iota_sb = AR3[:, 16384:24576].bitcast(F32)
    b01 = AR3[:, 24576:32768]
    gate_all = AR2[:, 0:NC * TOK].bitcast(F32)
    gwork = AR2[:, NC * TOK:2 * NC * TOK].bitcast(F32)
    thr_sb = L.sbuf("thr_sb", [128, 8], F32)
    ident_sb = L.sbuf("ident_sb", [128, 128], BF16)
    gm_sb = L.sbuf("gm_sb", [128, 8 * NB], F32)
    kg_sb = L.sbuf("kg_sb", [128, 16 * NB], BF16)
    eqm = AR3[:, 14336:16384].bitcast(F32)
    tau = L.sbuf("tau", [128, 16], F32)
    nm = L.sbuf("nm", [128, 16], F32)
    nm8 = L.sbuf("nm8", [128, 8], F32)
    nm1 = L.sbuf("nm1", [128, 1], F32)
    den = L.sbuf("den", [128, 16], F32)

    S.dma("sp", "cst", lambda s: s.dma_start(out=iota_sb, in_=iota[:, :]))
    S.dma("sp", "cst", lambda s: s.dma_start(out=thr_sb[:, :], in_=thr[:, :]))
    S.dma("sp", "cst", lambda s: s.dma_start(
        out=gm_sb.rearrange("p (q n) -> p q n", n=NB),
        in_=gmask.ap().rearrange("q p n -> p q n")))
    ev3 = S.dma("sp", "cst", lambda s: s.dma_start(out=ident_sb[:, :], in_=ident[:, :]))
    t["iota_ev"] = ev3
    t["thr_ev"] = ev3
    t["ident_ev"] = ev3
    ev_q1 = S.dma("sp", "q1", lambda s: s.dma_start(
        out=AR1.ap().rearrange("p (c n) -> p c n", n=TOK),
        in_=q1.ap().rearrange("(c p) n -> p c n", p=128)))

    kh_free = []
    ge_evs = []
    for hh in range(16):
        ev_k = S.dma("sp", "kh", lambda s, hh=hh: s.dma_start(
            out=kh[:, :], in_=kfull[hh * 128:(hh + 1) * 128, :]), waits=kh_free)
        ev_kg = S.emit("dve", lambda v, hh=hh: v.tensor_reduce(
            eqm.rearrange("p (a n) -> p a n", n=1)[:, 0:NB, :],
            kh.rearrange("p (a n) -> p a n", n=BLK).transpose([0, 2, 1])
            if False else kh.rearrange("p (a n) -> p a n", n=BLK),
            AX.X, ALU.add), waits=[ev_k], inc=True)
        ev_kg = S.emit("dve", lambda v, hh=hh: v.tensor_scalar_mul(
            kg_sb[:, hh * NB:(hh + 1) * NB], eqm[:, 0:NB],
            1.0 / BLK), inc=True)
        mm = None
        for qt in range(8):
            mm = S.emit("pe", lambda p, hh=hh, qt=qt: p.matmul(
                S_ps[:, 5 * 512 + qt * NB: 5 * 512 + (qt + 1) * NB],
                AR1[:, hh * TOK + qt * 128: hh * TOK + (qt + 1) * 128],
                kg_sb[:, hh * NB:(hh + 1) * NB], start=True, stop=True),
                waits=[ev_kg, ev_q1, ge_evs[-1] if ge_evs else None],
                inc=(qt == 7))
        kh_free = [ev_kg]
        ev_g = S.emit("dve", lambda v, hh=hh: v.tensor_tensor(
            gate_all.rearrange("p (q a n) -> p (q a) n", a=16, n=NB)
            .rearrange("p qa n -> p qa n")[:, hh::16, :],
            S_ps[:, 5 * 512:6 * 512].rearrange("p (q n) -> p q n", n=NB),
            gm_sb.rearrange("p (q n) -> p q n", n=NB), ALU.add),
            waits=[mm, S.ch_now("cst")], inc=True)
        ge_evs.append(ev_g)

    b01_evs = []
    for qt in range(8):
        W3 = gwork.rearrange("p (q x) -> p q x", x=TOK)[:, qt, :].rearrange(
            "p (a n) -> p a n", n=NB)
        G3 = gate_all.rearrange("p (q x) -> p q x", x=TOK)[:, qt, :].rearrange(
            "p (a n) -> p a n", n=NB)
        ev = S.emit("dve", lambda v, W3=W3, G3=G3: v.tensor_copy(W3, G3),
                    waits=ge_evs, inc=True)
        for it in range(7):
            ev = S.emit("dve", lambda v, W3=W3: v.tensor_reduce(
                tau.ap().rearrange("p (a n) -> p a n", n=1), W3, AX.X, ALU.max),
                inc=True)
            ev = S.emit("dve", lambda v, W3=W3: v.tensor_tensor(
                eqm.rearrange("p (a n) -> p a n", n=NB), W3,
                tau.ap().unsqueeze(2).broadcast_to([128, 16, NB]), ALU.is_equal),
                inc=True)
            ev = S.emit("dve", lambda v, W3=W3: v.scalar_tensor_tensor(
                W3, eqm.rearrange("p (a n) -> p a n", n=NB), -2.0e30, W3,
                ALU.mult, ALU.add), inc=True)
        ev = S.emit("dve", lambda v, W3=W3: v.tensor_reduce(
            tau.ap().rearrange("p (a n) -> p a n", n=1), W3, AX.X, ALU.max),
            inc=True)
        ev = S.emit("dve", lambda v, G3=G3, qt=qt: v.tensor_tensor(
            b01.rearrange("p (q a n) -> p q a n", a=16, n=NB)[:, qt, :, :], G3,
            tau.ap().unsqueeze(2).broadcast_to([128, 16, NB]), ALU.is_ge),
            inc=True)
        b01_evs.append(ev)

    scr = dict(nm=nm, den=den, outstage=outstage, qm=None, nm8=nm8, nm1=nm1)
    dbg = None
    if debug:
        dbg = {"P": L.out("dbg_P", [128, NC * TOK], BF16),
               "PT": L.out("dbg_PT", [128, NC * TOK], BF16),
               "nm": L.out("dbg_nm", [128, 16], F32),
               "den": L.out("dbg_den", [128, 16], F32),
               "vtm": L.out("dbg_vtm", [128, 4096], BF16)}
        dbg["gate"] = L.out("dbg_gate", [128, 8192], F32)
        dbg["b01"] = L.out("dbg_b01", [128, 8192], BF16)
        S.dma("sp", "dbg", lambda s: s.dma_start(out=dbg["gate"][:, :], in_=gate_all),
              waits=b01_evs)
        S.dma("sp", "dbg", lambda s: s.dma_start(out=dbg["b01"][:, :], in_=b01),
              waits=b01_evs)
    ev = attn_phase(L, t, S_ps, None, kh, vh, vtm, AR2[:, 0:NC * TOK],
                    AR2[:, NC * TOK:2 * NC * TOK], iota_sb, thr_sb, ident_sb,
                    kfull, vfull, outT.ap(), scr, pre=b01_evs + [S.ch_now("dbg")] if debug else b01_evs,
                    ar1_q=AR1, b01=b01, dbg=dbg)
    wfree = [None, None]
    ev = wo_phase(L, t, outT.ap(), Wo, None, resid4.ap(), resid5.ap(), wbuf, banks,
                  AR1, xstage, wfree, pre=ev, ar1_free=[S.now("pe")])
    S.dma("sp", "fin", lambda s: s.nop(), waits=ev)
    return L


def build_launch_C2():
    L = LaunchBuilder("C2")
    resid5 = L.inp("resid5", [D, TOK], F32)
    wgT1 = L.inp("wgT1", [128, 64, 16, 128], I32)
    wuT1 = L.inp("wuT1", [128, 64, 16, 128], I32)
    wdT1 = L.inp("wdT1", [128, 16, 64, 128], I32)
    sq1 = L.inp("sq1", [128, 128], F32)
    sd1 = L.inp("sd1", [128, NC], F32)
    yT = L.out("yT", [D, TOK], F32)

    L.begin()
    S = L.S
    t = alloc_common(L)
    S_ps = L.es.enter_context(L.nc.psum_tensor("S_ps", [128, 4096], F32))
    banks = [S_ps[:, b * 512:(b + 1) * 512] for b in range(8)]
    xh = L.sbuf("xh", [128, NC * TOK], BF16)
    h = L.sbuf("h", [128, 32 * TOK], BF16)
    yp = L.sbuf("yp", [128, NC * TOK], F32)
    wbuf = [L.sbuf("wb0", [128, 4096], BF16), L.sbuf("wb1", [128, 4096], BF16)]
    xstage = L.sbuf("xstage", [128, 2 * TG], F32)
    sqstage = yp[:, 0:2 * TG]
    sq_sb = L.sbuf("sq_sb", [128, 128], F32)
    sd05_sb = L.sbuf("sd05_sb", [128, NC], F32)
    S.dma("pool", "cb", lambda g: g.dma_start(out=sq_sb[:, :], in_=sq1[:, :]))
    S.dma("pool", "cb", lambda g: g.dma_start(out=sd05_sb[:, :], in_=sd1[:, :]))
    ev_sd = S.emit("dve", lambda v: v.tensor_scalar_mul(sd05_sb[:, :], sd05_sb[:, :], 0.5),
                   waits=[S.ch_now("cb")], inc=True)
    wfree = [None, None]
    ev = ln_phase(L, t, resid5.ap(), xh, banks, xstage, sqstage, sx=True)
    ev = ffn_phase(L, t, xh, h, yp, wbuf, banks, xstage,
                   wgT1, wuT1, wdT1, quant=True, sq_sb=sq_sb,
                   sd05_ap=lambda dc: sd05_sb[:, dc:dc + 1],
                   xold=resid5.ap(), resid_out=yT.ap(), xh_ready=ev, wfree=wfree)
    S.dma("sp", "fin", lambda s: s.nop(), waits=[ev, S.ch_now("sr")])
    return L




def build_launch_B():
    L = LaunchBuilder("B")
    resid1 = L.inp("resid1", [D, TOK], F32)
    qmine = L.inp("qmine", [16, 128, TOK], BF16)
    kfull = L.inp("kfull", [D, T], BF16)
    vfull = L.inp("vfull", [D, T], BF16)
    iota = L.inp("iota", [128, T], F32)
    thr = L.inp("thr", [128, 8], F32)
    ident = L.inp("ident", [128, 128], BF16)
    Wo = L.inp("Wo", [128, 16, 16, 128], F32)
    bo = L.inp("bo", [128, NC], F32)
    wgT0 = L.inp("wgT0", [128, 64, 16, 128], F32)
    wuT0 = L.inp("wuT0", [128, 64, 16, 128], F32)
    wdT0 = L.inp("wdT0", [128, 16, 64, 128], F32)
    wgT1 = L.inp("wgT1", [128, 64, 16, 128], I32)
    wuT1 = L.inp("wuT1", [128, 64, 16, 128], I32)
    wdT1 = L.inp("wdT1", [128, 16, 64, 128], I32)
    sq1 = L.inp("sq1", [128, 128], F32)
    sd1 = L.inp("sd1", [128, NC], F32)
    Wq1 = L.inp("Wq1", [128, 16, 16, 128], F32)
    Wk1 = L.inp("Wk1", [128, 16, 16, 128], F32)
    Wv1 = L.inp("Wv1", [128, 16, 16, 128], F32)
    resid4 = L.out("resid4", [D, TOK], F32)
    qT1 = L.out("qT1", [D, TOK], BF16)
    kT1 = L.out("kT1", [D, TOK], BF16)
    vT1 = L.out("vT1", [D, TOK], BF16)
    outT = L.scratch("outT", [D, TOK], BF16)
    resid2 = L.scratch("resid2", [D, TOK], F32)
    resid3 = L.scratch("resid3", [D, TOK], F32)

    L.begin()
    S = L.S
    t = alloc_common(L)
    S_ps = L.es.enter_context(L.nc.psum_tensor("S_ps", [128, 4096], F32))
    banks = [S_ps[:, b * 512:(b + 1) * 512] for b in range(8)]
    AR1 = L.sbuf("AR1", [128, NC * TOK], BF16)
    AR2 = L.sbuf("AR2", [128, 2 * NC * TOK], BF16)
    AR3 = L.sbuf("AR3", [128, 2 * NC * TOK], BF16)
    wbuf = [L.sbuf("wb0", [128, 4096], BF16), L.sbuf("wb1", [128, 4096], BF16)]
    xstage = L.sbuf("xstage", [128, 2 * TG], F32)
    kh = AR3[:, 0:4096]
    vh = AR3[:, 4096:8192]
    vtm = AR3[:, 8192:12288]
    qm = AR3[:, 12288:13312]
    outstage = AR3[:, 13312:14336]
    iota_sb = AR3[:, 16384:24576].bitcast(F32)
    thr_sb = L.sbuf("thr_sb", [128, 8], F32)
    ident_sb = AR3[:, 14336:14464]
    bo_sb = L.sbuf("bo_sb", [128, NC], F32)
    nm = L.sbuf("nm", [128, 16], F32)
    nm8 = L.sbuf("nm8", [128, 8], F32)
    nm1 = L.sbuf("nm1", [128, 1], F32)
    den = L.sbuf("den", [128, 16], F32)
    sq_sb = L.sbuf("sq_sb", [128, 128], F32)
    sd05_sb = L.sbuf("sd05_sb", [128, NC], F32)
    P = AR2[:, 0:NC * TOK]
    PT = AR2[:, NC * TOK:2 * NC * TOK]
    xh = AR1
    hbuf = AR2
    yp = AR3[:, 0:2 * NC * TOK].bitcast(F32)
    sqstage = yp[:, 0:2 * TG]

    S.dma("sp", "cst", lambda s: s.dma_start(out=iota_sb, in_=iota[:, :]))
    S.dma("sp", "cst", lambda s: s.dma_start(out=thr_sb[:, :], in_=thr[:, :]))
    ev3 = S.dma("sp", "cst", lambda s: s.dma_start(out=ident_sb[:, :], in_=ident[:, :]))
    t["iota_ev"] = ev3
    t["thr_ev"] = ev3
    t["ident_ev"] = ev3
    S.dma("pool", "cb", lambda g: g.dma_start(out=bo_sb[:, :], in_=bo[:, :]))
    S.dma("pool", "cb", lambda g: g.dma_start(out=sq_sb[:, :], in_=sq1[:, :]))
    S.dma("pool", "cb", lambda g: g.dma_start(out=sd05_sb[:, :], in_=sd1[:, :]))
    ev4 = S.ch_now("cb")
    ev_sd = S.emit("dve", lambda v: v.tensor_scalar_mul(sd05_sb[:, :], sd05_sb[:, :], 0.5),
                   waits=[ev4], inc=True)

    scr = dict(nm=nm, den=den, outstage=outstage, qm=qm, nm8=nm8, nm1=nm1)
    ev = attn_phase(L, t, S_ps, None, kh, vh, vtm, P, PT, iota_sb, thr_sb, ident_sb,
                    kfull, vfull, outT.ap(), scr, pre=[ev3], qmine_in=qmine.ap())
    wfree = [None, None]
    ev = wo_phase(L, t, outT.ap(), Wo, bo_sb, resid1.ap(), resid2.ap(), wbuf, banks,
                  AR1, xstage, wfree, pre=ev + [ev4])
    ev2 = ln_phase(L, t, resid2.ap(), xh, banks, xstage, sqstage, pre=ev)
    ev2 = ffn_phase(L, t, xh, hbuf, yp, wbuf, banks, xstage,
                    wgT0, wuT0, wdT0, quant=False, sq_sb=None, sd05_ap=lambda dc: 0.5,
                    xold=resid2.ap(), resid_out=resid3.ap(), xh_ready=ev2, wfree=wfree)
    ev2 = ln_phase(L, t, resid3.ap(), xh, banks, xstage, sqstage, pre=[ev2], sx=True)
    ev2 = S.emit("dve", lambda v: v.nop(), waits=[ev2, ev_sd], inc=True)
    ev2 = ffn_phase(L, t, xh, hbuf, yp, wbuf, banks, xstage,
                    wgT1, wuT1, wdT1, quant=True, sq_sb=sq_sb,
                    sd05_ap=lambda dc: sd05_sb[:, dc:dc + 1],
                    xold=resid3.ap(), resid_out=resid4.ap(), xh_ready=ev2, wfree=wfree)
    ev2 = ln_phase(L, t, resid4.ap(), xh, banks, xstage, sqstage, pre=[ev2])
    pq = hbuf[:, 0:2 * TOK]
    pk = hbuf[:, 2 * TOK:4 * TOK]
    pv = hbuf[:, 4 * TOK:6 * TOK]
    e1 = proj_phase(L, t, xh, pq, wbuf, banks, Wq1, None, qT1.ap(), ev2, wfree=wfree)
    e2 = proj_phase(L, t, xh, pk, wbuf, banks, Wk1, None, kT1.ap(), e1, wfree=wfree)
    e3 = proj_phase(L, t, xh, pv, wbuf, banks, Wv1, None, vT1.ap(), e2, wfree=wfree)
    S.dma("sp", "fin", lambda s: s.nop(),
          waits=[e3, S.ch_now("po0"), S.ch_now("po1"), S.ch_now("sr")])
    return L


def build_launch_C():
    L = LaunchBuilder("C")
    resid4 = L.inp("resid4", [D, TOK], F32)
    q1 = L.inp("q1", [D, TOK], BF16)
    kfull = L.inp("kfull", [D, T], BF16)
    vfull = L.inp("vfull", [D, T], BF16)
    iota = L.inp("iota", [128, T], F32)
    thr = L.inp("thr", [128, 8], F32)
    ident = L.inp("ident", [128, 128], BF16)
    gmask = L.inp("gmask", [8, 128, NB], F32)
    Wo = L.inp("Wo", [128, 16, 16, 128], F32)
    wgT1 = L.inp("wgT1", [128, 64, 16, 128], I32)
    wuT1 = L.inp("wuT1", [128, 64, 16, 128], I32)
    wdT1 = L.inp("wdT1", [128, 16, 64, 128], I32)
    sq1 = L.inp("sq1", [128, 128], F32)
    sd1 = L.inp("sd1", [128, NC], F32)
    yT = L.out("yT", [D, TOK], F32)
    outT = L.scratch("outT", [D, TOK], BF16)
    resid5 = L.scratch("resid5", [D, TOK], F32)

    L.begin()
    S = L.S
    t = alloc_common(L)
    S_ps = L.es.enter_context(L.nc.psum_tensor("S_ps", [128, 4096], F32))
    banks = [S_ps[:, b * 512:(b + 1) * 512] for b in range(8)]
    AR1 = L.sbuf("AR1", [128, NC * TOK], BF16)
    AR2 = L.sbuf("AR2", [128, 2 * NC * TOK], BF16)
    AR3 = L.sbuf("AR3", [128, 2 * NC * TOK], BF16)
    wbuf = [L.sbuf("wb0", [128, 4096], BF16), L.sbuf("wb1", [128, 4096], BF16)]
    xstage = L.sbuf("xstage", [128, 2 * TG], F32)
    kh = AR3[:, 0:4096]
    vh = AR3[:, 4096:8192]
    vtm = AR3[:, 8192:12288]
    outstage = AR3[:, 13312:14336]
    iota_sb = AR3[:, 16384:24576].bitcast(F32)
    b01 = AR3[:, 24576:32768]
    gate_all = AR2[:, 0:NC * TOK].bitcast(F32)
    gwork = AR2[:, NC * TOK:2 * NC * TOK].bitcast(F32)
    eqm = AR3[:, 14336:16384].bitcast(F32)
    thr_sb = L.sbuf("thr_sb", [128, 8], F32)
    ident_sb = AR3[:, 12288:12416]
    gm_sb = AR3[:, 4096:5120].bitcast(F32)
    kg_sb = AR3[:, 6144:7168]
    tau = L.sbuf("tau", [128, 16], F32)
    nm = L.sbuf("nm", [128, 16], F32)
    nm8 = L.sbuf("nm8", [128, 8], F32)
    nm1 = L.sbuf("nm1", [128, 1], F32)
    den = L.sbuf("den", [128, 16], F32)
    sq_sb = L.sbuf("sq_sb", [128, 128], F32)
    sd05_sb = L.sbuf("sd05_sb", [128, NC], F32)
    xh = AR1
    hbuf = AR2
    yp = AR3[:, 0:2 * NC * TOK].bitcast(F32)
    sqstage = yp[:, 0:2 * TG]

    S.dma("sp", "cst", lambda s: s.dma_start(out=iota_sb, in_=iota[:, :]))
    S.dma("sp", "cst", lambda s: s.dma_start(out=thr_sb[:, :], in_=thr[:, :]))
    S.dma("sp", "cst", lambda s: s.dma_start(
        out=gm_sb.rearrange("p (q n) -> p q n", n=NB),
        in_=gmask.ap().rearrange("q p n -> p q n")))
    ev3 = S.dma("sp", "cst", lambda s: s.dma_start(out=ident_sb[:, :], in_=ident[:, :]))
    t["iota_ev"] = ev3
    t["thr_ev"] = ev3
    t["ident_ev"] = ev3
    S.dma("pool", "cb", lambda g: g.dma_start(out=sq_sb[:, :], in_=sq1[:, :]))
    S.dma("pool", "cb", lambda g: g.dma_start(out=sd05_sb[:, :], in_=sd1[:, :]))
    ev_sd = S.emit("dve", lambda v: v.tensor_scalar_mul(sd05_sb[:, :], sd05_sb[:, :], 0.5),
                   waits=[S.ch_now("cb")], inc=True)
    ev_q1 = S.dma("sp", "q1", lambda s: s.dma_start(
        out=AR1.ap().rearrange("p (c n) -> p c n", n=TOK),
        in_=q1.ap().rearrange("(c p) n -> p c n", p=128)))

    kh_free = []
    ge_evs = []
    for hh in range(16):
        ev_k = S.dma("sp", "kh", lambda s, hh=hh: s.dma_start(
            out=kh[:, :], in_=kfull[hh * 128:(hh + 1) * 128, :]), waits=kh_free)
        ev_kg = S.emit("dve", lambda v, hh=hh: v.tensor_reduce(
            eqm.rearrange("p (a n) -> p a n", n=1)[:, 0:NB, :],
            kh.rearrange("p (a n) -> p a n", n=BLK),
            AX.X, ALU.add), waits=[ev_k], inc=True)
        ev_kg = S.emit("dve", lambda v, hh=hh: v.tensor_scalar_mul(
            kg_sb[:, hh * NB:(hh + 1) * NB], eqm[:, 0:NB], 1.0 / BLK), inc=True)
        mm = None
        for qt in range(8):
            mm = S.emit("pe", lambda p, hh=hh, qt=qt: p.matmul(
                S_ps[:, 5 * 512 + qt * NB: 5 * 512 + (qt + 1) * NB],
                AR1[:, hh * TOK + qt * 128: hh * TOK + (qt + 1) * 128],
                kg_sb[:, hh * NB:(hh + 1) * NB], start=True, stop=True),
                waits=[ev_kg, ev_q1, ge_evs[-1] if ge_evs else None],
                inc=(qt == 7))
        kh_free = [ev_kg]
        ev_g = S.emit("dve", lambda v, hh=hh: v.tensor_tensor(
            gate_all.rearrange("p (q a n) -> p (q a) n", a=16, n=NB)[:, hh::16, :],
            S_ps[:, 5 * 512:6 * 512].rearrange("p (q n) -> p q n", n=NB),
            gm_sb.rearrange("p (q n) -> p q n", n=NB), ALU.add),
            waits=[mm, S.ch_now("cst")], inc=True)
        ge_evs.append(ev_g)
    b01_evs = []
    for qt in range(8):
        W3 = gwork.rearrange("p (q x) -> p q x", x=TOK)[:, qt, :].rearrange(
            "p (a n) -> p a n", n=NB)
        G3 = gate_all.rearrange("p (q x) -> p q x", x=TOK)[:, qt, :].rearrange(
            "p (a n) -> p a n", n=NB)
        ev = S.emit("dve", lambda v, W3=W3, G3=G3: v.tensor_copy(W3, G3),
                    waits=ge_evs, inc=True)
        for it in range(7):
            ev = S.emit("dve", lambda v, W3=W3: v.tensor_reduce(
                tau.ap().rearrange("p (a n) -> p a n", n=1), W3, AX.X, ALU.max),
                inc=True)
            ev = S.emit("dve", lambda v, W3=W3: v.tensor_tensor(
                eqm.rearrange("p (a n) -> p a n", n=NB), W3,
                tau.ap().unsqueeze(2).broadcast_to([128, 16, NB]), ALU.is_equal),
                inc=True)
            ev = S.emit("dve", lambda v, W3=W3: v.scalar_tensor_tensor(
                W3, eqm.rearrange("p (a n) -> p a n", n=NB), -2.0e30, W3,
                ALU.mult, ALU.add), inc=True)
        ev = S.emit("dve", lambda v, W3=W3: v.tensor_reduce(
            tau.ap().rearrange("p (a n) -> p a n", n=1), W3, AX.X, ALU.max),
            inc=True)
        ev = S.emit("dve", lambda v, G3=G3, qt=qt: v.tensor_tensor(
            b01.rearrange("p (q a n) -> p q a n", a=16, n=NB)[:, qt, :, :], G3,
            tau.ap().unsqueeze(2).broadcast_to([128, 16, NB]), ALU.is_ge),
            inc=True)
        b01_evs.append(ev)

    scr = dict(nm=nm, den=den, outstage=outstage, qm=None, nm8=nm8, nm1=nm1)
    ev = attn_phase(L, t, S_ps, None, kh, vh, vtm, AR2[:, 0:NC * TOK],
                    AR2[:, NC * TOK:2 * NC * TOK], iota_sb, thr_sb, ident_sb,
                    kfull, vfull, outT.ap(), scr, pre=b01_evs,
                    ar1_q=AR1, b01=b01)
    wfree = [None, None]
    ev = wo_phase(L, t, outT.ap(), Wo, None, resid4.ap(), resid5.ap(), wbuf, banks,
                  AR1, xstage, wfree, pre=ev, ar1_free=[S.now("pe")])
    ev2 = ln_phase(L, t, resid5.ap(), xh, banks, xstage, sqstage, pre=ev, sx=True)
    ev2 = S.emit("dve", lambda v: v.nop(), waits=[ev2, ev_sd], inc=True)
    ev2 = ffn_phase(L, t, xh, hbuf, yp, wbuf, banks, xstage,
                    wgT1, wuT1, wdT1, quant=True, sq_sb=sq_sb,
                    sd05_ap=lambda dc: sd05_sb[:, dc:dc + 1],
                    xold=resid5.ap(), resid_out=yT.ap(), xh_ready=ev2, wfree=wfree)
    S.dma("sp", "fin", lambda s: s.nop(), waits=[ev2, S.ch_now("sr")])
    return L


_NCORES = 8
_BF = ml_dtypes.bfloat16
_PROG_CACHE = {}


def _prog(name, builder):
    if name not in _PROG_CACHE:
        L = builder()
        _PROG_CACHE[name] = L.finish()
    return _PROG_CACHE[name]


def _build_qmine(qflatT_b, qr):
    A = qflatT_b.reshape(16, 128, 4096)
    out = np.empty((16, 128, 1024), dtype=qflatT_b.dtype)
    for hh in range(16):
        sub = A[:, :, hh * 256 + 64 * qr: hh * 256 + 64 * qr + 64]
        out[hh] = np.transpose(sub, (1, 2, 0)).reshape(128, 1024)
    return out


def kernel(L0_attn_x, L0_attn_Wq, L0_attn_bq, L0_attn_Wk, L0_attn_bk,
           L0_attn_Wv, L0_attn_bv, L0_attn_Wo, L0_attn_bo,
           L0_ffn_w_gate, L0_ffn_w_up, L0_ffn_w_down,
           L1_attn_Wq, L1_attn_Wk, L1_attn_Wv, L1_attn_Wo,
           L1_ffn_gate_w, L1_ffn_gate_scale, L1_ffn_up_w, L1_ffn_up_scale,
           L1_ffn_down_w, L1_ffn_down_scale):
    C = np.ascontiguousarray

    def gu_img(w):
        return C(np.asarray(w).reshape(64, 128, 16, 128).transpose(3, 0, 2, 1))

    def dn_img(w):
        return C(np.asarray(w).reshape(16, 128, 64, 128).transpose(3, 0, 2, 1))

    def pj_img(w):
        return C(np.asarray(w, np.float32).reshape(16, 128, 16, 128).transpose(1, 2, 0, 3))

    x = np.asarray(L0_attn_x, np.float32)
    wgT0 = gu_img(np.asarray(L0_ffn_w_gate, np.float32))
    wuT0 = gu_img(np.asarray(L0_ffn_w_up, np.float32))
    wdT0 = dn_img(np.asarray(L0_ffn_w_down, np.float32))
    b_qkv = C(np.concatenate([np.asarray(b, np.float32).reshape(16, 128).T
                              for b in (L0_attn_bq, L0_attn_bk, L0_attn_bv)], axis=1))
    bo = C(np.asarray(L0_attn_bo, np.float32).reshape(16, 128).T)
    wgT1 = gu_img(np.asarray(L1_ffn_gate_w, np.int32))
    wuT1 = gu_img(np.asarray(L1_ffn_up_w, np.int32))
    wdT1 = dn_img(np.asarray(L1_ffn_down_w, np.int32))
    sq1 = C(np.concatenate([np.asarray(L1_ffn_gate_scale, np.float32).reshape(64, 128).T,
                            np.asarray(L1_ffn_up_scale, np.float32).reshape(64, 128).T],
                           axis=1))
    sd1 = C(np.asarray(L1_ffn_down_scale, np.float32).reshape(16, 128).T)
    iota = np.broadcast_to(np.arange(T, dtype=np.float32), (128, T)).copy()
    ident = np.eye(128, dtype=_BF)
    thr = [(1024 * qr + np.arange(8)[None, :] * 128 +
            np.arange(128)[:, None]).astype(np.float32) for qr in range(4)]
    gmask = []
    for qr in range(4):
        s_abs = qr * 1024 + np.arange(8)[:, None] * 128 + np.arange(128)[None, :]
        blk = np.arange(NB)[None, None, :]
        self_m = (s_abs[:, :, None] // BLK) == blk
        notpast = (s_abs[:, :, None] < (blk + 1) * BLK) & ~self_m
        gmask.append(C(np.where(self_m, BIGM, np.where(notpast, -BIGM, 0.0))
                     .astype(np.float32)))

    im = []
    for c in range(_NCORES):
        b, qr = c // 4, c % 4
        im.append(dict(xT=C(x[b, qr * TOK:(qr + 1) * TOK].T), wgT=wgT0, wuT=wuT0,
                       wdT=wdT0, Wq=pj_img(L0_attn_Wq), Wk=pj_img(L0_attn_Wk),
                       Wv=pj_img(L0_attn_Wv), b_qkv=b_qkv))
    rA = run_bass_kernel_spmd(_prog("A", build_launch_A), im,
                              core_ids=list(range(_NCORES))).results
    qfT = [np.concatenate([rA[b * 4 + qr]["qT"] for qr in range(4)], axis=1)
           for b in range(2)]
    kfT = [C(np.concatenate([rA[b * 4 + qr]["kT"] for qr in range(4)], axis=1))
           for b in range(2)]
    vfT = [C(np.concatenate([rA[b * 4 + qr]["vT"] for qr in range(4)], axis=1))
           for b in range(2)]

    im = []
    for c in range(_NCORES):
        b, qr = c // 4, c % 4
        im.append(dict(resid1=rA[c]["resid1"], qmine=C(_build_qmine(qfT[b], qr)),
                       kfull=kfT[b], vfull=vfT[b], iota=iota, thr=thr[qr],
                       ident=ident, Wo=pj_img(L0_attn_Wo), bo=bo))
    for c in range(_NCORES):
        im[c].update(dict(wgT0=wgT0, wuT0=wuT0, wdT0=wdT0,
                          wgT1=wgT1, wuT1=wuT1, wdT1=wdT1, sq1=sq1, sd1=sd1,
                          Wq1=pj_img(L1_attn_Wq), Wk1=pj_img(L1_attn_Wk),
                          Wv1=pj_img(L1_attn_Wv)))
    rB2 = run_bass_kernel_spmd(_prog("B", build_launch_B), im,
                               core_ids=list(range(_NCORES))).results
    k1T = [C(np.concatenate([rB2[b * 4 + qr]["kT1"] for qr in range(4)], axis=1))
           for b in range(2)]
    v1T = [C(np.concatenate([rB2[b * 4 + qr]["vT1"] for qr in range(4)], axis=1))
           for b in range(2)]

    im = []
    for c in range(_NCORES):
        b, qr = c // 4, c % 4
        im.append(dict(resid4=rB2[c]["resid4"], q1=rB2[c]["qT1"], kfull=k1T[b],
                       vfull=v1T[b], iota=iota, thr=thr[qr], ident=ident,
                       gmask=gmask[qr], Wo=pj_img(L1_attn_Wo)))
    for c in range(_NCORES):
        im[c].update(dict(wgT1=wgT1, wuT1=wuT1, wdT1=wdT1, sq1=sq1, sd1=sd1))
    rC2 = run_bass_kernel_spmd(_prog("C", build_launch_C), im,
                               core_ids=list(range(_NCORES))).results

    out = np.empty((2, T, D), np.float32)
    for c in range(_NCORES):
        b, qr = c // 4, c % 4
        out[b, qr * TOK:(qr + 1) * TOK] = rC2[c]["yT"].T
    return out
